# version 02
# baseline (speedup 1.0000x reference)
import numpy as np
import math
from contextlib import ExitStack
ExitSt = ExitStack
import ml_dtypes
import concourse.bass as bass
import concourse.mybir as mybir
from concourse.bass_utils import run_bass_kernel_spmd

F32 = mybir.dt.float32
BF16 = mybir.dt.bfloat16
I32 = mybir.dt.int32
OP = mybir.AluOpType
AF = mybir.ActivationFunctionType
BF = ml_dtypes.bfloat16

L = 4096
D = 1024
HF = 3072
NH = 8
DH = 128
T = 256
NT = L // T
FFN_TILES = NT
ATTN_TILES = NT
EPS = 1e-5
GN_N = float(L * DH)

ENGINES = ("sync", "tensor", "vector", "scalar", "gpsimd")


def _gammas():
    return 1.0 - np.exp(np.linspace(math.log(1.0 / 32.0), math.log(1.0 / 512.0), NH))


class Sem:
    __slots__ = ("h", "n", "name")


class Sched:
    def __init__(self, nc, stack):
        self.nc = nc
        self.stack = stack
        self.q = {e: [] for e in ENGINES}

    def sem(self, name):
        s = Sem()
        s.h = self.stack.enter_context(self.nc.semaphore(name))
        s.n = 0
        s.name = name
        return s

    def wait(self, eng, sem, val=None):
        v = sem.n if val is None else val
        if v <= 0:
            return
        f = lambda e, _h=sem.h, _v=v: e.wait_ge(_h, _v)
        f.meta = ("wait", sem.name, v)
        self.q[eng].append(f)

    def op(self, eng, fn, incs=()):
        incs = tuple(incs)

        def run(e, _fn=fn, _incs=incs):
            ins = _fn(e)
            for s, k in _incs:
                ins.then_inc(s.h, k)

        run.meta = ("op", tuple((s.name, k) for s, k in incs), getattr(fn, "__qualname__", ""))
        self.q[eng].append(run)
        for s, k in incs:
            s.n += k

    def snapshot(self):
        return {e: list(self.q[e]) for e in ENGINES}

    def flush_block(self):
        with self.nc.Block() as blk:
            for eng in ENGINES:
                ops = self.q[eng]

                def body(e, _ops=ops):
                    for f in _ops:
                        f(e)

                getattr(blk, eng)(body)
        self.q = {e: [] for e in ENGINES}


def build_nc(n_phases=4):
    nc = bass.Bass("TRN2", target_bir_lowering=False)

    x_in = nc.dram_tensor("x", [L, D], F32, kind="ExternalInput").ap()
    wqT = nc.dram_tensor("wqT", [D, D], F32, kind="ExternalInput").ap()
    wkT = nc.dram_tensor("wkT", [D, D], F32, kind="ExternalInput").ap()
    wvT = nc.dram_tensor("wvT", [D, D], F32, kind="ExternalInput").ap()
    woT = nc.dram_tensor("woT", [D, D], F32, kind="ExternalInput").ap()
    wgT = nc.dram_tensor("wgT", [D, HF], I32, kind="ExternalInput").ap()
    wuT = nc.dram_tensor("wuT", [D, HF], I32, kind="ExternalInput").ap()
    wdT = nc.dram_tensor("wdT", [HF, D], I32, kind="ExternalInput").ap()
    gnw_in = nc.dram_tensor("gnw", [128, NH], F32, kind="ExternalInput").ap()
    gnb_in = nc.dram_tensor("gnb", [128, NH], F32, kind="ExternalInput").ap()
    ident_in = nc.dram_tensor("ident", [128, 128], BF16, kind="ExternalInput").ap()
    mask_in = nc.dram_tensor("maskc", [128, NH * 128], BF16, kind="ExternalInput").ap()
    qdec_in = nc.dram_tensor("qdec", [128, NH * T], BF16, kind="ExternalInput").ap()
    kdec_in = nc.dram_tensor("kdec", [128, D], BF16, kind="ExternalInput").ap()
    out_hbm = nc.dram_tensor("out", [L, D], F32, kind="ExternalOutput").ap()

    xn0 = nc.dram_tensor("xn0_scr", [L, D], BF16, kind="Internal").ap()
    xn1 = nc.dram_tensor("xn1_scr", [L, D], BF16, kind="Internal").ap()
    xn2 = nc.dram_tensor("xn2_scr", [L, D], BF16, kind="Internal").ap()
    x1s = nc.dram_tensor("x1_scr", [L, D], F32, kind="Internal").ap()
    x2s = nc.dram_tensor("x2_scr", [L, D], F32, kind="Internal").ap()

    gam = _gammas()
    gam128 = [float(g) ** 128 for g in gam]

    def tile_half(ap, t, c):
        return ap[t * T + c * 128:t * T + (c + 1) * 128, :]

    with ExitStack() as ES:
        B = Sched(nc, ES)
        ld_x = B.sem("ld_x")
        ld_t = B.sem("ld_t")
        ld_w = B.sem("ld_w")
        st_s = B.sem("st_s")
        st_o = B.sem("st_o")
        pe = B.sem("pe")
        ve = B.sem("ve")
        se = B.sem("se")
        ge = B.sem("ge")
        bar = B.sem("bar")

        junk = ES.enter_context(nc.sbuf_tensor("junkbuf", [128, 4 * D], BF16))
        junk_v = junk[:, 0:D]
        junk_v1 = junk[:, D:2 * D]
        junk_s = junk[:, 2 * D:3 * D]
        junk_s1 = junk[:, 3 * D:4 * D]
        junk_last = {}

        def junk_guard(eng, key, sem):
            if key in junk_last:
                s_, v_ = junk_last[key]
                B.wait(eng, s_, v_)

        def junk_mark(key, sem):
            junk_last[key] = (sem, sem.n)
        ident_sb = ES.enter_context(nc.sbuf_tensor("ident_sb", [128, 128], BF16))
        ones_sb = ES.enter_context(nc.sbuf_tensor("ones_sb", [128, 128], F32))
        psum = ES.enter_context(nc.psum_tensor("psum_all", [128, 4096], F32))
        P = psum.ap()
        P_bf = P.bitcast(BF16)

        nbar = [0]

        def barrier():
            for eng in ENGINES:
                if eng == "sync":
                    B.op(eng, lambda e: e.nop(), [(bar, 1)])
                else:
                    B.op(eng, lambda e: e.drain(), [(bar, 1)])
            nbar[0] += 1
            for eng in ENGINES:
                B.wait(eng, bar, 5 * nbar[0])

        B.op("sync", lambda e: e.dma_start(out=ident_sb[:], in_=ident_in), [(ld_x, 16)])
        B.op("vector", lambda e: e.memset(ones_sb[:], 1.0), [(ve, 1)])

        def ln_finalize_apply(stats, x_tile, xn_tile, sumsq_ready_se):
            B.wait("vector", se, sumsq_ready_se)
            B.op("vector", lambda e, _s=stats: e.drain())
            B.op("vector", lambda e, _s=stats: e.scalar_tensor_tensor(
                _s[:, 4:6], _s[:, 0:2], 1.0 / (D * D), _s[:, 0:2],
                op0=OP.mult, op1=OP.mult))
            B.op("vector", lambda e, _s=stats: e.tensor_scalar(
                _s[:, 6:8], _s[:, 0:2], 1.0 / D, None, op0=OP.mult))
            B.op("vector", lambda e: e.drain())
            B.op("vector", lambda e, _s=stats: e.scalar_tensor_tensor(
                _s[:, 8:10], _s[:, 2:4], 1.0 / D, _s[:, 4:6],
                op0=OP.mult, op1=OP.subtract), [(ve, 1)])
            B.wait("scalar", ve, ve.n)
            B.op("scalar", lambda e, _s=stats: e.activation(
                _s[:, 10:12], _s[:, 8:10], AF.Sqrt, bias=np.float32(EPS)), [(se, 1)])
            B.wait("vector", se, se.n)
            B.op("vector", lambda e, _s=stats: e.reciprocal(_s[:, 12:14], _s[:, 10:12]))
            B.op("vector", lambda e: e.drain())
            B.op("vector", lambda e, _s=stats: e.scalar_tensor_tensor(
                _s[:, 14:16], _s[:, 6:8], -1.0, _s[:, 12:14],
                op0=OP.mult, op1=OP.mult), [(ve, 1)])
            B.wait("scalar", ve, ve.n)
            for c in range(2):
                B.op("scalar", lambda e, _s=stats, _x=x_tile, _o=xn_tile, _c=c: e.activation(
                    _o[:, _c * D:(_c + 1) * D], _x[:, _c * D:(_c + 1) * D],
                    AF.Identity, bias=_s[:, 14 + _c:15 + _c],
                    scale=_s[:, 12 + _c:13 + _c]), [(se, 1)])
            return se.n

        def ffn_weight_prologue(wg_sb, wu_sb, wd_sb, wstage):
            for wdram, wsb, sc in ((wgT, wg_sb, 1.0 / 127.0), (wuT, wu_sb, 1.0 / 127.0)):
                for d in range(8):
                    B.op("gpsimd", lambda e, _w=wdram, _d=d, _st=wstage: e.dma_start(
                        out=_st[:], in_=_w[_d * 128:(_d + 1) * 128, :]), [(ld_w, 16)])
                    B.wait("gpsimd", ld_w, ld_w.n)
                    B.op("gpsimd", lambda e, _sb=wsb, _d=d, _st=wstage, _sc=sc: e.tensor_scalar(
                        _sb[:, _d * HF:(_d + 1) * HF], _st[:], _sc, None,
                        op0=OP.mult), [(ge, 1)])
            for h in range(24):
                B.op("gpsimd", lambda e, _h=h, _st=wstage: e.dma_start(
                    out=_st[:, 0:D], in_=wdT[_h * 128:(_h + 1) * 128, :]), [(ld_w, 16)])
                B.wait("gpsimd", ld_w, ld_w.n)
                B.op("gpsimd", lambda e, _h=h, _st=wstage: e.tensor_scalar(
                    wd_sb[:, _h * D:(_h + 1) * D], _st[:, 0:D], 0.5 / 127.0, None,
                    op0=OP.mult), [(ge, 1)])
            return ge.n

        w1scope = ExitStack()
        w1_done = 0
        if n_phases >= 2:
            wg1 = w1scope.enter_context(nc.sbuf_tensor("wg_sb", [128, 8 * HF], BF16))
            wu1 = w1scope.enter_context(nc.sbuf_tensor("wu_sb", [128, 8 * HF], BF16))
            wd1 = w1scope.enter_context(nc.sbuf_tensor("wd_sb", [128, 24 * D], BF16))
            wst1 = w1scope.enter_context(nc.sbuf_tensor("wstage", [128, HF], I32))
            w1_done = ffn_weight_prologue(wg1, wu1, wd1, wst1)

        def build_lnpass(src_dram, xn_dram, tag):
            with nc.sbuf_tensor("pp_xin" + tag, [128, 2 * 2 * D], F32) as xin, \
                 nc.sbuf_tensor("pp_xn" + tag, [128, 2 * 2 * D], BF16) as xno, \
                 nc.sbuf_tensor("pp_stats" + tag, [128, 2 * 16], F32) as statsb:
                store_done = {}
                for t in range(NT):
                    p = t % 2
                    xin_p = xin[:, p * 2 * D:(p + 1) * 2 * D]
                    xno_p = xno[:, p * 2 * D:(p + 1) * 2 * D]
                    stats = statsb[:, p * 16:(p + 1) * 16]
                    if t >= 2:
                        B.wait("sync", st_s, store_done[p])
                        B.wait("scalar", st_s, store_done[p])
                    for _c in range(2):
                        B.op("sync", lambda e, _t=t, _x=xin_p, _cc=_c: e.dma_start(
                            out=_x[:, _cc * D:(_cc + 1) * D], in_=tile_half(src_dram, _t, _cc)), [(ld_x, 16)])
                    x_ld = ld_x.n
                    B.wait("vector", ld_x, x_ld)
                    for c in range(2):
                        jv = junk_v if c == 0 else junk_v1
                        junk_guard("vector", ("v", c), ve)
                        B.op("vector", lambda e, _x=xin_p, _s=stats, _c=c, _j=jv: e.tensor_scalar(
                            _j[:], _x[:, _c * D:(_c + 1) * D], 1.0, None, op0=OP.mult,
                            op1=OP.add, accum_out=_s[:, 0 + _c:1 + _c]), [(ve, 1)])
                        junk_mark(("v", c), ve)
                        B.op("vector", lambda e: e.drain())
                    B.wait("scalar", ld_x, x_ld)
                    for c in range(2):
                        js = junk_s if c == 0 else junk_s1
                        junk_guard("scalar", ("s", c), se)
                        B.op("scalar", lambda e, _x=xin_p, _s=stats, _c=c, _j=js: e.activation(
                            _j[:], _x[:, _c * D:(_c + 1) * D], AF.Square,
                            accum_out=_s[:, 2 + _c:3 + _c]), [(se, 1)])
                        junk_mark(("s", c), se)
                    apply_se = ln_finalize_apply(stats, xin_p, xno_p, se.n)
                    B.wait("sync", se, apply_se)
                    for _c in range(2):
                        B.op("sync", lambda e, _t=t, _x=xno_p, _cc=_c: e.dma_start(
                            out=tile_half(xn_dram, _t, _cc), in_=_x[:, _cc * D:(_cc + 1) * D]), [(st_s, 16)])
                    store_done[p] = st_s.n

        if n_phases >= 1:
            build_lnpass(x_in, xn0, "0")
            barrier()

        def build_ffn(xn_src, x_src, x_dst, xn_dst, wg_sb, wu_sb, wd_sb,
                      weights_ready_ge, final, tag=""):
            st_out = st_o if final else st_s
            with nc.sbuf_tensor("f_xnf" + tag, [128, 2 * 8 * T], BF16) as xnf, \
                 nc.sbuf_tensor("f_xin" + tag, [128, 2 * 2 * D], F32) as xin, \
                 nc.sbuf_tensor("f_hid" + tag, [128, 24 * T], BF16) as hid, \
                 nc.sbuf_tensor("f_gsb" + tag, [128, 2 * T], F32) as gsb:
                pg = [P[:, 0:T], P[:, 3072:3072 + T]]
                pu = [P[:, 512:512 + T], P[:, 3584:3584 + T]]
                pd = [P[:, 1024:2048], P[:, 2048:3072]]
                silu_done = {}
                hid_done = {}
                res_done = [0]
                store_done = {}
                xnf_used = {}
                B.wait("tensor", ge, weights_ready_ge)
                for t in range(FFN_TILES):
                    p = t % 2
                    xin_p = xin[:, p * 2 * D:(p + 1) * 2 * D]
                    xnf_p = [xnf[:, (p * 8 + d) * T:(p * 8 + d + 1) * T] for d in range(8)]
                    if t >= 2:
                        B.wait("sync", st_out, store_done[p])
                        B.wait("sync", pe, xnf_used[p])
                    for d in range(8):
                        B.op("sync", lambda e, _t=t, _d=d, _o=xnf_p[d]: e.dma_start_transpose(
                            out=_o, in_=xn_src[_t * T:(_t + 1) * T, _d * 128:(_d + 1) * 128]),
                            [(ld_t, 16)])
                    xnf_ld = ld_t.n
                    for _c in range(2):
                        B.op("sync", lambda e, _t=t, _x=xin_p, _cc=_c: e.dma_start(
                            out=_x[:, _cc * D:(_cc + 1) * D], in_=tile_half(x_src, _t, _cc)), [(ld_x, 16)])
                    x_ld = ld_x.n

                    B.wait("tensor", ld_t, xnf_ld)
                    if t >= 1:
                        B.wait("tensor", ve, res_done[0])

                    def down_mms(ha, hp):
                        B.wait("tensor", ve, hid_done[ha])
                        for c in range(2):
                            for half in range(2):
                                last = (hp == 23 and c == 1 and half == 1)
                                B.op("tensor", lambda e, _h=hp, _c=c, _f=half: e.matmul(
                                    pd[_c][:, _f * 512:(_f + 1) * 512],
                                    hid[:, _h * T + _c * 128:_h * T + (_c + 1) * 128],
                                    wd_sb[:, _h * D + _f * 512:_h * D + (_f + 1) * 512],
                                    start=(_h == 0), stop=(_h == 23)),
                                    [(pe, 1)] if last else ())

                    for h in range(24):
                        ha = t * 24 + h
                        bk = h % 2
                        if ha >= 2:
                            B.wait("tensor", se, silu_done[ha - 2])
                        for d in range(8):
                            B.op("tensor", lambda e, _h=h, _d=d, _b=bk, _xn=xnf_p[d]: e.matmul(
                                pg[_b], wg_sb[:, _d * HF + _h * 128:_d * HF + (_h + 1) * 128],
                                _xn, start=(_d == 0), stop=(_d == 7)),
                                [(pe, 1)] if d == 7 else ())
                        gate_pe = pe.n
                        if ha >= 2:
                            B.wait("tensor", ve, hid_done[ha - 2])
                        for d in range(8):
                            B.op("tensor", lambda e, _h=h, _d=d, _b=bk, _xn=xnf_p[d]: e.matmul(
                                pu[_b], wu_sb[:, _d * HF + _h * 128:_d * HF + (_h + 1) * 128],
                                _xn, start=(_d == 0), stop=(_d == 7)),
                                [(pe, 1)] if d == 7 else ())
                        up_pe = pe.n
                        if h >= 1:
                            down_mms(ha - 1, h - 1)
                        B.wait("scalar", pe, gate_pe)
                        if ha >= 2:
                            B.wait("scalar", ve, hid_done[ha - 2])
                        B.op("scalar", lambda e, _b=bk: e.activation(
                            gsb[:, _b * T:(_b + 1) * T], pg[_b], AF.Silu), [(se, 1)])
                        silu_done[ha] = se.n
                        B.wait("vector", se, silu_done[ha])
                        B.wait("vector", pe, up_pe)
                        B.op("vector", lambda e, _h=h, _b=bk: e.tensor_tensor(
                            hid[:, _h * T:(_h + 1) * T], pu[_b], gsb[:, _b * T:(_b + 1) * T],
                            op=OP.mult), [(ve, 1)])
                        hid_done[ha] = ve.n
                    down_mms(t * 24 + 23, 23)
                    down_pe = pe.n
                    xnf_used[p] = pe.n
                    B.wait("vector", pe, down_pe)
                    B.wait("vector", ld_x, x_ld)
                    for c in range(2):
                        B.op("vector", lambda e, _c=c, _x=xin_p: e.scalar_tensor_tensor(
                            _x[:, _c * D:(_c + 1) * D], pd[_c], 1.0, _x[:, _c * D:(_c + 1) * D],
                            op0=OP.mult, op1=OP.add),
                            [(ve, 1)])
                    res_done[0] = ve.n
                    res_ve = ve.n
                    B.wait("sync", ve, res_ve)
                    for _c in range(2):
                        B.op("sync", lambda e, _t=t, _x=xin_p, _cc=_c: e.dma_start(
                            out=tile_half(x_dst, _t, _cc), in_=_x[:, _cc * D:(_cc + 1) * D]), [(st_out, 16)])
                    store_done[p] = st_out.n

        if n_phases >= 2:
            build_ffn(xn0, x_in, x1s, None, wg1, wu1, wd1, w1_done,
                      final=(n_phases == 2))
            barrier()
            if n_phases >= 3:
                build_lnpass(x1s, xn1, "1")
                barrier()
        w1scope.close()

        if n_phases >= 3:
            with ExitSt() as AS:
                al = AS.enter_context
                wq_sb = al(nc.sbuf_tensor("a_wq", [128, 8 * D], BF16))
                wk_sb = al(nc.sbuf_tensor("a_wk", [128, 8 * D], BF16))
                wv_sb = al(nc.sbuf_tensor("a_wv", [128, 8 * D], BF16))
                wo_sb = al(nc.sbuf_tensor("a_wo", [128, 8 * D], BF16))
                awst = al(nc.sbuf_tensor("a_wstage", [128, D], F32))
                ysb = al(nc.sbuf_tensor("a_ys", [128, NH * L], BF16))
                xnf = al(nc.sbuf_tensor("a_xnf", [128, 2 * 8 * T], BF16))
                qt = al(nc.sbuf_tensor("a_qt", [128, NH * T], BF16))
                ktm = al(nc.sbuf_tensor("a_ktm", [128, 2 * D], BF16))
                kfm = al(nc.sbuf_tensor("a_kfm", [128, 2 * D], BF16))
                vtm = al(nc.sbuf_tensor("a_vtm", [128, 2 * D], BF16))
                adb = al(nc.sbuf_tensor("a_ad", [128, 2 * 128], BF16))
                s0f = al(nc.sbuf_tensor("a_s0f", [128, NH * DH], F32))
                s0b = al(nc.sbuf_tensor("a_s0b", [128, NH * DH], BF16))
                mask_sb = al(nc.sbuf_tensor("a_mask", [128, NH * 128], BF16))
                qdec_sb = al(nc.sbuf_tensor("a_qdec", [128, NH * T], BF16))
                kdec_sb = al(nc.sbuf_tensor("a_kdec", [128, D], BF16))
                ysum = al(nc.sbuf_tensor("a_ysum", [128, NH * 32], F32))
                ysq = al(nc.sbuf_tensor("a_ysq", [128, NH * 32], F32))
                gnw_sb = al(nc.sbuf_tensor("a_gnw", [128, NH], F32))
                gnb_sb = al(nc.sbuf_tensor("a_gnb", [128, NH], F32))
                gns = al(nc.sbuf_tensor("a_gns", [128, 64], F32))
                ynb = al(nc.sbuf_tensor("a_yn", [128, NH * T], BF16))
                xin = al(nc.sbuf_tensor("a_xin", [128, 2 * 2 * D], F32))

                B.op("sync", lambda e: e.dma_start(out=mask_sb[:], in_=mask_in), [(ld_x, 16)])
                B.op("sync", lambda e: e.dma_start(out=qdec_sb[:], in_=qdec_in), [(ld_x, 16)])
                B.op("sync", lambda e: e.dma_start(out=kdec_sb[:], in_=kdec_in), [(ld_x, 16)])
                B.op("sync", lambda e: e.dma_start(out=gnw_sb[:], in_=gnw_in), [(ld_x, 16)])
                B.op("sync", lambda e: e.dma_start(out=gnb_sb[:], in_=gnb_in), [(ld_x, 16)])
                const_ld = ld_x.n
                for wdram, wsb in ((wqT, wq_sb), (wkT, wk_sb), (wvT, wv_sb), (woT, wo_sb)):
                    for d in range(8):
                        B.op("gpsimd", lambda e, _w=wdram, _d=d: e.dma_start(
                            out=awst[:], in_=_w[_d * 128:(_d + 1) * 128, :]), [(ld_w, 16)])
                        B.wait("gpsimd", ld_w, ld_w.n)
                        B.op("gpsimd", lambda e, _sb=wsb, _d=d: e.tensor_copy(
                            _sb[:, _d * D:(_d + 1) * D], awst[:]), [(ge, 1)])
                aw_done = ge.n
                B.op("vector", lambda e: e.memset(s0f[:], 0.0))
                B.op("vector", lambda e: e.memset(s0b[:], 0.0), [(ve, 1)])
                s0_init = ve.n
                B.wait("vector", ld_x, const_ld)

                paq = [P[:, 0:T], P[:, 512:512 + T]]
                pa = [P[:, 0:128], P[:, 512:640]]
                pk = P[:, 1024:2048]
                py = [P[:, 1024:1152], P[:, 1536:1664]]
                pv = P[:, 2048:3072]
                pkv = [P[:, 2048:2176], P[:, 2560:2688]]
                ptr = [P_bf[:, 6144:7168], P_bf[:, 7168:8192]]

                B.wait("tensor", ge, aw_done)
                B.wait("tensor", ve, s0_init)

                q_evac = {}
                ktm_evac = {}
                kfm_evac = {}
                v_evac = {}
                ad_evac = {}
                supd = {}
                s0cast = {}
                a_done = {}
                y_done = {}
                kv_done = {}
                ysq_done = {}
                xnf_used = {}

                for t in range(NT):
                    p = t % 2
                    xnf_p = [xnf[:, (p * 8 + d) * T:(p * 8 + d + 1) * T] for d in range(8)]
                    if t >= 2:
                        B.wait("sync", pe, xnf_used[p])
                    for d in range(8):
                        B.op("sync", lambda e, _t=t, _d=d, _o=xnf_p[d]: e.dma_start_transpose(
                            out=_o, in_=xn1[_t * T:(_t + 1) * T, _d * 128:(_d + 1) * 128]),
                            [(ld_t, 16)])
                    xnf_ld = ld_t.n
                    B.wait("tensor", ld_t, xnf_ld)
                    for h in range(NH):
                        bk = h % 2
                        if t >= 1:
                            B.wait("tensor", ve, ad_evac[(2 * t - 1, 6 + bk)])
                        if h >= 2:
                            B.wait("tensor", ve, q_evac[(t, h - 2)])
                        for d in range(8):
                            B.op("tensor", lambda e, _h=h, _d=d, _b=bk, _xn=xnf_p[d]: e.matmul(
                                paq[_b], wq_sb[:, _d * D + _h * 128:_d * D + (_h + 1) * 128],
                                _xn, start=(_d == 0), stop=(_d == 7)),
                                [(pe, 1)] if d == 7 else ())
                        qpe = pe.n
                        B.wait("vector", pe, qpe)
                        if t >= 1:
                            B.wait("vector", pe, y_done[(2 * t - 1, h)])
                        B.op("vector", lambda e, _h=h, _b=bk: e.tensor_tensor(
                            qt[:, _h * T:(_h + 1) * T], paq[_b], qdec_sb[:, _h * T:(_h + 1) * T],
                            op=OP.mult), [(ve, 1)])
                        q_evac[(t, h)] = ve.n
                    for c in range(2):
                        gc = 2 * t + c
                        if c == 0:
                            if gc >= 1:
                                B.wait("tensor", se, ysq_done[(gc - 1, 7)])
                        else:
                            B.wait("tensor", ve, ktm_evac[gc - 1])
                        for half in range(2):
                            for d in range(8):
                                B.op("tensor", lambda e, _c=c, _f=half, _d=d, _xn=xnf_p[d]: e.matmul(
                                    pk[:, _f * 512:(_f + 1) * 512],
                                    _xn[:, _c * 128:(_c + 1) * 128],
                                    wk_sb[:, _d * D + _f * 512:_d * D + (_f + 1) * 512],
                                    start=(_d == 0), stop=(_d == 7)),
                                    [(pe, 1)] if (d == 7 and half == 1) else ())
                        kpe = pe.n
                        B.wait("vector", pe, kpe)
                        if gc >= 2:
                            B.wait("vector", pe, kv_done[(gc - 2, 7)])
                        B.op("vector", lambda e, _c=c: e.tensor_tensor(
                            ktm[:, _c * D:(_c + 1) * D], pk[:], kdec_sb[:], op=OP.mult),
                            [(ve, 1)])
                        ktm_evac[gc] = ve.n
                        if c == 0:
                            if gc >= 1:
                                B.wait("tensor", ve, supd[(gc - 1, 7)])
                        else:
                            B.wait("tensor", se, v_evac[gc - 1])
                        for half in range(2):
                            for d in range(8):
                                B.op("tensor", lambda e, _c=c, _f=half, _d=d, _xn=xnf_p[d]: e.matmul(
                                    pv[:, _f * 512:(_f + 1) * 512],
                                    _xn[:, _c * 128:(_c + 1) * 128],
                                    wv_sb[:, _d * D + _f * 512:_d * D + (_f + 1) * 512],
                                    start=(_d == 0), stop=(_d == 7)),
                                    [(pe, 1)] if (d == 7 and half == 1) else ())
                        vpe = pe.n
                        B.wait("scalar", pe, vpe)
                        if gc >= 2:
                            B.wait("scalar", pe, kv_done[(gc - 2, 7)])
                        B.op("scalar", lambda e, _c=c: e.activation(
                            vtm[:, _c * D:(_c + 1) * D], pv[:], AF.Copy), [(se, 1)])
                        v_evac[gc] = se.n
                        B.wait("tensor", ve, ktm_evac[gc])
                        if gc >= 2:
                            B.wait("tensor", ve, kfm_evac[gc - 2])
                        for h in range(NH):
                            B.op("tensor", lambda e, _c=c, _h=h: e.transpose(
                                ptr[_c][:, _h * 128:(_h + 1) * 128],
                                ktm[:, _c * D + _h * 128:_c * D + (_h + 1) * 128],
                                ident_sb[:]), [(pe, 1)] if h == NH - 1 else ())
                        trpe = pe.n
                        B.wait("vector", pe, trpe)
                        if gc >= 2:
                            B.wait("vector", pe, a_done[(gc - 2, 7)])
                        B.op("vector", lambda e, _c=c: e.tensor_copy(
                            kfm[:, _c * D:(_c + 1) * D], ptr[_c][:]), [(ve, 1)])
                        kfm_evac[gc] = ve.n
                    xnf_used[p] = pe.n
                    for c in range(2):
                        gc = 2 * t + c
                        B.wait("tensor", ve, kfm_evac[gc])
                        B.wait("tensor", se, v_evac[gc])
                        if c == 0:
                            B.wait("tensor", ve, q_evac[(t, NH - 1)])
                        for h in range(NH):
                            bk = h % 2
                            if h >= 2:
                                B.wait("tensor", ve, ad_evac[(gc, h - 2)])
                            elif gc >= 1 and c == 1:
                                B.wait("tensor", ve, ad_evac[(gc - 1, 6 + bk)])
                            B.op("tensor", lambda e, _c=c, _h=h, _b=bk: e.matmul(
                                pa[_b], kfm[:, _c * D + _h * 128:_c * D + (_h + 1) * 128],
                                qt[:, _h * T + _c * 128:_h * T + (_c + 1) * 128],
                                start=True, stop=True), [(pe, 1)])
                            a_done[(gc, h)] = pe.n
                            B.wait("vector", pe, a_done[(gc, h)])
                            prev_ad = (gc, h - 2) if h >= 2 else ((gc - 1, 6 + bk) if gc >= 1 else None)
                            if prev_ad is not None and prev_ad in y_done:
                                B.wait("vector", pe, y_done[prev_ad])
                            B.op("vector", lambda e, _h=h, _b=bk: e.tensor_tensor(
                                adb[:, _b * 128:(_b + 1) * 128], pa[_b],
                                mask_sb[:, _h * 128:(_h + 1) * 128], op=OP.mult), [(ve, 1)])
                            ad_evac[(gc, h)] = ve.n
                            B.wait("tensor", ve, ad_evac[(gc, h)])
                            if h >= 2:
                                B.wait("tensor", se, ysq_done[(gc, h - 2)])
                            elif c == 1:
                                B.wait("tensor", se, ysq_done[(gc - 1, 6 + bk)])
                            else:
                                B.wait("tensor", ve, ktm_evac[gc + 1])
                            if gc >= 1:
                                B.wait("tensor", se, s0cast[(gc - 1, h)])
                            B.op("tensor", lambda e, _c=c, _h=h, _b=bk: e.matmul(
                                py[_b], vtm[:, _c * D + _h * 128:_c * D + (_h + 1) * 128],
                                adb[:, _b * 128:(_b + 1) * 128], start=True, stop=False))
                            B.op("tensor", lambda e, _c=c, _h=h, _b=bk: e.matmul(
                                py[_b], s0b[:, _h * DH:(_h + 1) * DH],
                                qt[:, _h * T + _c * 128:_h * T + (_c + 1) * 128],
                                start=False, stop=True), [(pe, 1)])
                            y_done[(gc, h)] = pe.n
                            if h >= 2:
                                B.wait("tensor", ve, supd[(gc, h - 2)])
                            elif c == 1:
                                B.wait("tensor", ve, supd[(gc - 1, 6 + bk)])
                            else:
                                B.wait("tensor", se, v_evac[gc + 1])
                            B.op("tensor", lambda e, _c=c, _h=h, _b=bk: e.matmul(
                                pkv[_b], ktm[:, _c * D + _h * 128:_c * D + (_h + 1) * 128],
                                vtm[:, _c * D + _h * 128:_c * D + (_h + 1) * 128],
                                start=True, stop=True), [(pe, 1)])
                            kv_done[(gc, h)] = pe.n
                            B.wait("scalar", pe, y_done[(gc, h)])
                            B.op("scalar", lambda e, _gc=gc, _h=h, _b=bk: e.activation(
                                ysb[:, (_h * 32 + _gc) * 128:(_h * 32 + _gc + 1) * 128],
                                py[_b], AF.Copy,
                                accum_out=ysum[:, _h * 32 + _gc:_h * 32 + _gc + 1]), [(se, 1)])
                            junk_guard("scalar", ("ysq", 0), se)
                            B.op("scalar", lambda e, _gc=gc, _h=h, _b=bk: e.activation(
                                junk_s[:, 0:128], py[_b], AF.Square,
                                accum_out=ysq[:, _h * 32 + _gc:_h * 32 + _gc + 1]), [(se, 1)])
                            junk_mark(("ysq", 0), se)
                            ysq_done[(gc, h)] = se.n
                            B.wait("vector", pe, kv_done[(gc, h)])
                            if gc >= 1:
                                B.wait("vector", se, s0cast[(gc - 1, h)])
                            B.op("vector", lambda e, _h=h, _b=bk, _g=gam128[h]: e.scalar_tensor_tensor(
                                s0f[:, _h * DH:(_h + 1) * DH], s0f[:, _h * DH:(_h + 1) * DH],
                                _g, pkv[_b], op0=OP.mult, op1=OP.add), [(ve, 1)])
                            supd[(gc, h)] = ve.n
                            B.wait("scalar", ve, supd[(gc, h)])
                            B.op("scalar", lambda e, _h=h: e.activation(
                                s0b[:, _h * DH:(_h + 1) * DH], s0f[:, _h * DH:(_h + 1) * DH],
                                AF.Copy), [(se, 1)])
                            s0cast[(gc, h)] = se.n

                B.wait("vector", se, ysq_done[(2 * NT - 1, NH - 1)])
                for h in range(NH):
                    junk_guard("vector", ("gn", 0), ve)
                    B.op("vector", lambda e, _h=h: e.tensor_scalar(
                        junk_v[:, 0:32], ysum[:, _h * 32:(_h + 1) * 32], 1.0, None,
                        op0=OP.mult, op1=OP.add, accum_out=gns[:, _h:_h + 1]), [(ve, 1)])
                    junk_mark(("gn", 0), ve)
                    B.op("vector", lambda e: e.drain())
                for h in range(NH):
                    junk_guard("vector", ("gn", 0), ve)
                    B.op("vector", lambda e, _h=h: e.tensor_scalar(
                        junk_v[:, 0:32], ysq[:, _h * 32:(_h + 1) * 32], 1.0, None,
                        op0=OP.mult, op1=OP.add, accum_out=gns[:, 8 + _h:9 + _h]), [(ve, 1)])
                    junk_mark(("gn", 0), ve)
                    B.op("vector", lambda e: e.drain())
                B.op("vector", lambda e: e.drain(), [(ve, 1)])
                gsum_ve = ve.n
                B.wait("tensor", ve, gsum_ve)
                B.wait("tensor", se, ysq_done[(2 * NT - 1, NH - 1)])
                B.op("tensor", lambda e: e.matmul(
                    P[0:1, 3072:3088], ones_sb[:, 0:1], gns[:, 0:16],
                    start=True, stop=True), [(pe, 1)])
                B.wait("vector", pe, pe.n)
                B.op("vector", lambda e: e.tensor_copy(gns[0:1, 16:32], P[0:1, 3072:3088]),
                     [(ve, 1)])
                B.wait("tensor", ve, ve.n)
                B.op("tensor", lambda e: e.matmul(
                    P[:, 3584:3600], ones_sb[0:1, 0:128], gns[0:1, 16:32],
                    start=True, stop=True), [(pe, 1)])
                pb = P[:, 3584:3600]
                B.wait("vector", pe, pe.n)
                B.op("vector", lambda e: e.tensor_copy(gns[:, 0:16], pb[:]))
                B.op("vector", lambda e: e.drain())
                B.op("vector", lambda e: e.scalar_tensor_tensor(
                    gns[:, 48:56], gns[:, 0:8], 1.0 / (GN_N * GN_N), gns[:, 0:8],
                    op0=OP.mult, op1=OP.mult))
                B.op("vector", lambda e: e.tensor_scalar(
                    gns[:, 16:24], gns[:, 0:8], 1.0 / GN_N, None, op0=OP.mult))
                B.op("vector", lambda e: e.drain())
                B.op("vector", lambda e: e.scalar_tensor_tensor(
                    gns[:, 24:32], gns[:, 8:16], 1.0 / GN_N, gns[:, 48:56],
                    op0=OP.mult, op1=OP.subtract), [(ve, 1)])
                B.wait("scalar", ve, ve.n)
                B.op("scalar", lambda e: e.activation(
                    gns[:, 32:40], gns[:, 24:32], AF.Sqrt, bias=np.float32(EPS)), [(se, 1)])
                B.wait("vector", se, se.n)
                B.op("vector", lambda e: e.reciprocal(gns[:, 40:48], gns[:, 32:40]))
                B.op("vector", lambda e: e.drain())
                B.op("vector", lambda e: e.tensor_tensor(
                    gns[:, 48:56], gns[:, 40:48], gnw_sb[:], op=OP.mult))
                B.op("vector", lambda e: e.scalar_tensor_tensor(
                    gns[:, 56:64], gns[:, 16:24], -1.0, gns[:, 40:48],
                    op0=OP.mult, op1=OP.mult))
                B.op("vector", lambda e: e.drain())
                B.op("vector", lambda e: e.scalar_tensor_tensor(
                    gns[:, 16:24], gns[:, 56:64], 1.0, gnw_sb[:],
                    op0=OP.mult, op1=OP.mult))
                B.op("vector", lambda e: e.drain())
                B.op("vector", lambda e: e.tensor_tensor(
                    gns[:, 24:32], gns[:, 16:24], gnb_sb[:], op=OP.add), [(ve, 1)])
                gn_ve = ve.n

                po = [P[:, 0:1024], P[:, 1024:2048]]
                store_done = {}
                wo_done = {}
                resv = {}
                for t in range(NT):
                    p = t % 2
                    xin_p = xin[:, p * 2 * D:(p + 1) * 2 * D]
                    if t >= 2:
                        B.wait("sync", st_s, store_done[p])
                    for _c in range(2):
                        B.op("sync", lambda e, _t=t, _x=xin_p, _cc=_c: e.dma_start(
                            out=_x[:, _cc * D:(_cc + 1) * D], in_=tile_half(x1s, _t, _cc)), [(ld_x, 16)])
                    x_ld = ld_x.n
                    B.wait("scalar", ve, gn_ve)
                    if t >= 1:
                        B.wait("scalar", pe, wo_done[t - 1])
                    for h in range(NH):
                        B.op("scalar", lambda e, _t=t, _h=h: e.activation(
                            ynb[:, _h * T:(_h + 1) * T],
                            ysb[:, _h * L + _t * T:_h * L + (_t + 1) * T],
                            AF.Identity, scale=gns[:, 48 + _h:49 + _h],
                            bias=gns[:, 24 + _h:25 + _h]), [(se, 1)])
                    yn_se = se.n
                    B.wait("tensor", se, yn_se)
                    if t >= 2:
                        B.wait("tensor", ve, resv[t - 2])
                    for c in range(2):
                        if c == 1:
                            B.wait("tensor", ve, ve.n)
                        for eh in range(NH):
                            B.op("tensor", lambda e, _p=p, _c=c, _e=eh: e.matmul(
                                po[_p][:, 0:512], ynb[:, _e * T + _c * 128:_e * T + (_c + 1) * 128],
                                wo_sb[:, _e * D + 0:_e * D + 512],
                                start=(_e == 0), stop=(_e == 7)))
                            B.op("tensor", lambda e, _p=p, _c=c, _e=eh: e.matmul(
                                po[_p][:, 512:1024], ynb[:, _e * T + _c * 128:_e * T + (_c + 1) * 128],
                                wo_sb[:, _e * D + 512:_e * D + 1024],
                                start=(_e == 0), stop=(_e == 7)),
                                [(pe, 1)] if eh == 7 else ())
                        wope = pe.n
                        B.wait("vector", pe, wope)
                        B.wait("vector", ld_x, x_ld)
                        B.op("vector", lambda e, _c=c, _x=xin_p, _p=p: e.scalar_tensor_tensor(
                            _x[:, _c * D:(_c + 1) * D], po[_p], 1.0, _x[:, _c * D:(_c + 1) * D],
                            op0=OP.mult, op1=OP.add), [(ve, 1)])
                    wo_done[t] = pe.n
                    resv[t] = ve.n
                    res_ve = ve.n
                    B.wait("sync", ve, res_ve)
                    for _c in range(2):
                        B.op("sync", lambda e, _t=t, _x=xin_p, _cc=_c: e.dma_start(
                            out=tile_half(x2s, _t, _cc), in_=_x[:, _cc * D:(_cc + 1) * D]), [(st_s, 16)])
                    store_done[p] = st_s.n
            barrier()
            build_lnpass(x2s, xn2, "2")
            barrier()

        if n_phases >= 4:
            w3scope = ExitStack()
            wg3 = w3scope.enter_context(nc.sbuf_tensor("wg3_sb", [128, 8 * HF], BF16))
            wu3 = w3scope.enter_context(nc.sbuf_tensor("wu3_sb", [128, 8 * HF], BF16))
            wd3 = w3scope.enter_context(nc.sbuf_tensor("wd3_sb", [128, 24 * D], BF16))
            wst3 = w3scope.enter_context(nc.sbuf_tensor("wstage3", [128, HF], I32))
            w3_done = ffn_weight_prologue(wg3, wu3, wd3, wst3)
            build_ffn(xn2, x2s, out_hbm, None, wg3, wu3, wd3, w3_done, final=True, tag="3")
            w3scope.close()

        if n_phases >= 4:
            B.wait("sync", st_o, st_o.n)
        else:
            B.wait("sync", st_s, st_s.n)
        build_nc.last_sched = B.snapshot()
        B.flush_block()

    return nc


_NC_CACHE = {}


def _get_nc(n_phases=4):
    if n_phases not in _NC_CACHE:
        _NC_CACHE[n_phases] = build_nc(n_phases)
    return _NC_CACHE[n_phases]


def _host_constants():
    gam = _gammas()
    ident = np.eye(128, dtype=BF)
    j = np.arange(128)[:, None]
    i = np.arange(128)[None, :]
    maskc = np.zeros((128, NH * 128), dtype=np.float64)
    for h in range(NH):
        maskc[:, h * 128:(h + 1) * 128] = np.where(i >= j, gam[h] ** (-128.0), 0.0)
    qdec = np.zeros((128, NH * T), dtype=np.float64)
    ii = np.arange(T) % 128
    for h in range(NH):
        qdec[:, h * T:(h + 1) * T] = (gam[h] ** (ii + 1.0))[None, :]
    kdec = np.zeros((128, D), dtype=np.float64)
    jj = np.arange(128)
    for h in range(NH):
        kdec[:, h * 128:(h + 1) * 128] = (gam[h] ** (127.0 - jj))[:, None]
    return ident, maskc.astype(BF), qdec.astype(BF), kdec.astype(BF)


def kernel(L0_attn_x, L0_attn_Wq, L0_attn_Wk, L0_attn_Wv, L0_attn_Wo,
           L0_attn_gn_weight, L0_attn_gn_bias,
           L0_ffn_w_gate, L0_ffn_w_up, L0_ffn_w_down, _n_phases=4):
    x = np.ascontiguousarray(np.asarray(L0_attn_x, dtype=np.float32))
    wq = np.ascontiguousarray(np.asarray(L0_attn_Wq, dtype=np.float32).T)
    wk = np.ascontiguousarray(np.asarray(L0_attn_Wk, dtype=np.float32).T)
    wv = np.ascontiguousarray(np.asarray(L0_attn_Wv, dtype=np.float32).T)
    wo = np.ascontiguousarray(np.asarray(L0_attn_Wo, dtype=np.float32).T)
    wg = np.ascontiguousarray(np.asarray(L0_ffn_w_gate, dtype=np.int32).T)
    wu = np.ascontiguousarray(np.asarray(L0_ffn_w_up, dtype=np.int32).T)
    wd = np.ascontiguousarray(np.asarray(L0_ffn_w_down, dtype=np.int32).T)
    gnw = np.ascontiguousarray(np.asarray(L0_attn_gn_weight, dtype=np.float32).reshape(NH, 128).T)
    gnb = np.ascontiguousarray(np.asarray(L0_attn_gn_bias, dtype=np.float32).reshape(NH, 128).T)
    ident, maskc, qdec, kdec = _host_constants()

    nc = _get_nc(_n_phases)
    shared = {"wqT": wq, "wkT": wk, "wvT": wv, "woT": wo,
              "wgT": wg, "wuT": wu, "wdT": wd,
              "gnw": gnw, "gnb": gnb, "ident": ident,
              "maskc": maskc, "qdec": qdec, "kdec": kdec}
    in_maps = [dict(shared, x=np.ascontiguousarray(x[b])) for b in range(8)]
    res = run_bass_kernel_spmd(nc, in_maps, core_ids=list(range(8)))
    out = np.stack([res.results[b]["out"] for b in range(8)], axis=0)
    return out


# version 03
# speedup vs baseline: 1.0183x; 1.0183x over previous
import numpy as np
import math
from contextlib import ExitStack
ExitSt = ExitStack
import ml_dtypes
import concourse.bass as bass
import concourse.mybir as mybir
from concourse.bass_utils import run_bass_kernel_spmd

F32 = mybir.dt.float32
BF16 = mybir.dt.bfloat16
I32 = mybir.dt.int32
OP = mybir.AluOpType
AF = mybir.ActivationFunctionType
BF = ml_dtypes.bfloat16

L = 4096
D = 1024
HF = 3072
NH = 8
DH = 128
T = 256
NT = L // T
FFN_TILES = NT
ATTN_TILES = NT
EPS = 1e-5
GN_N = float(L * DH)

ENGINES = ("sync", "tensor", "vector", "scalar", "gpsimd")


def _gammas():
    return 1.0 - np.exp(np.linspace(math.log(1.0 / 32.0), math.log(1.0 / 512.0), NH))


class Sem:
    __slots__ = ("h", "n", "name")


class Sched:
    def __init__(self, nc, stack):
        self.nc = nc
        self.stack = stack
        self.q = {e: [] for e in ENGINES}

    def sem(self, name):
        s = Sem()
        s.h = self.stack.enter_context(self.nc.semaphore(name))
        s.n = 0
        s.name = name
        return s

    def wait(self, eng, sem, val=None):
        v = sem.n if val is None else val
        if v <= 0:
            return
        f = lambda e, _h=sem.h, _v=v: e.wait_ge(_h, _v)
        f.meta = ("wait", sem.name, v)
        self.q[eng].append(f)

    def op(self, eng, fn, incs=()):
        incs = tuple(incs)

        def run(e, _fn=fn, _incs=incs):
            ins = _fn(e)
            for s, k in _incs:
                ins.then_inc(s.h, k)

        run.meta = ("op", tuple((s.name, k) for s, k in incs), getattr(fn, "__qualname__", ""))
        self.q[eng].append(run)
        for s, k in incs:
            s.n += k

    def snapshot(self):
        return {e: list(self.q[e]) for e in ENGINES}

    def flush_block(self):
        with self.nc.Block() as blk:
            for eng in ENGINES:
                ops = self.q[eng]

                def body(e, _ops=ops):
                    for f in _ops:
                        f(e)

                getattr(blk, eng)(body)
        self.q = {e: [] for e in ENGINES}


def build_nc(n_phases=4):
    nc = bass.Bass("TRN2", target_bir_lowering=False)

    x_in = nc.dram_tensor("x", [L, D], F32, kind="ExternalInput").ap()
    wqT = nc.dram_tensor("wqT", [D, D], F32, kind="ExternalInput").ap()
    wkT = nc.dram_tensor("wkT", [D, D], F32, kind="ExternalInput").ap()
    wvT = nc.dram_tensor("wvT", [D, D], F32, kind="ExternalInput").ap()
    woT = nc.dram_tensor("woT", [D, D], F32, kind="ExternalInput").ap()
    wgT = nc.dram_tensor("wgT", [D, HF], I32, kind="ExternalInput").ap()
    wuT = nc.dram_tensor("wuT", [D, HF], I32, kind="ExternalInput").ap()
    wdT = nc.dram_tensor("wdT", [HF, D], I32, kind="ExternalInput").ap()
    gnw_in = nc.dram_tensor("gnw", [128, NH], F32, kind="ExternalInput").ap()
    gnb_in = nc.dram_tensor("gnb", [128, NH], F32, kind="ExternalInput").ap()
    ident_in = nc.dram_tensor("ident", [128, 128], BF16, kind="ExternalInput").ap()
    mask_in = nc.dram_tensor("maskc", [128, NH * 128], BF16, kind="ExternalInput").ap()
    qdec_in = nc.dram_tensor("qdec", [128, NH * T], BF16, kind="ExternalInput").ap()
    kdec_in = nc.dram_tensor("kdec", [128, D], BF16, kind="ExternalInput").ap()
    out_hbm = nc.dram_tensor("out", [L, D], F32, kind="ExternalOutput").ap()

    xn0 = nc.dram_tensor("xn0_scr", [L, D], BF16, kind="Internal").ap()
    xn1 = nc.dram_tensor("xn1_scr", [L, D], BF16, kind="Internal").ap()
    xn2 = nc.dram_tensor("xn2_scr", [L, D], BF16, kind="Internal").ap()
    x1s = nc.dram_tensor("x1_scr", [L, D], F32, kind="Internal").ap()
    x2s = nc.dram_tensor("x2_scr", [L, D], F32, kind="Internal").ap()

    gam = _gammas()
    gam128 = [float(g) ** 128 for g in gam]

    def tile_half(ap, t, c):
        return ap[t * T + c * 128:t * T + (c + 1) * 128, :]

    with ExitStack() as ES:
        B = Sched(nc, ES)
        ld_xp = [B.sem("ld_x0"), B.sem("ld_x1")]
        ld_tp = [B.sem("ld_t0"), B.sem("ld_t1")]
        ld_c = B.sem("ld_c")
        ld_w = B.sem("ld_w")
        st_sp = [B.sem("st_s0"), B.sem("st_s1")]
        pe = B.sem("pe")
        ve = B.sem("ve")
        se = B.sem("se")
        ge = B.sem("ge")
        bar = B.sem("bar")

        junk = ES.enter_context(nc.sbuf_tensor("junkbuf", [128, 4 * D], BF16))
        junk_v = junk[:, 0:D]
        junk_v1 = junk[:, D:2 * D]
        junk_s = junk[:, 2 * D:3 * D]
        junk_s1 = junk[:, 3 * D:4 * D]
        junk_last = {}

        def junk_guard(eng, key, sem):
            if key in junk_last:
                s_, v_ = junk_last[key]
                B.wait(eng, s_, v_)

        def junk_mark(key, sem):
            junk_last[key] = (sem, sem.n)
        ident_sb = ES.enter_context(nc.sbuf_tensor("ident_sb", [128, 128], BF16))
        ones_sb = ES.enter_context(nc.sbuf_tensor("ones_sb", [128, 128], F32))
        psum = ES.enter_context(nc.psum_tensor("psum_all", [128, 4096], F32))
        P = psum.ap()
        P_bf = P.bitcast(BF16)

        nbar = [0]

        def barrier():
            for s_ in (ld_xp[0], ld_xp[1], ld_tp[0], ld_tp[1], ld_c, st_sp[0], st_sp[1]):
                B.wait("sync", s_, s_.n)
            B.wait("gpsimd", ld_w, ld_w.n)
            for eng in ENGINES:
                if eng == "sync":
                    B.op(eng, lambda e: e.nop(), [(bar, 1)])
                else:
                    B.op(eng, lambda e: e.drain(), [(bar, 1)])
            nbar[0] += 1
            for eng in ENGINES:
                B.wait(eng, bar, 5 * nbar[0])

        B.op("sync", lambda e: e.dma_start(out=ident_sb[:], in_=ident_in), [(ld_c, 16)])
        B.op("vector", lambda e: e.memset(ones_sb[:], 1.0), [(ve, 1)])

        def ln_finalize_apply(stats, x_tile, xn_tile, sumsq_ready_se):
            B.wait("vector", se, sumsq_ready_se)
            B.op("vector", lambda e, _s=stats: e.drain())
            B.op("vector", lambda e, _s=stats: e.scalar_tensor_tensor(
                _s[:, 4:6], _s[:, 0:2], 1.0 / (D * D), _s[:, 0:2],
                op0=OP.mult, op1=OP.mult))
            B.op("vector", lambda e, _s=stats: e.tensor_scalar(
                _s[:, 6:8], _s[:, 0:2], 1.0 / D, None, op0=OP.mult))
            B.op("vector", lambda e: e.drain())
            B.op("vector", lambda e, _s=stats: e.scalar_tensor_tensor(
                _s[:, 8:10], _s[:, 2:4], 1.0 / D, _s[:, 4:6],
                op0=OP.mult, op1=OP.subtract), [(ve, 1)])
            B.wait("scalar", ve, ve.n)
            B.op("scalar", lambda e, _s=stats: e.activation(
                _s[:, 10:12], _s[:, 8:10], AF.Sqrt, bias=np.float32(EPS)), [(se, 1)])
            B.wait("vector", se, se.n)
            B.op("vector", lambda e, _s=stats: e.reciprocal(_s[:, 12:14], _s[:, 10:12]))
            B.op("vector", lambda e: e.drain())
            B.op("vector", lambda e, _s=stats: e.scalar_tensor_tensor(
                _s[:, 14:16], _s[:, 6:8], -1.0, _s[:, 12:14],
                op0=OP.mult, op1=OP.mult), [(ve, 1)])
            B.wait("scalar", ve, ve.n)
            for c in range(2):
                B.op("scalar", lambda e, _s=stats, _x=x_tile, _o=xn_tile, _c=c: e.activation(
                    _o[:, _c * D:(_c + 1) * D], _x[:, _c * D:(_c + 1) * D],
                    AF.Identity, bias=_s[:, 14 + _c:15 + _c],
                    scale=_s[:, 12 + _c:13 + _c]), [(se, 1)])
            return se.n

        def ffn_weight_prologue(wg_sb, wu_sb, wd_sb, wstage):
            for wdram, wsb, sc in ((wgT, wg_sb, 1.0 / 127.0), (wuT, wu_sb, 1.0 / 127.0)):
                for d in range(8):
                    B.wait("gpsimd", ge, ge.n)
                    B.op("gpsimd", lambda e, _w=wdram, _d=d, _st=wstage: e.dma_start(
                        out=_st[:], in_=_w[_d * 128:(_d + 1) * 128, :]), [(ld_w, 16)])
                    B.wait("gpsimd", ld_w, ld_w.n)
                    B.op("gpsimd", lambda e, _sb=wsb, _d=d, _st=wstage, _sc=sc: e.tensor_scalar(
                        _sb[:, _d * HF:(_d + 1) * HF], _st[:], _sc, None,
                        op0=OP.mult), [(ge, 1)])
            for h in range(24):
                B.wait("gpsimd", ge, ge.n)
                B.op("gpsimd", lambda e, _h=h, _st=wstage: e.dma_start(
                    out=_st[:, 0:D], in_=wdT[_h * 128:(_h + 1) * 128, :]), [(ld_w, 16)])
                B.wait("gpsimd", ld_w, ld_w.n)
                B.op("gpsimd", lambda e, _h=h, _st=wstage: e.tensor_scalar(
                    wd_sb[:, _h * D:(_h + 1) * D], _st[:, 0:D], 0.5 / 127.0, None,
                    op0=OP.mult), [(ge, 1)])
            return ge.n

        w1scope = ExitStack()
        w1_done = 0
        if n_phases >= 2:
            wg1 = w1scope.enter_context(nc.sbuf_tensor("wg_sb", [128, 8 * HF], BF16))
            wu1 = w1scope.enter_context(nc.sbuf_tensor("wu_sb", [128, 8 * HF], BF16))
            wd1 = w1scope.enter_context(nc.sbuf_tensor("wd_sb", [128, 24 * D], BF16))
            wst1 = w1scope.enter_context(nc.sbuf_tensor("wstage", [128, HF], I32))
            w1_done = ffn_weight_prologue(wg1, wu1, wd1, wst1)

        def build_lnpass(src_dram, xn_dram, tag):
            with nc.sbuf_tensor("pp_xin" + tag, [128, 2 * 2 * D], F32) as xin, \
                 nc.sbuf_tensor("pp_xn" + tag, [128, 2 * 2 * D], BF16) as xno, \
                 nc.sbuf_tensor("pp_stats" + tag, [128, 2 * 16], F32) as statsb:
                store_done = {}
                for t in range(NT):
                    p = t % 2
                    ld_x = ld_xp[p]
                    st_s = st_sp[p]
                    xin_p = xin[:, p * 2 * D:(p + 1) * 2 * D]
                    xno_p = xno[:, p * 2 * D:(p + 1) * 2 * D]
                    stats = statsb[:, p * 16:(p + 1) * 16]
                    if t >= 2:
                        B.wait("sync", st_s, store_done[p])
                        B.wait("scalar", st_s, store_done[p])
                    for _c in range(2):
                        B.op("sync", lambda e, _t=t, _x=xin_p, _cc=_c: e.dma_start(
                            out=_x[:, _cc * D:(_cc + 1) * D], in_=tile_half(src_dram, _t, _cc)), [(ld_x, 16)])
                    x_ld = ld_x.n
                    B.wait("vector", ld_x, x_ld)
                    for c in range(2):
                        jv = junk_v if c == 0 else junk_v1
                        junk_guard("vector", ("v", c), ve)
                        B.op("vector", lambda e, _x=xin_p, _s=stats, _c=c, _j=jv: e.tensor_scalar(
                            _j[:], _x[:, _c * D:(_c + 1) * D], 1.0, None, op0=OP.mult,
                            op1=OP.add, accum_out=_s[:, 0 + _c:1 + _c]), [(ve, 1)])
                        junk_mark(("v", c), ve)
                        B.op("vector", lambda e: e.drain())
                    B.wait("scalar", ld_x, x_ld)
                    for c in range(2):
                        js = junk_s if c == 0 else junk_s1
                        junk_guard("scalar", ("s", c), se)
                        B.op("scalar", lambda e, _x=xin_p, _s=stats, _c=c, _j=js: e.activation(
                            _j[:], _x[:, _c * D:(_c + 1) * D], AF.Square,
                            accum_out=_s[:, 2 + _c:3 + _c]), [(se, 1)])
                        junk_mark(("s", c), se)
                    apply_se = ln_finalize_apply(stats, xin_p, xno_p, se.n)
                    B.wait("sync", se, apply_se)
                    for _c in range(2):
                        B.op("sync", lambda e, _t=t, _x=xno_p, _cc=_c: e.dma_start(
                            out=tile_half(xn_dram, _t, _cc), in_=_x[:, _cc * D:(_cc + 1) * D]), [(st_s, 16)])
                    store_done[p] = st_s.n

        if n_phases >= 1:
            build_lnpass(x_in, xn0, "0")
            barrier()

        def build_ffn(xn_src, x_src, x_dst, xn_dst, wg_sb, wu_sb, wd_sb,
                      weights_ready_ge, final, tag=""):
            with nc.sbuf_tensor("f_xnf" + tag, [128, 2 * 8 * T], BF16) as xnf, \
                 nc.sbuf_tensor("f_xin" + tag, [128, 2 * 2 * D], F32) as xin, \
                 nc.sbuf_tensor("f_hid" + tag, [128, 24 * T], BF16) as hid, \
                 nc.sbuf_tensor("f_gsb" + tag, [128, 2 * T], F32) as gsb:
                pg = [P[:, 0:T], P[:, 3072:3072 + T]]
                pu = [P[:, 512:512 + T], P[:, 3584:3584 + T]]
                pd = [P[:, 1024:2048], P[:, 2048:3072]]
                silu_done = {}
                hid_done = {}
                res_done = [0]
                store_done = {}
                xnf_used = {}
                B.wait("tensor", ge, weights_ready_ge)
                for t in range(FFN_TILES):
                    p = t % 2
                    ld_x = ld_xp[p]
                    ld_t = ld_tp[p]
                    st_out = st_sp[p]
                    xin_p = xin[:, p * 2 * D:(p + 1) * 2 * D]
                    xnf_p = [xnf[:, (p * 8 + d) * T:(p * 8 + d + 1) * T] for d in range(8)]
                    if t >= 2:
                        B.wait("sync", st_out, store_done[p])
                        B.wait("sync", pe, xnf_used[p])
                    for d in range(8):
                        B.op("sync", lambda e, _t=t, _d=d, _o=xnf_p[d]: e.dma_start_transpose(
                            out=_o, in_=xn_src[_t * T:(_t + 1) * T, _d * 128:(_d + 1) * 128]),
                            [(ld_t, 16)])
                    xnf_ld = ld_t.n
                    for _c in range(2):
                        B.op("sync", lambda e, _t=t, _x=xin_p, _cc=_c: e.dma_start(
                            out=_x[:, _cc * D:(_cc + 1) * D], in_=tile_half(x_src, _t, _cc)), [(ld_x, 16)])
                    x_ld = ld_x.n

                    B.wait("tensor", ld_t, xnf_ld)
                    if t >= 1:
                        B.wait("tensor", ve, res_done[0])

                    def down_mms(ha, hp):
                        B.wait("tensor", ve, hid_done[ha])
                        for c in range(2):
                            for half in range(2):
                                last = (hp == 23 and c == 1 and half == 1)
                                B.op("tensor", lambda e, _h=hp, _c=c, _f=half: e.matmul(
                                    pd[_c][:, _f * 512:(_f + 1) * 512],
                                    hid[:, _h * T + _c * 128:_h * T + (_c + 1) * 128],
                                    wd_sb[:, _h * D + _f * 512:_h * D + (_f + 1) * 512],
                                    start=(_h == 0), stop=(_h == 23)),
                                    [(pe, 1)] if last else ())

                    for h in range(24):
                        ha = t * 24 + h
                        bk = h % 2
                        if ha >= 2:
                            B.wait("tensor", se, silu_done[ha - 2])
                        for d in range(8):
                            B.op("tensor", lambda e, _h=h, _d=d, _b=bk, _xn=xnf_p[d]: e.matmul(
                                pg[_b], wg_sb[:, _d * HF + _h * 128:_d * HF + (_h + 1) * 128],
                                _xn, start=(_d == 0), stop=(_d == 7)),
                                [(pe, 1)] if d == 7 else ())
                        gate_pe = pe.n
                        if ha >= 2:
                            B.wait("tensor", ve, hid_done[ha - 2])
                        for d in range(8):
                            B.op("tensor", lambda e, _h=h, _d=d, _b=bk, _xn=xnf_p[d]: e.matmul(
                                pu[_b], wu_sb[:, _d * HF + _h * 128:_d * HF + (_h + 1) * 128],
                                _xn, start=(_d == 0), stop=(_d == 7)),
                                [(pe, 1)] if d == 7 else ())
                        up_pe = pe.n
                        if h >= 1:
                            down_mms(ha - 1, h - 1)
                        B.wait("scalar", pe, gate_pe)
                        if ha >= 2:
                            B.wait("scalar", ve, hid_done[ha - 2])
                        B.op("scalar", lambda e, _b=bk: e.activation(
                            gsb[:, _b * T:(_b + 1) * T], pg[_b], AF.Silu), [(se, 1)])
                        silu_done[ha] = se.n
                        B.wait("vector", se, silu_done[ha])
                        B.wait("vector", pe, up_pe)
                        B.op("vector", lambda e, _h=h, _b=bk: e.tensor_tensor(
                            hid[:, _h * T:(_h + 1) * T], pu[_b], gsb[:, _b * T:(_b + 1) * T],
                            op=OP.mult), [(ve, 1)])
                        hid_done[ha] = ve.n
                    down_mms(t * 24 + 23, 23)
                    down_pe = pe.n
                    xnf_used[p] = pe.n
                    B.wait("vector", pe, down_pe)
                    B.wait("vector", ld_x, x_ld)
                    for c in range(2):
                        B.op("vector", lambda e, _c=c, _x=xin_p: e.scalar_tensor_tensor(
                            _x[:, _c * D:(_c + 1) * D], pd[_c], 1.0, _x[:, _c * D:(_c + 1) * D],
                            op0=OP.mult, op1=OP.add),
                            [(ve, 1)])
                    res_done[0] = ve.n
                    res_ve = ve.n
                    B.wait("sync", ve, res_ve)
                    for _c in range(2):
                        B.op("sync", lambda e, _t=t, _x=xin_p, _cc=_c: e.dma_start(
                            out=tile_half(x_dst, _t, _cc), in_=_x[:, _cc * D:(_cc + 1) * D]), [(st_out, 16)])
                    store_done[p] = st_out.n

        if n_phases >= 2:
            build_ffn(xn0, x_in, x1s, None, wg1, wu1, wd1, w1_done,
                      final=(n_phases == 2))
            barrier()
            if n_phases >= 3:
                build_lnpass(x1s, xn1, "1")
                barrier()
        w1scope.close()

        if n_phases >= 3:
            with ExitSt() as AS:
                al = AS.enter_context
                wq_sb = al(nc.sbuf_tensor("a_wq", [128, 8 * D], BF16))
                wk_sb = al(nc.sbuf_tensor("a_wk", [128, 8 * D], BF16))
                wv_sb = al(nc.sbuf_tensor("a_wv", [128, 8 * D], BF16))
                wo_sb = al(nc.sbuf_tensor("a_wo", [128, 8 * D], BF16))
                awst = al(nc.sbuf_tensor("a_wstage", [128, D], F32))
                ysb = al(nc.sbuf_tensor("a_ys", [128, NH * L], BF16))
                xnf = al(nc.sbuf_tensor("a_xnf", [128, 2 * 8 * T], BF16))
                qt = al(nc.sbuf_tensor("a_qt", [128, NH * T], BF16))
                ktm = al(nc.sbuf_tensor("a_ktm", [128, 2 * D], BF16))
                kfm = al(nc.sbuf_tensor("a_kfm", [128, 2 * D], BF16))
                vtm = al(nc.sbuf_tensor("a_vtm", [128, 2 * D], BF16))
                adb = al(nc.sbuf_tensor("a_ad", [128, 2 * 128], BF16))
                s0f = al(nc.sbuf_tensor("a_s0f", [128, NH * DH], F32))
                s0b = al(nc.sbuf_tensor("a_s0b", [128, NH * DH], BF16))
                mask_sb = al(nc.sbuf_tensor("a_mask", [128, NH * 128], BF16))
                qdec_sb = al(nc.sbuf_tensor("a_qdec", [128, NH * T], BF16))
                kdec_sb = al(nc.sbuf_tensor("a_kdec", [128, D], BF16))
                ysum = al(nc.sbuf_tensor("a_ysum", [128, NH * 32], F32))
                ysq = al(nc.sbuf_tensor("a_ysq", [128, NH * 32], F32))
                gnw_sb = al(nc.sbuf_tensor("a_gnw", [128, NH], F32))
                gnb_sb = al(nc.sbuf_tensor("a_gnb", [128, NH], F32))
                gns = al(nc.sbuf_tensor("a_gns", [128, 64], F32))
                ynb = al(nc.sbuf_tensor("a_yn", [128, NH * T], BF16))
                xin = al(nc.sbuf_tensor("a_xin", [128, 2 * 2 * D], F32))

                B.op("sync", lambda e: e.dma_start(out=mask_sb[:], in_=mask_in), [(ld_c, 16)])
                B.op("sync", lambda e: e.dma_start(out=qdec_sb[:], in_=qdec_in), [(ld_c, 16)])
                B.op("sync", lambda e: e.dma_start(out=kdec_sb[:], in_=kdec_in), [(ld_c, 16)])
                B.op("sync", lambda e: e.dma_start(out=gnw_sb[:], in_=gnw_in), [(ld_c, 16)])
                B.op("sync", lambda e: e.dma_start(out=gnb_sb[:], in_=gnb_in), [(ld_c, 16)])
                const_ld = ld_c.n
                for wdram, wsb in ((wqT, wq_sb), (wkT, wk_sb), (wvT, wv_sb), (woT, wo_sb)):
                    for d in range(8):
                        B.wait("gpsimd", ge, ge.n)
                        B.op("gpsimd", lambda e, _w=wdram, _d=d: e.dma_start(
                            out=awst[:], in_=_w[_d * 128:(_d + 1) * 128, :]), [(ld_w, 16)])
                        B.wait("gpsimd", ld_w, ld_w.n)
                        B.op("gpsimd", lambda e, _sb=wsb, _d=d: e.tensor_copy(
                            _sb[:, _d * D:(_d + 1) * D], awst[:]), [(ge, 1)])
                aw_done = ge.n
                B.op("vector", lambda e: e.memset(s0f[:], 0.0))
                B.op("vector", lambda e: e.memset(s0b[:], 0.0), [(ve, 1)])
                s0_init = ve.n
                B.wait("vector", ld_c, const_ld)

                paq = [P[:, 0:T], P[:, 512:512 + T]]
                pa = [P[:, 0:128], P[:, 512:640]]
                pk = P[:, 1024:2048]
                py = [P[:, 1024:1152], P[:, 1536:1664]]
                pv = P[:, 2048:3072]
                pkv = [P[:, 2048:2176], P[:, 2560:2688]]
                ptr = [P_bf[:, 6144:7168], P_bf[:, 7168:8192]]

                B.wait("tensor", ge, aw_done)
                B.wait("tensor", ve, s0_init)

                q_evac = {}
                ktm_evac = {}
                kfm_evac = {}
                v_evac = {}
                ad_evac = {}
                supd = {}
                s0cast = {}
                a_done = {}
                y_done = {}
                kv_done = {}
                ysq_done = {}
                xnf_used = {}

                for t in range(NT):
                    p = t % 2
                    ld_t = ld_tp[p]
                    xnf_p = [xnf[:, (p * 8 + d) * T:(p * 8 + d + 1) * T] for d in range(8)]
                    if t >= 2:
                        B.wait("sync", pe, xnf_used[p])
                    for d in range(8):
                        B.op("sync", lambda e, _t=t, _d=d, _o=xnf_p[d]: e.dma_start_transpose(
                            out=_o, in_=xn1[_t * T:(_t + 1) * T, _d * 128:(_d + 1) * 128]),
                            [(ld_t, 16)])
                    xnf_ld = ld_t.n
                    B.wait("tensor", ld_t, xnf_ld)
                    for h in range(NH):
                        bk = h % 2
                        if t >= 1:
                            B.wait("tensor", ve, ad_evac[(2 * t - 1, 6 + bk)])
                        if h >= 2:
                            B.wait("tensor", ve, q_evac[(t, h - 2)])
                        for d in range(8):
                            B.op("tensor", lambda e, _h=h, _d=d, _b=bk, _xn=xnf_p[d]: e.matmul(
                                paq[_b], wq_sb[:, _d * D + _h * 128:_d * D + (_h + 1) * 128],
                                _xn, start=(_d == 0), stop=(_d == 7)),
                                [(pe, 1)] if d == 7 else ())
                        qpe = pe.n
                        B.wait("vector", pe, qpe)
                        if t >= 1:
                            B.wait("vector", pe, y_done[(2 * t - 1, h)])
                        B.op("vector", lambda e, _h=h, _b=bk: e.tensor_tensor(
                            qt[:, _h * T:(_h + 1) * T], paq[_b], qdec_sb[:, _h * T:(_h + 1) * T],
                            op=OP.mult), [(ve, 1)])
                        q_evac[(t, h)] = ve.n
                    for c in range(2):
                        gc = 2 * t + c
                        if c == 0:
                            if gc >= 1:
                                B.wait("tensor", se, ysq_done[(gc - 1, 7)])
                        else:
                            B.wait("tensor", ve, ktm_evac[gc - 1])
                        for half in range(2):
                            for d in range(8):
                                B.op("tensor", lambda e, _c=c, _f=half, _d=d, _xn=xnf_p[d]: e.matmul(
                                    pk[:, _f * 512:(_f + 1) * 512],
                                    _xn[:, _c * 128:(_c + 1) * 128],
                                    wk_sb[:, _d * D + _f * 512:_d * D + (_f + 1) * 512],
                                    start=(_d == 0), stop=(_d == 7)),
                                    [(pe, 1)] if (d == 7 and half == 1) else ())
                        kpe = pe.n
                        B.wait("vector", pe, kpe)
                        if gc >= 2:
                            B.wait("vector", pe, kv_done[(gc - 2, 7)])
                        B.op("vector", lambda e, _c=c: e.tensor_tensor(
                            ktm[:, _c * D:(_c + 1) * D], pk[:], kdec_sb[:], op=OP.mult),
                            [(ve, 1)])
                        ktm_evac[gc] = ve.n
                        if c == 0:
                            if gc >= 1:
                                B.wait("tensor", ve, supd[(gc - 1, 7)])
                        else:
                            B.wait("tensor", se, v_evac[gc - 1])
                        for half in range(2):
                            for d in range(8):
                                B.op("tensor", lambda e, _c=c, _f=half, _d=d, _xn=xnf_p[d]: e.matmul(
                                    pv[:, _f * 512:(_f + 1) * 512],
                                    _xn[:, _c * 128:(_c + 1) * 128],
                                    wv_sb[:, _d * D + _f * 512:_d * D + (_f + 1) * 512],
                                    start=(_d == 0), stop=(_d == 7)),
                                    [(pe, 1)] if (d == 7 and half == 1) else ())
                        vpe = pe.n
                        B.wait("scalar", pe, vpe)
                        if gc >= 2:
                            B.wait("scalar", pe, kv_done[(gc - 2, 7)])
                        B.op("scalar", lambda e, _c=c: e.activation(
                            vtm[:, _c * D:(_c + 1) * D], pv[:], AF.Copy), [(se, 1)])
                        v_evac[gc] = se.n
                        B.wait("tensor", ve, ktm_evac[gc])
                        if gc >= 2:
                            B.wait("tensor", ve, kfm_evac[gc - 2])
                        for h in range(NH):
                            B.op("tensor", lambda e, _c=c, _h=h: e.transpose(
                                ptr[_c][:, _h * 128:(_h + 1) * 128],
                                ktm[:, _c * D + _h * 128:_c * D + (_h + 1) * 128],
                                ident_sb[:]), [(pe, 1)] if h == NH - 1 else ())
                        trpe = pe.n
                        B.wait("vector", pe, trpe)
                        if gc >= 2:
                            B.wait("vector", pe, a_done[(gc - 2, 7)])
                        B.op("vector", lambda e, _c=c: e.tensor_copy(
                            kfm[:, _c * D:(_c + 1) * D], ptr[_c][:]), [(ve, 1)])
                        kfm_evac[gc] = ve.n
                    xnf_used[p] = pe.n
                    for c in range(2):
                        gc = 2 * t + c
                        B.wait("tensor", ve, kfm_evac[gc])
                        B.wait("tensor", se, v_evac[gc])
                        if c == 0:
                            B.wait("tensor", ve, q_evac[(t, NH - 1)])
                        for h in range(NH):
                            bk = h % 2
                            if h >= 2:
                                B.wait("tensor", ve, ad_evac[(gc, h - 2)])
                            elif gc >= 1 and c == 1:
                                B.wait("tensor", ve, ad_evac[(gc - 1, 6 + bk)])
                            B.op("tensor", lambda e, _c=c, _h=h, _b=bk: e.matmul(
                                pa[_b], kfm[:, _c * D + _h * 128:_c * D + (_h + 1) * 128],
                                qt[:, _h * T + _c * 128:_h * T + (_c + 1) * 128],
                                start=True, stop=True), [(pe, 1)])
                            a_done[(gc, h)] = pe.n
                            B.wait("vector", pe, a_done[(gc, h)])
                            prev_ad = (gc, h - 2) if h >= 2 else ((gc - 1, 6 + bk) if gc >= 1 else None)
                            if prev_ad is not None and prev_ad in y_done:
                                B.wait("vector", pe, y_done[prev_ad])
                            B.op("vector", lambda e, _h=h, _b=bk: e.tensor_tensor(
                                adb[:, _b * 128:(_b + 1) * 128], pa[_b],
                                mask_sb[:, _h * 128:(_h + 1) * 128], op=OP.mult), [(ve, 1)])
                            ad_evac[(gc, h)] = ve.n
                            B.wait("tensor", ve, ad_evac[(gc, h)])
                            if h >= 2:
                                B.wait("tensor", se, ysq_done[(gc, h - 2)])
                            elif c == 1:
                                B.wait("tensor", se, ysq_done[(gc - 1, 6 + bk)])
                            else:
                                B.wait("tensor", ve, ktm_evac[gc + 1])
                            if gc >= 1:
                                B.wait("tensor", se, s0cast[(gc - 1, h)])
                            B.op("tensor", lambda e, _c=c, _h=h, _b=bk: e.matmul(
                                py[_b], vtm[:, _c * D + _h * 128:_c * D + (_h + 1) * 128],
                                adb[:, _b * 128:(_b + 1) * 128], start=True, stop=False))
                            B.op("tensor", lambda e, _c=c, _h=h, _b=bk: e.matmul(
                                py[_b], s0b[:, _h * DH:(_h + 1) * DH],
                                qt[:, _h * T + _c * 128:_h * T + (_c + 1) * 128],
                                start=False, stop=True), [(pe, 1)])
                            y_done[(gc, h)] = pe.n
                            if h >= 2:
                                B.wait("tensor", ve, supd[(gc, h - 2)])
                            elif c == 1:
                                B.wait("tensor", ve, supd[(gc - 1, 6 + bk)])
                            else:
                                B.wait("tensor", se, v_evac[gc + 1])
                            B.op("tensor", lambda e, _c=c, _h=h, _b=bk: e.matmul(
                                pkv[_b], ktm[:, _c * D + _h * 128:_c * D + (_h + 1) * 128],
                                vtm[:, _c * D + _h * 128:_c * D + (_h + 1) * 128],
                                start=True, stop=True), [(pe, 1)])
                            kv_done[(gc, h)] = pe.n
                            B.wait("scalar", pe, y_done[(gc, h)])
                            B.op("scalar", lambda e, _gc=gc, _h=h, _b=bk: e.activation(
                                ysb[:, (_h * 32 + _gc) * 128:(_h * 32 + _gc + 1) * 128],
                                py[_b], AF.Copy,
                                accum_out=ysum[:, _h * 32 + _gc:_h * 32 + _gc + 1]), [(se, 1)])
                            junk_guard("scalar", ("ysq", 0), se)
                            B.op("scalar", lambda e, _gc=gc, _h=h, _b=bk: e.activation(
                                junk_s[:, 0:128], py[_b], AF.Square,
                                accum_out=ysq[:, _h * 32 + _gc:_h * 32 + _gc + 1]), [(se, 1)])
                            junk_mark(("ysq", 0), se)
                            ysq_done[(gc, h)] = se.n
                            B.wait("vector", pe, kv_done[(gc, h)])
                            if gc >= 1:
                                B.wait("vector", se, s0cast[(gc - 1, h)])
                            B.op("vector", lambda e, _h=h, _b=bk, _g=gam128[h]: e.scalar_tensor_tensor(
                                s0f[:, _h * DH:(_h + 1) * DH], s0f[:, _h * DH:(_h + 1) * DH],
                                _g, pkv[_b], op0=OP.mult, op1=OP.add), [(ve, 1)])
                            supd[(gc, h)] = ve.n
                            B.wait("scalar", ve, supd[(gc, h)])
                            B.op("scalar", lambda e, _h=h: e.activation(
                                s0b[:, _h * DH:(_h + 1) * DH], s0f[:, _h * DH:(_h + 1) * DH],
                                AF.Copy), [(se, 1)])
                            s0cast[(gc, h)] = se.n

                B.wait("vector", se, ysq_done[(2 * NT - 1, NH - 1)])
                for h in range(NH):
                    junk_guard("vector", ("gn", 0), ve)
                    B.op("vector", lambda e, _h=h: e.tensor_scalar(
                        junk_v[:, 0:32], ysum[:, _h * 32:(_h + 1) * 32], 1.0, None,
                        op0=OP.mult, op1=OP.add, accum_out=gns[:, _h:_h + 1]), [(ve, 1)])
                    junk_mark(("gn", 0), ve)
                    B.op("vector", lambda e: e.drain())
                for h in range(NH):
                    junk_guard("vector", ("gn", 0), ve)
                    B.op("vector", lambda e, _h=h: e.tensor_scalar(
                        junk_v[:, 0:32], ysq[:, _h * 32:(_h + 1) * 32], 1.0, None,
                        op0=OP.mult, op1=OP.add, accum_out=gns[:, 8 + _h:9 + _h]), [(ve, 1)])
                    junk_mark(("gn", 0), ve)
                    B.op("vector", lambda e: e.drain())
                B.op("vector", lambda e: e.drain(), [(ve, 1)])
                gsum_ve = ve.n
                B.wait("tensor", ve, gsum_ve)
                B.wait("tensor", se, ysq_done[(2 * NT - 1, NH - 1)])
                B.op("tensor", lambda e: e.matmul(
                    P[0:1, 3072:3088], ones_sb[:, 0:1], gns[:, 0:16],
                    start=True, stop=True), [(pe, 1)])
                B.wait("vector", pe, pe.n)
                B.op("vector", lambda e: e.tensor_copy(gns[0:1, 16:32], P[0:1, 3072:3088]),
                     [(ve, 1)])
                B.wait("tensor", ve, ve.n)
                B.op("tensor", lambda e: e.matmul(
                    P[:, 3584:3600], ones_sb[0:1, 0:128], gns[0:1, 16:32],
                    start=True, stop=True), [(pe, 1)])
                pb = P[:, 3584:3600]
                B.wait("vector", pe, pe.n)
                B.op("vector", lambda e: e.tensor_copy(gns[:, 0:16], pb[:]))
                B.op("vector", lambda e: e.drain())
                B.op("vector", lambda e: e.scalar_tensor_tensor(
                    gns[:, 48:56], gns[:, 0:8], 1.0 / (GN_N * GN_N), gns[:, 0:8],
                    op0=OP.mult, op1=OP.mult))
                B.op("vector", lambda e: e.tensor_scalar(
                    gns[:, 16:24], gns[:, 0:8], 1.0 / GN_N, None, op0=OP.mult))
                B.op("vector", lambda e: e.drain())
                B.op("vector", lambda e: e.scalar_tensor_tensor(
                    gns[:, 24:32], gns[:, 8:16], 1.0 / GN_N, gns[:, 48:56],
                    op0=OP.mult, op1=OP.subtract), [(ve, 1)])
                B.wait("scalar", ve, ve.n)
                B.op("scalar", lambda e: e.activation(
                    gns[:, 32:40], gns[:, 24:32], AF.Sqrt, bias=np.float32(EPS)), [(se, 1)])
                B.wait("vector", se, se.n)
                B.op("vector", lambda e: e.reciprocal(gns[:, 40:48], gns[:, 32:40]))
                B.op("vector", lambda e: e.drain())
                B.op("vector", lambda e: e.tensor_tensor(
                    gns[:, 48:56], gns[:, 40:48], gnw_sb[:], op=OP.mult))
                B.op("vector", lambda e: e.scalar_tensor_tensor(
                    gns[:, 56:64], gns[:, 16:24], -1.0, gns[:, 40:48],
                    op0=OP.mult, op1=OP.mult))
                B.op("vector", lambda e: e.drain())
                B.op("vector", lambda e: e.scalar_tensor_tensor(
                    gns[:, 16:24], gns[:, 56:64], 1.0, gnw_sb[:],
                    op0=OP.mult, op1=OP.mult))
                B.op("vector", lambda e: e.drain())
                B.op("vector", lambda e: e.tensor_tensor(
                    gns[:, 24:32], gns[:, 16:24], gnb_sb[:], op=OP.add), [(ve, 1)])
                gn_ve = ve.n

                po = [P[:, 0:1024], P[:, 1024:2048]]
                store_done = {}
                wo_done = {}
                resv = {}
                for t in range(NT):
                    p = t % 2
                    ld_x = ld_xp[p]
                    st_s = st_sp[p]
                    xin_p = xin[:, p * 2 * D:(p + 1) * 2 * D]
                    if t >= 2:
                        B.wait("sync", st_s, store_done[p])
                    for _c in range(2):
                        B.op("sync", lambda e, _t=t, _x=xin_p, _cc=_c: e.dma_start(
                            out=_x[:, _cc * D:(_cc + 1) * D], in_=tile_half(x1s, _t, _cc)), [(ld_x, 16)])
                    x_ld = ld_x.n
                    B.wait("scalar", ve, gn_ve)
                    if t >= 1:
                        B.wait("scalar", pe, wo_done[t - 1])
                    for h in range(NH):
                        B.op("scalar", lambda e, _t=t, _h=h: e.activation(
                            ynb[:, _h * T:(_h + 1) * T],
                            ysb[:, _h * L + _t * T:_h * L + (_t + 1) * T],
                            AF.Identity, scale=gns[:, 48 + _h:49 + _h],
                            bias=gns[:, 24 + _h:25 + _h]), [(se, 1)])
                    yn_se = se.n
                    B.wait("tensor", se, yn_se)
                    if t >= 2:
                        B.wait("tensor", ve, resv[t - 2])
                    for c in range(2):
                        if c == 1:
                            B.wait("tensor", ve, ve.n)
                        for eh in range(NH):
                            B.op("tensor", lambda e, _p=p, _c=c, _e=eh: e.matmul(
                                po[_p][:, 0:512], ynb[:, _e * T + _c * 128:_e * T + (_c + 1) * 128],
                                wo_sb[:, _e * D + 0:_e * D + 512],
                                start=(_e == 0), stop=(_e == 7)))
                            B.op("tensor", lambda e, _p=p, _c=c, _e=eh: e.matmul(
                                po[_p][:, 512:1024], ynb[:, _e * T + _c * 128:_e * T + (_c + 1) * 128],
                                wo_sb[:, _e * D + 512:_e * D + 1024],
                                start=(_e == 0), stop=(_e == 7)),
                                [(pe, 1)] if eh == 7 else ())
                        wope = pe.n
                        B.wait("vector", pe, wope)
                        B.wait("vector", ld_x, x_ld)
                        B.op("vector", lambda e, _c=c, _x=xin_p, _p=p: e.scalar_tensor_tensor(
                            _x[:, _c * D:(_c + 1) * D], po[_p], 1.0, _x[:, _c * D:(_c + 1) * D],
                            op0=OP.mult, op1=OP.add), [(ve, 1)])
                    wo_done[t] = pe.n
                    resv[t] = ve.n
                    res_ve = ve.n
                    B.wait("sync", ve, res_ve)
                    for _c in range(2):
                        B.op("sync", lambda e, _t=t, _x=xin_p, _cc=_c: e.dma_start(
                            out=tile_half(x2s, _t, _cc), in_=_x[:, _cc * D:(_cc + 1) * D]), [(st_s, 16)])
                    store_done[p] = st_s.n
            barrier()
            build_lnpass(x2s, xn2, "2")
            barrier()

        if n_phases >= 4:
            w3scope = ExitStack()
            wg3 = w3scope.enter_context(nc.sbuf_tensor("wg3_sb", [128, 8 * HF], BF16))
            wu3 = w3scope.enter_context(nc.sbuf_tensor("wu3_sb", [128, 8 * HF], BF16))
            wd3 = w3scope.enter_context(nc.sbuf_tensor("wd3_sb", [128, 24 * D], BF16))
            wst3 = w3scope.enter_context(nc.sbuf_tensor("wstage3", [128, HF], I32))
            w3_done = ffn_weight_prologue(wg3, wu3, wd3, wst3)
            build_ffn(xn2, x2s, out_hbm, None, wg3, wu3, wd3, w3_done, final=True, tag="3")
            w3scope.close()

        B.wait("sync", st_sp[0], st_sp[0].n)
        B.wait("sync", st_sp[1], st_sp[1].n)
        build_nc.last_sched = B.snapshot()
        B.flush_block()

    return nc


_NC_CACHE = {}


def _get_nc(n_phases=4):
    if n_phases not in _NC_CACHE:
        _NC_CACHE[n_phases] = build_nc(n_phases)
    return _NC_CACHE[n_phases]


def _host_constants():
    gam = _gammas()
    ident = np.eye(128, dtype=BF)
    j = np.arange(128)[:, None]
    i = np.arange(128)[None, :]
    maskc = np.zeros((128, NH * 128), dtype=np.float64)
    for h in range(NH):
        maskc[:, h * 128:(h + 1) * 128] = np.where(i >= j, gam[h] ** (-128.0), 0.0)
    qdec = np.zeros((128, NH * T), dtype=np.float64)
    ii = np.arange(T) % 128
    for h in range(NH):
        qdec[:, h * T:(h + 1) * T] = (gam[h] ** (ii + 1.0))[None, :]
    kdec = np.zeros((128, D), dtype=np.float64)
    jj = np.arange(128)
    for h in range(NH):
        kdec[:, h * 128:(h + 1) * 128] = (gam[h] ** (127.0 - jj))[:, None]
    return ident, maskc.astype(BF), qdec.astype(BF), kdec.astype(BF)


def kernel(L0_attn_x, L0_attn_Wq, L0_attn_Wk, L0_attn_Wv, L0_attn_Wo,
           L0_attn_gn_weight, L0_attn_gn_bias,
           L0_ffn_w_gate, L0_ffn_w_up, L0_ffn_w_down, _n_phases=4):
    x = np.ascontiguousarray(np.asarray(L0_attn_x, dtype=np.float32))
    wq = np.ascontiguousarray(np.asarray(L0_attn_Wq, dtype=np.float32).T)
    wk = np.ascontiguousarray(np.asarray(L0_attn_Wk, dtype=np.float32).T)
    wv = np.ascontiguousarray(np.asarray(L0_attn_Wv, dtype=np.float32).T)
    wo = np.ascontiguousarray(np.asarray(L0_attn_Wo, dtype=np.float32).T)
    wg = np.ascontiguousarray(np.asarray(L0_ffn_w_gate, dtype=np.int32).T)
    wu = np.ascontiguousarray(np.asarray(L0_ffn_w_up, dtype=np.int32).T)
    wd = np.ascontiguousarray(np.asarray(L0_ffn_w_down, dtype=np.int32).T)
    gnw = np.ascontiguousarray(np.asarray(L0_attn_gn_weight, dtype=np.float32).reshape(NH, 128).T)
    gnb = np.ascontiguousarray(np.asarray(L0_attn_gn_bias, dtype=np.float32).reshape(NH, 128).T)
    ident, maskc, qdec, kdec = _host_constants()

    nc = _get_nc(_n_phases)
    shared = {"wqT": wq, "wkT": wk, "wvT": wv, "woT": wo,
              "wgT": wg, "wuT": wu, "wdT": wd,
              "gnw": gnw, "gnb": gnb, "ident": ident,
              "maskc": maskc, "qdec": qdec, "kdec": kdec}
    in_maps = [dict(shared, x=np.ascontiguousarray(x[b])) for b in range(8)]
    res = run_bass_kernel_spmd(nc, in_maps, core_ids=list(range(8)))
    out = np.stack([res.results[b]["out"] for b in range(8)], axis=0)
    return out


# version 04
# speedup vs baseline: 1.3687x; 1.3441x over previous
import numpy as np
import math
from contextlib import ExitStack
ExitSt = ExitStack
import ml_dtypes
import concourse.bass as bass
import concourse.mybir as mybir
from concourse.bass_utils import run_bass_kernel_spmd

F32 = mybir.dt.float32
BF16 = mybir.dt.bfloat16
I32 = mybir.dt.int32
OP = mybir.AluOpType
AF = mybir.ActivationFunctionType
BF = ml_dtypes.bfloat16

L = 4096
D = 1024
HF = 3072
NH = 8
DH = 128
T = 256
NT = L // T
FFN_TILES = NT
ATTN_TILES = NT
EPS = 1e-5
GN_N = float(L * DH)

ENGINES = ("sync", "tensor", "vector", "scalar", "gpsimd")


def _gammas():
    return 1.0 - np.exp(np.linspace(math.log(1.0 / 32.0), math.log(1.0 / 512.0), NH))


class Sem:
    __slots__ = ("h", "n", "name")


class Sched:
    def __init__(self, nc, stack):
        self.nc = nc
        self.stack = stack
        self.q = {e: [] for e in ENGINES}

    def sem(self, name):
        s = Sem()
        s.h = self.stack.enter_context(self.nc.semaphore(name))
        s.n = 0
        s.name = name
        return s

    def wait(self, eng, sem, val=None):
        v = sem.n if val is None else val
        if v <= 0:
            return
        f = lambda e, _h=sem.h, _v=v: e.wait_ge(_h, _v)
        f.meta = ("wait", sem.name, v)
        self.q[eng].append(f)

    def op(self, eng, fn, incs=()):
        incs = tuple(incs)

        def run(e, _fn=fn, _incs=incs):
            ins = _fn(e)
            for s, k in _incs:
                ins.then_inc(s.h, k)

        run.meta = ("op", tuple((s.name, k) for s, k in incs), getattr(fn, "__qualname__", ""))
        self.q[eng].append(run)
        for s, k in incs:
            s.n += k

    def snapshot(self):
        return {e: list(self.q[e]) for e in ENGINES}

    def flush_block(self):
        with self.nc.Block() as blk:
            for eng in ENGINES:
                ops = self.q[eng]

                def body(e, _ops=ops):
                    for f in _ops:
                        f(e)

                getattr(blk, eng)(body)
        self.q = {e: [] for e in ENGINES}


def build_nc(n_phases=4):
    nc = bass.Bass("TRN2", target_bir_lowering=False)

    x_in = nc.dram_tensor("x", [L, D], F32, kind="ExternalInput").ap()
    wqT = nc.dram_tensor("wqT", [D, D], F32, kind="ExternalInput").ap()
    wkT = nc.dram_tensor("wkT", [D, D], F32, kind="ExternalInput").ap()
    wvT = nc.dram_tensor("wvT", [D, D], F32, kind="ExternalInput").ap()
    woT = nc.dram_tensor("woT", [D, D], F32, kind="ExternalInput").ap()
    wgT = nc.dram_tensor("wgT", [D, HF], I32, kind="ExternalInput").ap()
    wuT = nc.dram_tensor("wuT", [D, HF], I32, kind="ExternalInput").ap()
    wdT = nc.dram_tensor("wdT", [HF, D], I32, kind="ExternalInput").ap()
    gnw_in = nc.dram_tensor("gnw", [128, NH], F32, kind="ExternalInput").ap()
    gnb_in = nc.dram_tensor("gnb", [128, NH], F32, kind="ExternalInput").ap()
    ident_in = nc.dram_tensor("ident", [128, 128], BF16, kind="ExternalInput").ap()
    mask_in = nc.dram_tensor("maskc", [128, NH * 128], BF16, kind="ExternalInput").ap()
    qdec_in = nc.dram_tensor("qdec", [128, NH * T], BF16, kind="ExternalInput").ap()
    kdec_in = nc.dram_tensor("kdec", [128, D], BF16, kind="ExternalInput").ap()
    out_hbm = nc.dram_tensor("out", [L, D], F32, kind="ExternalOutput").ap()

    xn0 = nc.dram_tensor("xn0_scr", [L, D], BF16, kind="Internal").ap()
    xn1 = nc.dram_tensor("xn1_scr", [L, D], BF16, kind="Internal").ap()
    xn2 = nc.dram_tensor("xn2_scr", [L, D], BF16, kind="Internal").ap()
    x1s = nc.dram_tensor("x1_scr", [L, D], F32, kind="Internal").ap()
    x2s = nc.dram_tensor("x2_scr", [L, D], F32, kind="Internal").ap()

    gam = _gammas()
    gam128 = [float(g) ** 128 for g in gam]

    def tile_half(ap, t, c):
        return ap[t * T + c * 128:t * T + (c + 1) * 128, :]

    with ExitStack() as ES:
        B = Sched(nc, ES)
        ld_xp = [B.sem("ld_x0"), B.sem("ld_x1")]
        ld_tp = [B.sem("ld_t0"), B.sem("ld_t1")]
        ld_c = B.sem("ld_c")
        ld_w = B.sem("ld_w")
        st_sp = [B.sem("st_s0"), B.sem("st_s1")]
        pe = B.sem("pe")
        ve = B.sem("ve")
        se = B.sem("se")
        ge = B.sem("ge")
        bar = B.sem("bar")

        junk = ES.enter_context(nc.sbuf_tensor("junkbuf", [128, 4 * D], BF16))
        junk_v = junk[:, 0:D]
        junk_v1 = junk[:, D:2 * D]
        junk_s = junk[:, 2 * D:3 * D]
        junk_s1 = junk[:, 3 * D:4 * D]
        junk_last = {}

        def junk_guard(eng, key, sem):
            if key in junk_last:
                s_, v_ = junk_last[key]
                B.wait(eng, s_, v_)

        def junk_mark(key, sem):
            junk_last[key] = (sem, sem.n)
        ident_sb = ES.enter_context(nc.sbuf_tensor("ident_sb", [128, 128], BF16))
        ones_sb = ES.enter_context(nc.sbuf_tensor("ones_sb", [128, 128], F32))
        psum = ES.enter_context(nc.psum_tensor("psum_all", [128, 4096], F32))
        P = psum.ap()
        P_bf = P.bitcast(BF16)

        nbar = [0]

        def barrier():
            for s_ in (ld_xp[0], ld_xp[1], ld_tp[0], ld_tp[1], ld_c, st_sp[0], st_sp[1]):
                B.wait("sync", s_, s_.n)
            B.wait("gpsimd", ld_w, ld_w.n)
            for eng in ENGINES:
                if eng == "sync":
                    B.op(eng, lambda e: e.nop(), [(bar, 1)])
                else:
                    B.op(eng, lambda e: e.drain(), [(bar, 1)])
            nbar[0] += 1
            for eng in ENGINES:
                B.wait(eng, bar, 5 * nbar[0])

        B.op("sync", lambda e: e.dma_start(out=ident_sb[:], in_=ident_in), [(ld_c, 16)])
        B.op("vector", lambda e: e.memset(ones_sb[:], 1.0), [(ve, 1)])

        def ln_finalize_apply(stats, x_tile, xn_tile, sumsq_ready_se):
            B.wait("vector", se, sumsq_ready_se)
            B.op("vector", lambda e, _s=stats: e.drain())
            B.op("vector", lambda e, _s=stats: e.scalar_tensor_tensor(
                _s[:, 4:6], _s[:, 0:2], 1.0 / (D * D), _s[:, 0:2],
                op0=OP.mult, op1=OP.mult))
            B.op("vector", lambda e, _s=stats: e.tensor_scalar(
                _s[:, 6:8], _s[:, 0:2], 1.0 / D, None, op0=OP.mult))
            B.op("vector", lambda e: e.drain())
            B.op("vector", lambda e, _s=stats: e.scalar_tensor_tensor(
                _s[:, 8:10], _s[:, 2:4], 1.0 / D, _s[:, 4:6],
                op0=OP.mult, op1=OP.subtract), [(ve, 1)])
            B.wait("scalar", ve, ve.n)
            B.op("scalar", lambda e, _s=stats: e.activation(
                _s[:, 10:12], _s[:, 8:10], AF.Sqrt, bias=np.float32(EPS)), [(se, 1)])
            B.wait("vector", se, se.n)
            B.op("vector", lambda e, _s=stats: e.reciprocal(_s[:, 12:14], _s[:, 10:12]))
            B.op("vector", lambda e: e.drain())
            B.op("vector", lambda e, _s=stats: e.scalar_tensor_tensor(
                _s[:, 14:16], _s[:, 6:8], -1.0, _s[:, 12:14],
                op0=OP.mult, op1=OP.mult), [(ve, 1)])
            B.wait("scalar", ve, ve.n)
            for c in range(2):
                B.op("scalar", lambda e, _s=stats, _x=x_tile, _o=xn_tile, _c=c: e.activation(
                    _o[:, _c * D:(_c + 1) * D], _x[:, _c * D:(_c + 1) * D],
                    AF.Identity, bias=_s[:, 14 + _c:15 + _c],
                    scale=_s[:, 12 + _c:13 + _c]), [(se, 1)])
            return se.n

        def ffn_weight_prologue(wg_sb, wu_sb, wd_sb, wstage):
            for wdram, wsb, sc in ((wgT, wg_sb, 1.0 / 127.0), (wuT, wu_sb, 1.0 / 127.0)):
                for d in range(8):
                    B.wait("gpsimd", ge, ge.n)
                    B.op("gpsimd", lambda e, _w=wdram, _d=d, _st=wstage: e.dma_start(
                        out=_st[:], in_=_w[_d * 128:(_d + 1) * 128, :]), [(ld_w, 16)])
                    B.wait("vector", ld_w, ld_w.n)
                    B.op("vector", lambda e, _sb=wsb, _d=d, _st=wstage, _sc=sc: e.tensor_scalar(
                        _sb[:, _d * HF:(_d + 1) * HF], _st[:], _sc, None,
                        op0=OP.mult), [(ge, 1)])
            for h in range(24):
                B.wait("gpsimd", ge, ge.n)
                B.op("gpsimd", lambda e, _h=h, _st=wstage: e.dma_start(
                    out=_st[:, 0:D], in_=wdT[_h * 128:(_h + 1) * 128, :]), [(ld_w, 16)])
                B.wait("vector", ld_w, ld_w.n)
                B.op("vector", lambda e, _h=h, _st=wstage: e.tensor_scalar(
                    wd_sb[:, _h * D:(_h + 1) * D], _st[:, 0:D], 0.5 / 127.0, None,
                    op0=OP.mult), [(ge, 1)])
            return ge.n

        w1scope = ExitStack()
        w1_done = 0
        if n_phases >= 2:
            wg1 = w1scope.enter_context(nc.sbuf_tensor("wg_sb", [128, 8 * HF], BF16))
            wu1 = w1scope.enter_context(nc.sbuf_tensor("wu_sb", [128, 8 * HF], BF16))
            wd1 = w1scope.enter_context(nc.sbuf_tensor("wd_sb", [128, 24 * D], BF16))
            wst1 = w1scope.enter_context(nc.sbuf_tensor("wstage", [128, HF], I32))
            w1_done = ffn_weight_prologue(wg1, wu1, wd1, wst1)

        def build_lnpass(src_dram, xn_dram, tag):
            with nc.sbuf_tensor("pp_xin" + tag, [128, 2 * 2 * D], F32) as xin, \
                 nc.sbuf_tensor("pp_xn" + tag, [128, 2 * 2 * D], BF16) as xno, \
                 nc.sbuf_tensor("pp_stats" + tag, [128, 2 * 16], F32) as statsb:
                store_done = {}
                for t in range(NT):
                    p = t % 2
                    ld_x = ld_xp[p]
                    st_s = st_sp[p]
                    xin_p = xin[:, p * 2 * D:(p + 1) * 2 * D]
                    xno_p = xno[:, p * 2 * D:(p + 1) * 2 * D]
                    stats = statsb[:, p * 16:(p + 1) * 16]
                    if t >= 2:
                        B.wait("sync", st_s, store_done[p])
                        B.wait("scalar", st_s, store_done[p])
                    for _c in range(2):
                        B.op("sync", lambda e, _t=t, _x=xin_p, _cc=_c: e.dma_start(
                            out=_x[:, _cc * D:(_cc + 1) * D], in_=tile_half(src_dram, _t, _cc)), [(ld_x, 16)])
                    x_ld = ld_x.n
                    B.wait("vector", ld_x, x_ld)
                    for c in range(2):
                        jv = junk_v if c == 0 else junk_v1
                        junk_guard("vector", ("v", c), ve)
                        B.op("vector", lambda e, _x=xin_p, _s=stats, _c=c, _j=jv: e.tensor_scalar(
                            _j[:], _x[:, _c * D:(_c + 1) * D], 1.0, None, op0=OP.mult,
                            op1=OP.add, accum_out=_s[:, 0 + _c:1 + _c]), [(ve, 1)])
                        junk_mark(("v", c), ve)
                        B.op("vector", lambda e: e.drain())
                    B.wait("scalar", ld_x, x_ld)
                    for c in range(2):
                        js = junk_s if c == 0 else junk_s1
                        junk_guard("scalar", ("s", c), se)
                        B.op("scalar", lambda e, _x=xin_p, _s=stats, _c=c, _j=js: e.activation(
                            _j[:], _x[:, _c * D:(_c + 1) * D], AF.Square,
                            accum_out=_s[:, 2 + _c:3 + _c]), [(se, 1)])
                        junk_mark(("s", c), se)
                    apply_se = ln_finalize_apply(stats, xin_p, xno_p, se.n)
                    B.wait("sync", se, apply_se)
                    for _c in range(2):
                        B.op("sync", lambda e, _t=t, _x=xno_p, _cc=_c: e.dma_start(
                            out=tile_half(xn_dram, _t, _cc), in_=_x[:, _cc * D:(_cc + 1) * D]), [(st_s, 16)])
                    store_done[p] = st_s.n

        if n_phases >= 1:
            build_lnpass(x_in, xn0, "0")
            barrier()

        def build_ffn(xn_src, x_src, x_dst, xn_dst, wg_sb, wu_sb, wd_sb,
                      weights_ready_ge, final, tag=""):
            with nc.sbuf_tensor("f_xnf" + tag, [128, 2 * 8 * T], BF16) as xnf, \
                 nc.sbuf_tensor("f_xin" + tag, [128, 2 * 2 * D], F32) as xin, \
                 nc.sbuf_tensor("f_hid" + tag, [128, 24 * T], BF16) as hid, \
                 nc.sbuf_tensor("f_gsb" + tag, [128, 2 * T], F32) as gsb:
                pg = [P[:, 0:T], P[:, 3072:3072 + T]]
                pu = [P[:, 512:512 + T], P[:, 3584:3584 + T]]
                pd = [P[:, 1024:2048], P[:, 2048:3072]]
                silu_done = {}
                hid_done = {}
                res_done = [0]
                store_done = {}
                xnf_used = {}
                B.wait("tensor", ge, weights_ready_ge)
                B.wait("sync", ge, weights_ready_ge)
                for t in range(FFN_TILES):
                    p = t % 2
                    ld_x = ld_xp[p]
                    ld_t = ld_tp[p]
                    st_out = st_sp[p]
                    xin_p = xin[:, p * 2 * D:(p + 1) * 2 * D]
                    xnf_p = [xnf[:, (p * 8 + d) * T:(p * 8 + d + 1) * T] for d in range(8)]
                    if t >= 2:
                        B.wait("sync", st_out, store_done[p])
                        B.wait("sync", pe, xnf_used[p])
                    for d in range(8):
                        B.op("sync", lambda e, _t=t, _d=d, _o=xnf_p[d]: e.dma_start_transpose(
                            out=_o, in_=xn_src[_t * T:(_t + 1) * T, _d * 128:(_d + 1) * 128]),
                            [(ld_t, 16)])
                    xnf_ld = ld_t.n
                    for _c in range(2):
                        B.op("sync", lambda e, _t=t, _x=xin_p, _cc=_c: e.dma_start(
                            out=_x[:, _cc * D:(_cc + 1) * D], in_=tile_half(x_src, _t, _cc)), [(ld_x, 16)])
                    x_ld = ld_x.n

                    B.wait("tensor", ld_t, xnf_ld)
                    if t >= 1:
                        B.wait("tensor", ve, res_done[0])

                    def down_mms(ha, hp):
                        B.wait("tensor", ve, hid_done[ha])
                        for c in range(2):
                            for half in range(2):
                                last = (hp == 23 and c == 1 and half == 1)
                                B.op("tensor", lambda e, _h=hp, _c=c, _f=half: e.matmul(
                                    pd[_c][:, _f * 512:(_f + 1) * 512],
                                    hid[:, _h * T + _c * 128:_h * T + (_c + 1) * 128],
                                    wd_sb[:, _h * D + _f * 512:_h * D + (_f + 1) * 512],
                                    start=(_h == 0), stop=(_h == 23)),
                                    [(pe, 1)] if last else ())

                    for h in range(24):
                        ha = t * 24 + h
                        bk = h % 2
                        if ha >= 2:
                            B.wait("tensor", se, silu_done[ha - 2])
                        for d in range(8):
                            B.op("tensor", lambda e, _h=h, _d=d, _b=bk, _xn=xnf_p[d]: e.matmul(
                                pg[_b], wg_sb[:, _d * HF + _h * 128:_d * HF + (_h + 1) * 128],
                                _xn, start=(_d == 0), stop=(_d == 7)),
                                [(pe, 1)] if d == 7 else ())
                        gate_pe = pe.n
                        if ha >= 2:
                            B.wait("tensor", ve, hid_done[ha - 2])
                        for d in range(8):
                            B.op("tensor", lambda e, _h=h, _d=d, _b=bk, _xn=xnf_p[d]: e.matmul(
                                pu[_b], wu_sb[:, _d * HF + _h * 128:_d * HF + (_h + 1) * 128],
                                _xn, start=(_d == 0), stop=(_d == 7)),
                                [(pe, 1)] if d == 7 else ())
                        up_pe = pe.n
                        if h >= 1:
                            down_mms(ha - 1, h - 1)
                        B.wait("scalar", pe, gate_pe)
                        if ha >= 2:
                            B.wait("scalar", ve, hid_done[ha - 2])
                        B.op("scalar", lambda e, _b=bk: e.activation(
                            gsb[:, _b * T:(_b + 1) * T], pg[_b], AF.Silu), [(se, 1)])
                        silu_done[ha] = se.n
                        B.wait("vector", se, silu_done[ha])
                        B.wait("vector", pe, up_pe)
                        B.op("vector", lambda e, _h=h, _b=bk: e.tensor_tensor(
                            hid[:, _h * T:(_h + 1) * T], pu[_b], gsb[:, _b * T:(_b + 1) * T],
                            op=OP.mult), [(ve, 1)])
                        hid_done[ha] = ve.n
                    down_mms(t * 24 + 23, 23)
                    down_pe = pe.n
                    xnf_used[p] = pe.n
                    B.wait("vector", pe, down_pe)
                    B.wait("vector", ld_x, x_ld)
                    for c in range(2):
                        B.op("vector", lambda e, _c=c, _x=xin_p: e.scalar_tensor_tensor(
                            _x[:, _c * D:(_c + 1) * D], pd[_c], 1.0, _x[:, _c * D:(_c + 1) * D],
                            op0=OP.mult, op1=OP.add),
                            [(ve, 1)])
                    res_done[0] = ve.n
                    res_ve = ve.n
                    B.wait("sync", ve, res_ve)
                    for _c in range(2):
                        B.op("sync", lambda e, _t=t, _x=xin_p, _cc=_c: e.dma_start(
                            out=tile_half(x_dst, _t, _cc), in_=_x[:, _cc * D:(_cc + 1) * D]), [(st_out, 16)])
                    store_done[p] = st_out.n

        if n_phases >= 2:
            build_ffn(xn0, x_in, x1s, None, wg1, wu1, wd1, w1_done,
                      final=(n_phases == 2))
            barrier()
            if n_phases >= 3:
                build_lnpass(x1s, xn1, "1")
                barrier()
        w1scope.close()

        if n_phases >= 3:
            with ExitSt() as AS:
                al = AS.enter_context
                wq_sb = al(nc.sbuf_tensor("a_wq", [128, 8 * D], BF16))
                wk_sb = al(nc.sbuf_tensor("a_wk", [128, 8 * D], BF16))
                wv_sb = al(nc.sbuf_tensor("a_wv", [128, 8 * D], BF16))
                wo_sb = al(nc.sbuf_tensor("a_wo", [128, 8 * D], BF16))
                awst = al(nc.sbuf_tensor("a_wstage", [128, D], F32))
                ysb = al(nc.sbuf_tensor("a_ys", [128, NH * L], BF16))
                xnf = al(nc.sbuf_tensor("a_xnf", [128, 2 * 8 * T], BF16))
                qt = al(nc.sbuf_tensor("a_qt", [128, NH * T], BF16))
                ktm = al(nc.sbuf_tensor("a_ktm", [128, 2 * D], BF16))
                kfm = al(nc.sbuf_tensor("a_kfm", [128, 2 * D], BF16))
                vtm = al(nc.sbuf_tensor("a_vtm", [128, 2 * D], BF16))
                adb = al(nc.sbuf_tensor("a_ad", [128, 2 * 128], BF16))
                s0f = al(nc.sbuf_tensor("a_s0f", [128, NH * DH], F32))
                s0b = al(nc.sbuf_tensor("a_s0b", [128, NH * DH], BF16))
                mask_sb = al(nc.sbuf_tensor("a_mask", [128, NH * 128], BF16))
                qdec_sb = al(nc.sbuf_tensor("a_qdec", [128, NH * T], BF16))
                kdec_sb = al(nc.sbuf_tensor("a_kdec", [128, D], BF16))
                ysum = al(nc.sbuf_tensor("a_ysum", [128, NH * 32], F32))
                ysq = al(nc.sbuf_tensor("a_ysq", [128, NH * 32], F32))
                gnw_sb = al(nc.sbuf_tensor("a_gnw", [128, NH], F32))
                gnb_sb = al(nc.sbuf_tensor("a_gnb", [128, NH], F32))
                gns = al(nc.sbuf_tensor("a_gns", [128, 64], F32))
                ynb = al(nc.sbuf_tensor("a_yn", [128, NH * T], BF16))
                xin = al(nc.sbuf_tensor("a_xin", [128, 2 * 2 * D], F32))

                B.op("sync", lambda e: e.dma_start(out=mask_sb[:], in_=mask_in), [(ld_c, 16)])
                B.op("sync", lambda e: e.dma_start(out=qdec_sb[:], in_=qdec_in), [(ld_c, 16)])
                B.op("sync", lambda e: e.dma_start(out=kdec_sb[:], in_=kdec_in), [(ld_c, 16)])
                B.op("sync", lambda e: e.dma_start(out=gnw_sb[:], in_=gnw_in), [(ld_c, 16)])
                B.op("sync", lambda e: e.dma_start(out=gnb_sb[:], in_=gnb_in), [(ld_c, 16)])
                const_ld = ld_c.n
                for wdram, wsb in ((wqT, wq_sb), (wkT, wk_sb), (wvT, wv_sb), (woT, wo_sb)):
                    for d in range(8):
                        B.wait("gpsimd", ge, ge.n)
                        B.op("gpsimd", lambda e, _w=wdram, _d=d: e.dma_start(
                            out=awst[:], in_=_w[_d * 128:(_d + 1) * 128, :]), [(ld_w, 16)])
                        B.wait("vector", ld_w, ld_w.n)
                        B.op("vector", lambda e, _sb=wsb, _d=d: e.tensor_copy(
                            _sb[:, _d * D:(_d + 1) * D], awst[:]), [(ge, 1)])
                aw_done = ge.n
                B.op("vector", lambda e: e.memset(s0f[:], 0.0))
                B.op("vector", lambda e: e.memset(s0b[:], 0.0), [(ve, 1)])
                s0_init = ve.n
                B.wait("vector", ld_c, const_ld)

                paq = [P[:, 0:T], P[:, 512:512 + T]]
                pa = [P[:, 0:128], P[:, 512:640]]
                pk = P[:, 1024:2048]
                py = [P[:, 1024:1152], P[:, 1536:1664]]
                pv = P[:, 2048:3072]
                pkv = [P[:, 2048:2176], P[:, 2560:2688]]
                ptr = [P_bf[:, 6144:7168], P_bf[:, 7168:8192]]

                B.wait("tensor", ge, aw_done)
                B.wait("tensor", ve, s0_init)

                q_evac = {}
                ktm_evac = {}
                kfm_evac = {}
                v_evac = {}
                ad_evac = {}
                supd = {}
                s0cast = {}
                a_done = {}
                y_done = {}
                kv_done = {}
                ysq_done = {}
                xnf_used = {}

                for t in range(NT):
                    p = t % 2
                    ld_t = ld_tp[p]
                    xnf_p = [xnf[:, (p * 8 + d) * T:(p * 8 + d + 1) * T] for d in range(8)]
                    if t >= 2:
                        B.wait("sync", pe, xnf_used[p])
                    for d in range(8):
                        B.op("sync", lambda e, _t=t, _d=d, _o=xnf_p[d]: e.dma_start_transpose(
                            out=_o, in_=xn1[_t * T:(_t + 1) * T, _d * 128:(_d + 1) * 128]),
                            [(ld_t, 16)])
                    xnf_ld = ld_t.n
                    B.wait("tensor", ld_t, xnf_ld)
                    for h in range(NH):
                        bk = h % 2
                        if t >= 1:
                            B.wait("tensor", ve, ad_evac[(2 * t - 1, 6 + bk)])
                        if h >= 2:
                            B.wait("tensor", ve, q_evac[(t, h - 2)])
                        for d in range(8):
                            B.op("tensor", lambda e, _h=h, _d=d, _b=bk, _xn=xnf_p[d]: e.matmul(
                                paq[_b], wq_sb[:, _d * D + _h * 128:_d * D + (_h + 1) * 128],
                                _xn, start=(_d == 0), stop=(_d == 7)),
                                [(pe, 1)] if d == 7 else ())
                        qpe = pe.n
                        B.wait("vector", pe, qpe)
                        if t >= 1:
                            B.wait("vector", pe, y_done[(2 * t - 1, h)])
                        B.op("vector", lambda e, _h=h, _b=bk: e.tensor_tensor(
                            qt[:, _h * T:(_h + 1) * T], paq[_b], qdec_sb[:, _h * T:(_h + 1) * T],
                            op=OP.mult), [(ve, 1)])
                        q_evac[(t, h)] = ve.n
                    for c in range(2):
                        gc = 2 * t + c
                        if c == 0:
                            if gc >= 1:
                                B.wait("tensor", se, ysq_done[(gc - 1, 7)])
                        else:
                            B.wait("tensor", ve, ktm_evac[gc - 1])
                        for half in range(2):
                            for d in range(8):
                                B.op("tensor", lambda e, _c=c, _f=half, _d=d, _xn=xnf_p[d]: e.matmul(
                                    pk[:, _f * 512:(_f + 1) * 512],
                                    _xn[:, _c * 128:(_c + 1) * 128],
                                    wk_sb[:, _d * D + _f * 512:_d * D + (_f + 1) * 512],
                                    start=(_d == 0), stop=(_d == 7)),
                                    [(pe, 1)] if (d == 7 and half == 1) else ())
                        kpe = pe.n
                        B.wait("vector", pe, kpe)
                        if gc >= 2:
                            B.wait("vector", pe, kv_done[(gc - 2, 7)])
                        B.op("vector", lambda e, _c=c: e.tensor_tensor(
                            ktm[:, _c * D:(_c + 1) * D], pk[:], kdec_sb[:], op=OP.mult),
                            [(ve, 1)])
                        ktm_evac[gc] = ve.n
                        if c == 0:
                            if gc >= 1:
                                B.wait("tensor", ve, supd[(gc - 1, 7)])
                        else:
                            B.wait("tensor", se, v_evac[gc - 1])
                        for half in range(2):
                            for d in range(8):
                                B.op("tensor", lambda e, _c=c, _f=half, _d=d, _xn=xnf_p[d]: e.matmul(
                                    pv[:, _f * 512:(_f + 1) * 512],
                                    _xn[:, _c * 128:(_c + 1) * 128],
                                    wv_sb[:, _d * D + _f * 512:_d * D + (_f + 1) * 512],
                                    start=(_d == 0), stop=(_d == 7)),
                                    [(pe, 1)] if (d == 7 and half == 1) else ())
                        vpe = pe.n
                        B.wait("scalar", pe, vpe)
                        if gc >= 2:
                            B.wait("scalar", pe, kv_done[(gc - 2, 7)])
                        B.op("scalar", lambda e, _c=c: e.activation(
                            vtm[:, _c * D:(_c + 1) * D], pv[:], AF.Copy), [(se, 1)])
                        v_evac[gc] = se.n
                        B.wait("tensor", ve, ktm_evac[gc])
                        if gc >= 2:
                            B.wait("tensor", ve, kfm_evac[gc - 2])
                        for h in range(NH):
                            B.op("tensor", lambda e, _c=c, _h=h: e.transpose(
                                ptr[_c][:, _h * 128:(_h + 1) * 128],
                                ktm[:, _c * D + _h * 128:_c * D + (_h + 1) * 128],
                                ident_sb[:]), [(pe, 1)] if h == NH - 1 else ())
                        trpe = pe.n
                        B.wait("vector", pe, trpe)
                        if gc >= 2:
                            B.wait("vector", pe, a_done[(gc - 2, 7)])
                        B.op("vector", lambda e, _c=c: e.tensor_copy(
                            kfm[:, _c * D:(_c + 1) * D], ptr[_c][:]), [(ve, 1)])
                        kfm_evac[gc] = ve.n
                    xnf_used[p] = pe.n
                    for c in range(2):
                        gc = 2 * t + c
                        B.wait("tensor", ve, kfm_evac[gc])
                        B.wait("tensor", se, v_evac[gc])
                        if c == 0:
                            B.wait("tensor", ve, q_evac[(t, NH - 1)])
                        for h in range(NH):
                            bk = h % 2
                            if h >= 2:
                                B.wait("tensor", ve, ad_evac[(gc, h - 2)])
                            elif gc >= 1 and c == 1:
                                B.wait("tensor", ve, ad_evac[(gc - 1, 6 + bk)])
                            B.op("tensor", lambda e, _c=c, _h=h, _b=bk: e.matmul(
                                pa[_b], kfm[:, _c * D + _h * 128:_c * D + (_h + 1) * 128],
                                qt[:, _h * T + _c * 128:_h * T + (_c + 1) * 128],
                                start=True, stop=True), [(pe, 1)])
                            a_done[(gc, h)] = pe.n
                            B.wait("vector", pe, a_done[(gc, h)])
                            prev_ad = (gc, h - 2) if h >= 2 else ((gc - 1, 6 + bk) if gc >= 1 else None)
                            if prev_ad is not None and prev_ad in y_done:
                                B.wait("vector", pe, y_done[prev_ad])
                            B.op("vector", lambda e, _h=h, _b=bk: e.tensor_tensor(
                                adb[:, _b * 128:(_b + 1) * 128], pa[_b],
                                mask_sb[:, _h * 128:(_h + 1) * 128], op=OP.mult), [(ve, 1)])
                            ad_evac[(gc, h)] = ve.n
                            B.wait("tensor", ve, ad_evac[(gc, h)])
                            if h >= 2:
                                B.wait("tensor", se, ysq_done[(gc, h - 2)])
                            elif c == 1:
                                B.wait("tensor", se, ysq_done[(gc - 1, 6 + bk)])
                            else:
                                B.wait("tensor", ve, ktm_evac[gc + 1])
                            if gc >= 1:
                                B.wait("tensor", se, s0cast[(gc - 1, h)])
                            B.op("tensor", lambda e, _c=c, _h=h, _b=bk: e.matmul(
                                py[_b], vtm[:, _c * D + _h * 128:_c * D + (_h + 1) * 128],
                                adb[:, _b * 128:(_b + 1) * 128], start=True, stop=False))
                            B.op("tensor", lambda e, _c=c, _h=h, _b=bk: e.matmul(
                                py[_b], s0b[:, _h * DH:(_h + 1) * DH],
                                qt[:, _h * T + _c * 128:_h * T + (_c + 1) * 128],
                                start=False, stop=True), [(pe, 1)])
                            y_done[(gc, h)] = pe.n
                            if h >= 2:
                                B.wait("tensor", ve, supd[(gc, h - 2)])
                            elif c == 1:
                                B.wait("tensor", ve, supd[(gc - 1, 6 + bk)])
                            else:
                                B.wait("tensor", se, v_evac[gc + 1])
                            B.op("tensor", lambda e, _c=c, _h=h, _b=bk: e.matmul(
                                pkv[_b], ktm[:, _c * D + _h * 128:_c * D + (_h + 1) * 128],
                                vtm[:, _c * D + _h * 128:_c * D + (_h + 1) * 128],
                                start=True, stop=True), [(pe, 1)])
                            kv_done[(gc, h)] = pe.n
                            B.wait("scalar", pe, y_done[(gc, h)])
                            B.op("scalar", lambda e, _gc=gc, _h=h, _b=bk: e.activation(
                                ysb[:, (_h * 32 + _gc) * 128:(_h * 32 + _gc + 1) * 128],
                                py[_b], AF.Copy,
                                accum_out=ysum[:, _h * 32 + _gc:_h * 32 + _gc + 1]), [(se, 1)])
                            junk_guard("scalar", ("ysq", 0), se)
                            B.op("scalar", lambda e, _gc=gc, _h=h, _b=bk: e.activation(
                                junk_s[:, 0:128], py[_b], AF.Square,
                                accum_out=ysq[:, _h * 32 + _gc:_h * 32 + _gc + 1]), [(se, 1)])
                            junk_mark(("ysq", 0), se)
                            ysq_done[(gc, h)] = se.n
                            B.wait("vector", pe, kv_done[(gc, h)])
                            if gc >= 1:
                                B.wait("vector", se, s0cast[(gc - 1, h)])
                            B.op("vector", lambda e, _h=h, _b=bk, _g=gam128[h]: e.scalar_tensor_tensor(
                                s0f[:, _h * DH:(_h + 1) * DH], s0f[:, _h * DH:(_h + 1) * DH],
                                _g, pkv[_b], op0=OP.mult, op1=OP.add), [(ve, 1)])
                            supd[(gc, h)] = ve.n
                            B.wait("scalar", ve, supd[(gc, h)])
                            B.op("scalar", lambda e, _h=h: e.activation(
                                s0b[:, _h * DH:(_h + 1) * DH], s0f[:, _h * DH:(_h + 1) * DH],
                                AF.Copy), [(se, 1)])
                            s0cast[(gc, h)] = se.n

                B.wait("vector", se, ysq_done[(2 * NT - 1, NH - 1)])
                for h in range(NH):
                    junk_guard("vector", ("gn", 0), ve)
                    B.op("vector", lambda e, _h=h: e.tensor_scalar(
                        junk_v[:, 0:32], ysum[:, _h * 32:(_h + 1) * 32], 1.0, None,
                        op0=OP.mult, op1=OP.add, accum_out=gns[:, _h:_h + 1]), [(ve, 1)])
                    junk_mark(("gn", 0), ve)
                    B.op("vector", lambda e: e.drain())
                for h in range(NH):
                    junk_guard("vector", ("gn", 0), ve)
                    B.op("vector", lambda e, _h=h: e.tensor_scalar(
                        junk_v[:, 0:32], ysq[:, _h * 32:(_h + 1) * 32], 1.0, None,
                        op0=OP.mult, op1=OP.add, accum_out=gns[:, 8 + _h:9 + _h]), [(ve, 1)])
                    junk_mark(("gn", 0), ve)
                    B.op("vector", lambda e: e.drain())
                B.op("vector", lambda e: e.drain(), [(ve, 1)])
                gsum_ve = ve.n
                B.wait("tensor", ve, gsum_ve)
                B.wait("tensor", se, ysq_done[(2 * NT - 1, NH - 1)])
                B.op("tensor", lambda e: e.matmul(
                    P[0:1, 3072:3088], ones_sb[:, 0:1], gns[:, 0:16],
                    start=True, stop=True), [(pe, 1)])
                B.wait("vector", pe, pe.n)
                B.op("vector", lambda e: e.tensor_copy(gns[0:1, 16:32], P[0:1, 3072:3088]),
                     [(ve, 1)])
                B.wait("tensor", ve, ve.n)
                B.op("tensor", lambda e: e.matmul(
                    P[:, 3584:3600], ones_sb[0:1, 0:128], gns[0:1, 16:32],
                    start=True, stop=True), [(pe, 1)])
                pb = P[:, 3584:3600]
                B.wait("vector", pe, pe.n)
                B.op("vector", lambda e: e.tensor_copy(gns[:, 0:16], pb[:]))
                B.op("vector", lambda e: e.drain())
                B.op("vector", lambda e: e.scalar_tensor_tensor(
                    gns[:, 48:56], gns[:, 0:8], 1.0 / (GN_N * GN_N), gns[:, 0:8],
                    op0=OP.mult, op1=OP.mult))
                B.op("vector", lambda e: e.tensor_scalar(
                    gns[:, 16:24], gns[:, 0:8], 1.0 / GN_N, None, op0=OP.mult))
                B.op("vector", lambda e: e.drain())
                B.op("vector", lambda e: e.scalar_tensor_tensor(
                    gns[:, 24:32], gns[:, 8:16], 1.0 / GN_N, gns[:, 48:56],
                    op0=OP.mult, op1=OP.subtract), [(ve, 1)])
                B.wait("scalar", ve, ve.n)
                B.op("scalar", lambda e: e.activation(
                    gns[:, 32:40], gns[:, 24:32], AF.Sqrt, bias=np.float32(EPS)), [(se, 1)])
                B.wait("vector", se, se.n)
                B.op("vector", lambda e: e.reciprocal(gns[:, 40:48], gns[:, 32:40]))
                B.op("vector", lambda e: e.drain())
                B.op("vector", lambda e: e.tensor_tensor(
                    gns[:, 48:56], gns[:, 40:48], gnw_sb[:], op=OP.mult))
                B.op("vector", lambda e: e.scalar_tensor_tensor(
                    gns[:, 56:64], gns[:, 16:24], -1.0, gns[:, 40:48],
                    op0=OP.mult, op1=OP.mult))
                B.op("vector", lambda e: e.drain())
                B.op("vector", lambda e: e.scalar_tensor_tensor(
                    gns[:, 16:24], gns[:, 56:64], 1.0, gnw_sb[:],
                    op0=OP.mult, op1=OP.mult))
                B.op("vector", lambda e: e.drain())
                B.op("vector", lambda e: e.tensor_tensor(
                    gns[:, 24:32], gns[:, 16:24], gnb_sb[:], op=OP.add), [(ve, 1)])
                gn_ve = ve.n

                po = [P[:, 0:1024], P[:, 1024:2048]]
                store_done = {}
                wo_done = {}
                resv = {}
                for t in range(NT):
                    p = t % 2
                    ld_x = ld_xp[p]
                    st_s = st_sp[p]
                    xin_p = xin[:, p * 2 * D:(p + 1) * 2 * D]
                    if t >= 2:
                        B.wait("sync", st_s, store_done[p])
                    for _c in range(2):
                        B.op("sync", lambda e, _t=t, _x=xin_p, _cc=_c: e.dma_start(
                            out=_x[:, _cc * D:(_cc + 1) * D], in_=tile_half(x1s, _t, _cc)), [(ld_x, 16)])
                    x_ld = ld_x.n
                    B.wait("scalar", ve, gn_ve)
                    if t >= 1:
                        B.wait("scalar", pe, wo_done[t - 1])
                    for h in range(NH):
                        B.op("scalar", lambda e, _t=t, _h=h: e.activation(
                            ynb[:, _h * T:(_h + 1) * T],
                            ysb[:, _h * L + _t * T:_h * L + (_t + 1) * T],
                            AF.Identity, scale=gns[:, 48 + _h:49 + _h],
                            bias=gns[:, 24 + _h:25 + _h]), [(se, 1)])
                    yn_se = se.n
                    B.wait("tensor", se, yn_se)
                    if t >= 2:
                        B.wait("tensor", ve, resv[t - 2])
                    for c in range(2):
                        if c == 1:
                            B.wait("tensor", ve, ve.n)
                        for eh in range(NH):
                            B.op("tensor", lambda e, _p=p, _c=c, _e=eh: e.matmul(
                                po[_p][:, 0:512], ynb[:, _e * T + _c * 128:_e * T + (_c + 1) * 128],
                                wo_sb[:, _e * D + 0:_e * D + 512],
                                start=(_e == 0), stop=(_e == 7)))
                            B.op("tensor", lambda e, _p=p, _c=c, _e=eh: e.matmul(
                                po[_p][:, 512:1024], ynb[:, _e * T + _c * 128:_e * T + (_c + 1) * 128],
                                wo_sb[:, _e * D + 512:_e * D + 1024],
                                start=(_e == 0), stop=(_e == 7)),
                                [(pe, 1)] if eh == 7 else ())
                        wope = pe.n
                        B.wait("vector", pe, wope)
                        B.wait("vector", ld_x, x_ld)
                        B.op("vector", lambda e, _c=c, _x=xin_p, _p=p: e.scalar_tensor_tensor(
                            _x[:, _c * D:(_c + 1) * D], po[_p], 1.0, _x[:, _c * D:(_c + 1) * D],
                            op0=OP.mult, op1=OP.add), [(ve, 1)])
                    wo_done[t] = pe.n
                    resv[t] = ve.n
                    res_ve = ve.n
                    B.wait("sync", ve, res_ve)
                    for _c in range(2):
                        B.op("sync", lambda e, _t=t, _x=xin_p, _cc=_c: e.dma_start(
                            out=tile_half(x2s, _t, _cc), in_=_x[:, _cc * D:(_cc + 1) * D]), [(st_s, 16)])
                    store_done[p] = st_s.n
            barrier()
            build_lnpass(x2s, xn2, "2")
            barrier()

        if n_phases >= 4:
            w3scope = ExitStack()
            wg3 = w3scope.enter_context(nc.sbuf_tensor("wg3_sb", [128, 8 * HF], BF16))
            wu3 = w3scope.enter_context(nc.sbuf_tensor("wu3_sb", [128, 8 * HF], BF16))
            wd3 = w3scope.enter_context(nc.sbuf_tensor("wd3_sb", [128, 24 * D], BF16))
            wst3 = w3scope.enter_context(nc.sbuf_tensor("wstage3", [128, HF], I32))
            w3_done = ffn_weight_prologue(wg3, wu3, wd3, wst3)
            build_ffn(xn2, x2s, out_hbm, None, wg3, wu3, wd3, w3_done, final=True, tag="3")
            w3scope.close()

        B.wait("sync", st_sp[0], st_sp[0].n)
        B.wait("sync", st_sp[1], st_sp[1].n)
        build_nc.last_sched = B.snapshot()
        B.flush_block()

    return nc


_NC_CACHE = {}


def _get_nc(n_phases=4):
    if n_phases not in _NC_CACHE:
        _NC_CACHE[n_phases] = build_nc(n_phases)
    return _NC_CACHE[n_phases]


def _host_constants():
    gam = _gammas()
    ident = np.eye(128, dtype=BF)
    j = np.arange(128)[:, None]
    i = np.arange(128)[None, :]
    maskc = np.zeros((128, NH * 128), dtype=np.float64)
    for h in range(NH):
        maskc[:, h * 128:(h + 1) * 128] = np.where(i >= j, gam[h] ** (-128.0), 0.0)
    qdec = np.zeros((128, NH * T), dtype=np.float64)
    ii = np.arange(T) % 128
    for h in range(NH):
        qdec[:, h * T:(h + 1) * T] = (gam[h] ** (ii + 1.0))[None, :]
    kdec = np.zeros((128, D), dtype=np.float64)
    jj = np.arange(128)
    for h in range(NH):
        kdec[:, h * 128:(h + 1) * 128] = (gam[h] ** (127.0 - jj))[:, None]
    return ident, maskc.astype(BF), qdec.astype(BF), kdec.astype(BF)


def kernel(L0_attn_x, L0_attn_Wq, L0_attn_Wk, L0_attn_Wv, L0_attn_Wo,
           L0_attn_gn_weight, L0_attn_gn_bias,
           L0_ffn_w_gate, L0_ffn_w_up, L0_ffn_w_down, _n_phases=4):
    x = np.ascontiguousarray(np.asarray(L0_attn_x, dtype=np.float32))
    wq = np.ascontiguousarray(np.asarray(L0_attn_Wq, dtype=np.float32).T)
    wk = np.ascontiguousarray(np.asarray(L0_attn_Wk, dtype=np.float32).T)
    wv = np.ascontiguousarray(np.asarray(L0_attn_Wv, dtype=np.float32).T)
    wo = np.ascontiguousarray(np.asarray(L0_attn_Wo, dtype=np.float32).T)
    wg = np.ascontiguousarray(np.asarray(L0_ffn_w_gate, dtype=np.int32).T)
    wu = np.ascontiguousarray(np.asarray(L0_ffn_w_up, dtype=np.int32).T)
    wd = np.ascontiguousarray(np.asarray(L0_ffn_w_down, dtype=np.int32).T)
    gnw = np.ascontiguousarray(np.asarray(L0_attn_gn_weight, dtype=np.float32).reshape(NH, 128).T)
    gnb = np.ascontiguousarray(np.asarray(L0_attn_gn_bias, dtype=np.float32).reshape(NH, 128).T)
    ident, maskc, qdec, kdec = _host_constants()

    nc = _get_nc(_n_phases)
    shared = {"wqT": wq, "wkT": wk, "wvT": wv, "woT": wo,
              "wgT": wg, "wuT": wu, "wdT": wd,
              "gnw": gnw, "gnb": gnb, "ident": ident,
              "maskc": maskc, "qdec": qdec, "kdec": kdec}
    in_maps = [dict(shared, x=np.ascontiguousarray(x[b])) for b in range(8)]
    res = run_bass_kernel_spmd(nc, in_maps, core_ids=list(range(8)))
    out = np.stack([res.results[b]["out"] for b in range(8)], axis=0)
    return out


# version 05
# speedup vs baseline: 1.3756x; 1.0051x over previous
import numpy as np
import math
from contextlib import ExitStack
ExitSt = ExitStack
import ml_dtypes
import concourse.bass as bass
import concourse.mybir as mybir
from concourse.bass_utils import run_bass_kernel_spmd

F32 = mybir.dt.float32
BF16 = mybir.dt.bfloat16
I32 = mybir.dt.int32
OP = mybir.AluOpType
AF = mybir.ActivationFunctionType
BF = ml_dtypes.bfloat16

L = 4096
D = 1024
HF = 3072
NH = 8
DH = 128
T = 256
NT = L // T
FFN_TILES = NT
ATTN_TILES = NT
EPS = 1e-5
GN_N = float(L * DH)

ENGINES = ("sync", "tensor", "vector", "scalar", "gpsimd")


def _gammas():
    return 1.0 - np.exp(np.linspace(math.log(1.0 / 32.0), math.log(1.0 / 512.0), NH))


class Sem:
    __slots__ = ("h", "n", "name")


class Sched:
    def __init__(self, nc, stack):
        self.nc = nc
        self.stack = stack
        self.q = {e: [] for e in ENGINES}

    def sem(self, name):
        s = Sem()
        s.h = self.stack.enter_context(self.nc.semaphore(name))
        s.n = 0
        s.name = name
        return s

    def wait(self, eng, sem, val=None):
        v = sem.n if val is None else val
        if v <= 0:
            return
        f = lambda e, _h=sem.h, _v=v: e.wait_ge(_h, _v)
        f.meta = ("wait", sem.name, v)
        self.q[eng].append(f)

    def op(self, eng, fn, incs=()):
        incs = tuple(incs)

        def run(e, _fn=fn, _incs=incs):
            ins = _fn(e)
            for s, k in _incs:
                ins.then_inc(s.h, k)

        run.meta = ("op", tuple((s.name, k) for s, k in incs), getattr(fn, "__qualname__", ""))
        self.q[eng].append(run)
        for s, k in incs:
            s.n += k

    def snapshot(self):
        return {e: list(self.q[e]) for e in ENGINES}

    def flush_block(self):
        with self.nc.Block() as blk:
            for eng in ENGINES:
                ops = self.q[eng]

                def body(e, _ops=ops):
                    for f in _ops:
                        f(e)

                getattr(blk, eng)(body)
        self.q = {e: [] for e in ENGINES}


def build_nc(n_phases=4):
    nc = bass.Bass("TRN2", target_bir_lowering=False)

    x_in = nc.dram_tensor("x", [L, D], F32, kind="ExternalInput").ap()
    wqT = nc.dram_tensor("wqT", [D, D], F32, kind="ExternalInput").ap()
    wkT = nc.dram_tensor("wkT", [D, D], F32, kind="ExternalInput").ap()
    wvT = nc.dram_tensor("wvT", [D, D], F32, kind="ExternalInput").ap()
    woT = nc.dram_tensor("woT", [D, D], F32, kind="ExternalInput").ap()
    wgT = nc.dram_tensor("wgT", [D, HF], I32, kind="ExternalInput").ap()
    wuT = nc.dram_tensor("wuT", [D, HF], I32, kind="ExternalInput").ap()
    wdT = nc.dram_tensor("wdT", [HF, D], I32, kind="ExternalInput").ap()
    gnw_in = nc.dram_tensor("gnw", [128, NH], F32, kind="ExternalInput").ap()
    gnb_in = nc.dram_tensor("gnb", [128, NH], F32, kind="ExternalInput").ap()
    ident_in = nc.dram_tensor("ident", [128, 128], BF16, kind="ExternalInput").ap()
    mask_in = nc.dram_tensor("maskc", [128, NH * 128], BF16, kind="ExternalInput").ap()
    qdec_in = nc.dram_tensor("qdec", [128, NH * T], BF16, kind="ExternalInput").ap()
    kdec_in = nc.dram_tensor("kdec", [128, D], BF16, kind="ExternalInput").ap()
    out_hbm = nc.dram_tensor("out", [L, D], F32, kind="ExternalOutput").ap()

    xn0 = nc.dram_tensor("xn0_scr", [L, D], BF16, kind="Internal").ap()
    xn1 = nc.dram_tensor("xn1_scr", [L, D], BF16, kind="Internal").ap()
    xn2 = nc.dram_tensor("xn2_scr", [L, D], BF16, kind="Internal").ap()
    x1s = nc.dram_tensor("x1_scr", [L, D], F32, kind="Internal").ap()
    x2s = nc.dram_tensor("x2_scr", [L, D], F32, kind="Internal").ap()

    gam = _gammas()
    gam128 = [float(g) ** 128 for g in gam]

    def tile_half(ap, t, c):
        return ap[t * T + c * 128:t * T + (c + 1) * 128, :]

    with ExitStack() as ES:
        B = Sched(nc, ES)
        ld_xp = [B.sem("ld_x0"), B.sem("ld_x1")]
        ld_tp = [B.sem("ld_t0"), B.sem("ld_t1")]
        ld_c = B.sem("ld_c")
        ld_w = B.sem("ld_w")
        st_sp = [B.sem("st_s0"), B.sem("st_s1")]
        pe = B.sem("pe")
        ve = B.sem("ve")
        se = B.sem("se")
        ge = B.sem("ge")
        bar = B.sem("bar")

        junk = ES.enter_context(nc.sbuf_tensor("junkbuf", [128, 4 * D], BF16))
        junk_v = junk[:, 0:D]
        junk_v1 = junk[:, D:2 * D]
        junk_s = junk[:, 2 * D:3 * D]
        junk_s1 = junk[:, 3 * D:4 * D]
        junk_last = {}

        def junk_guard(eng, key, sem):
            if key in junk_last:
                s_, v_ = junk_last[key]
                B.wait(eng, s_, v_)

        def junk_mark(key, sem):
            junk_last[key] = (sem, sem.n)
        ident_sb = ES.enter_context(nc.sbuf_tensor("ident_sb", [128, 128], BF16))
        ones_sb = ES.enter_context(nc.sbuf_tensor("ones_sb", [128, 128], F32))
        psum = ES.enter_context(nc.psum_tensor("psum_all", [128, 4096], F32))
        P = psum.ap()
        P_bf = P.bitcast(BF16)

        nbar = [0]

        def barrier():
            for s_ in (ld_xp[0], ld_xp[1], ld_tp[0], ld_tp[1], ld_c, st_sp[0], st_sp[1]):
                B.wait("sync", s_, s_.n)
            B.wait("gpsimd", ld_w, ld_w.n)
            for eng in ENGINES:
                if eng == "sync":
                    B.op(eng, lambda e: e.nop(), [(bar, 1)])
                else:
                    B.op(eng, lambda e: e.drain(), [(bar, 1)])
            nbar[0] += 1
            for eng in ENGINES:
                B.wait(eng, bar, 5 * nbar[0])

        B.op("sync", lambda e: e.dma_start(out=ident_sb[:], in_=ident_in), [(ld_c, 16)])
        B.op("vector", lambda e: e.memset(ones_sb[:], 1.0), [(ve, 1)])

        def ln_finalize_apply(stats, x_tile, xn_tile, sumsq_ready_se):
            B.wait("vector", se, sumsq_ready_se)
            B.op("vector", lambda e, _s=stats: e.drain())
            B.op("vector", lambda e, _s=stats: e.scalar_tensor_tensor(
                _s[:, 4:6], _s[:, 0:2], 1.0 / (D * D), _s[:, 0:2],
                op0=OP.mult, op1=OP.mult))
            B.op("vector", lambda e, _s=stats: e.tensor_scalar(
                _s[:, 6:8], _s[:, 0:2], 1.0 / D, None, op0=OP.mult))
            B.op("vector", lambda e: e.drain())
            B.op("vector", lambda e, _s=stats: e.scalar_tensor_tensor(
                _s[:, 8:10], _s[:, 2:4], 1.0 / D, _s[:, 4:6],
                op0=OP.mult, op1=OP.subtract), [(ve, 1)])
            B.wait("scalar", ve, ve.n)
            B.op("scalar", lambda e, _s=stats: e.activation(
                _s[:, 10:12], _s[:, 8:10], AF.Sqrt, bias=np.float32(EPS)), [(se, 1)])
            B.wait("vector", se, se.n)
            B.op("vector", lambda e, _s=stats: e.reciprocal(_s[:, 12:14], _s[:, 10:12]))
            B.op("vector", lambda e: e.drain())
            B.op("vector", lambda e, _s=stats: e.scalar_tensor_tensor(
                _s[:, 14:16], _s[:, 6:8], -1.0, _s[:, 12:14],
                op0=OP.mult, op1=OP.mult), [(ve, 1)])
            B.wait("scalar", ve, ve.n)
            for c in range(2):
                B.op("scalar", lambda e, _s=stats, _x=x_tile, _o=xn_tile, _c=c: e.activation(
                    _o[:, _c * D:(_c + 1) * D], _x[:, _c * D:(_c + 1) * D],
                    AF.Identity, bias=_s[:, 14 + _c:15 + _c],
                    scale=_s[:, 12 + _c:13 + _c]), [(se, 1)])
            return se.n

        def ffn_weight_prologue(wg_sb, wu_sb, wd_sb, wstage):
            for wdram, wsb, sc in ((wgT, wg_sb, 1.0 / 127.0), (wuT, wu_sb, 1.0 / 127.0)):
                for d in range(8):
                    B.wait("gpsimd", ge, ge.n)
                    B.op("gpsimd", lambda e, _w=wdram, _d=d, _st=wstage: e.dma_start(
                        out=_st[:], in_=_w[_d * 128:(_d + 1) * 128, :]), [(ld_w, 16)])
                    B.wait("vector", ld_w, ld_w.n)
                    B.op("vector", lambda e, _sb=wsb, _d=d, _st=wstage, _sc=sc: e.tensor_scalar(
                        _sb[:, _d * HF:(_d + 1) * HF], _st[:], _sc, None,
                        op0=OP.mult), [(ge, 1)])
            for h in range(24):
                B.wait("gpsimd", ge, ge.n)
                B.op("gpsimd", lambda e, _h=h, _st=wstage: e.dma_start(
                    out=_st[:, 0:D], in_=wdT[_h * 128:(_h + 1) * 128, :]), [(ld_w, 16)])
                B.wait("vector", ld_w, ld_w.n)
                B.op("vector", lambda e, _h=h, _st=wstage: e.tensor_scalar(
                    wd_sb[:, _h * D:(_h + 1) * D], _st[:, 0:D], 0.5 / 127.0, None,
                    op0=OP.mult), [(ge, 1)])
            return ge.n

        w1scope = ExitStack()
        w1_done = 0
        if n_phases >= 2:
            wg1 = w1scope.enter_context(nc.sbuf_tensor("wg_sb", [128, 8 * HF], BF16))
            wu1 = w1scope.enter_context(nc.sbuf_tensor("wu_sb", [128, 8 * HF], BF16))
            wd1 = w1scope.enter_context(nc.sbuf_tensor("wd_sb", [128, 24 * D], BF16))
            wst1 = w1scope.enter_context(nc.sbuf_tensor("wstage", [128, HF], I32))
            w1_done = ffn_weight_prologue(wg1, wu1, wd1, wst1)

        def build_lnpass(src_dram, xn_dram, tag):
            with nc.sbuf_tensor("pp_xin" + tag, [128, 2 * 2 * D], F32) as xin, \
                 nc.sbuf_tensor("pp_xn" + tag, [128, 2 * 2 * D], BF16) as xno, \
                 nc.sbuf_tensor("pp_stats" + tag, [128, 2 * 16], F32) as statsb:
                store_done = {}
                for t in range(NT):
                    p = t % 2
                    ld_x = ld_xp[p]
                    st_s = st_sp[p]
                    xin_p = xin[:, p * 2 * D:(p + 1) * 2 * D]
                    xno_p = xno[:, p * 2 * D:(p + 1) * 2 * D]
                    stats = statsb[:, p * 16:(p + 1) * 16]
                    if t >= 2:
                        B.wait("sync", st_s, store_done[p])
                        B.wait("scalar", st_s, store_done[p])
                    for _c in range(2):
                        B.op("sync", lambda e, _t=t, _x=xin_p, _cc=_c: e.dma_start(
                            out=_x[:, _cc * D:(_cc + 1) * D], in_=tile_half(src_dram, _t, _cc)), [(ld_x, 16)])
                    x_ld = ld_x.n
                    B.wait("vector", ld_x, x_ld)
                    for c in range(2):
                        jv = junk_v if c == 0 else junk_v1
                        junk_guard("vector", ("v", c), ve)
                        B.op("vector", lambda e, _x=xin_p, _s=stats, _c=c, _j=jv: e.tensor_scalar(
                            _j[:], _x[:, _c * D:(_c + 1) * D], 1.0, None, op0=OP.mult,
                            op1=OP.add, accum_out=_s[:, 0 + _c:1 + _c]), [(ve, 1)])
                        junk_mark(("v", c), ve)
                        B.op("vector", lambda e: e.drain())
                    B.wait("scalar", ld_x, x_ld)
                    for c in range(2):
                        js = junk_s if c == 0 else junk_s1
                        junk_guard("scalar", ("s", c), se)
                        B.op("scalar", lambda e, _x=xin_p, _s=stats, _c=c, _j=js: e.activation(
                            _j[:], _x[:, _c * D:(_c + 1) * D], AF.Square,
                            accum_out=_s[:, 2 + _c:3 + _c]), [(se, 1)])
                        junk_mark(("s", c), se)
                    apply_se = ln_finalize_apply(stats, xin_p, xno_p, se.n)
                    B.wait("sync", se, apply_se)
                    for _c in range(2):
                        B.op("sync", lambda e, _t=t, _x=xno_p, _cc=_c: e.dma_start(
                            out=tile_half(xn_dram, _t, _cc), in_=_x[:, _cc * D:(_cc + 1) * D]), [(st_s, 16)])
                    store_done[p] = st_s.n

        if n_phases >= 1:
            build_lnpass(x_in, xn0, "0")
            barrier()

        def build_ffn(xn_src, x_src, x_dst, xn_dst, wg_sb, wu_sb, wd_sb,
                      weights_ready_ge, final, tag=""):
            with nc.sbuf_tensor("f_xnf" + tag, [128, 2 * 8 * T], BF16) as xnf, \
                 nc.sbuf_tensor("f_xin" + tag, [128, 2 * 2 * D], F32) as xin, \
                 nc.sbuf_tensor("f_hid" + tag, [128, 24 * T], BF16) as hid, \
                 nc.sbuf_tensor("f_gsb" + tag, [128, 2 * T], F32) as gsb:
                pg = [P[:, 0:T], P[:, 3072:3072 + T]]
                pu = [P[:, 512:512 + T], P[:, 3584:3584 + T]]
                pd = [P[:, 1024:2048], P[:, 2048:3072]]
                silu_done = {}
                hid_done = {}
                res_done = [0]
                store_done = {}
                xnf_used = {}
                B.wait("tensor", ge, weights_ready_ge)
                B.wait("sync", ge, weights_ready_ge)
                for t in range(FFN_TILES):
                    p = t % 2
                    ld_x = ld_xp[p]
                    ld_t = ld_tp[p]
                    st_out = st_sp[p]
                    xin_p = xin[:, p * 2 * D:(p + 1) * 2 * D]
                    xnf_p = [xnf[:, (p * 8 + d) * T:(p * 8 + d + 1) * T] for d in range(8)]
                    if t >= 2:
                        B.wait("sync", st_out, store_done[p])
                        B.wait("sync", pe, xnf_used[p])
                        B.wait("scalar", pe, xnf_used[p])
                    for d in range(8):
                        B.op("sync" if d % 2 == 0 else "scalar",
                             lambda e, _t=t, _d=d, _o=xnf_p[d]: e.dma_start_transpose(
                            out=_o, in_=xn_src[_t * T:(_t + 1) * T, _d * 128:(_d + 1) * 128]),
                            [(ld_t, 16)])
                    xnf_ld = ld_t.n
                    for _c in range(2):
                        B.op("sync", lambda e, _t=t, _x=xin_p, _cc=_c: e.dma_start(
                            out=_x[:, _cc * D:(_cc + 1) * D], in_=tile_half(x_src, _t, _cc)), [(ld_x, 16)])
                    x_ld = ld_x.n

                    B.wait("tensor", ld_t, xnf_ld)
                    if t >= 1:
                        B.wait("tensor", ve, res_done[0])

                    def down_mms(ha, hp):
                        B.wait("tensor", ve, hid_done[ha])
                        for c in range(2):
                            for half in range(2):
                                last = (hp == 23 and c == 1 and half == 1)
                                B.op("tensor", lambda e, _h=hp, _c=c, _f=half: e.matmul(
                                    pd[_c][:, _f * 512:(_f + 1) * 512],
                                    hid[:, _h * T + _c * 128:_h * T + (_c + 1) * 128],
                                    wd_sb[:, _h * D + _f * 512:_h * D + (_f + 1) * 512],
                                    start=(_h == 0), stop=(_h == 23)),
                                    [(pe, 1)] if last else ())

                    for h in range(24):
                        ha = t * 24 + h
                        bk = h % 2
                        if ha >= 2:
                            B.wait("tensor", se, silu_done[ha - 2])
                        for d in range(8):
                            B.op("tensor", lambda e, _h=h, _d=d, _b=bk, _xn=xnf_p[d]: e.matmul(
                                pg[_b], wg_sb[:, _d * HF + _h * 128:_d * HF + (_h + 1) * 128],
                                _xn, start=(_d == 0), stop=(_d == 7)),
                                [(pe, 1)] if d == 7 else ())
                        gate_pe = pe.n
                        if ha >= 2:
                            B.wait("tensor", ve, hid_done[ha - 2])
                        for d in range(8):
                            B.op("tensor", lambda e, _h=h, _d=d, _b=bk, _xn=xnf_p[d]: e.matmul(
                                pu[_b], wu_sb[:, _d * HF + _h * 128:_d * HF + (_h + 1) * 128],
                                _xn, start=(_d == 0), stop=(_d == 7)),
                                [(pe, 1)] if d == 7 else ())
                        up_pe = pe.n
                        if h >= 1:
                            down_mms(ha - 1, h - 1)
                        B.wait("scalar", pe, gate_pe)
                        if ha >= 2:
                            B.wait("scalar", ve, hid_done[ha - 2])
                        B.op("scalar", lambda e, _b=bk: e.activation(
                            gsb[:, _b * T:(_b + 1) * T], pg[_b], AF.Silu), [(se, 1)])
                        silu_done[ha] = se.n
                        B.wait("vector", se, silu_done[ha])
                        B.wait("vector", pe, up_pe)
                        B.op("vector", lambda e, _h=h, _b=bk: e.tensor_tensor(
                            hid[:, _h * T:(_h + 1) * T], pu[_b], gsb[:, _b * T:(_b + 1) * T],
                            op=OP.mult), [(ve, 1)])
                        hid_done[ha] = ve.n
                    down_mms(t * 24 + 23, 23)
                    down_pe = pe.n
                    xnf_used[p] = pe.n
                    B.wait("vector", pe, down_pe)
                    B.wait("vector", ld_x, x_ld)
                    for c in range(2):
                        B.op("vector", lambda e, _c=c, _x=xin_p: e.scalar_tensor_tensor(
                            _x[:, _c * D:(_c + 1) * D], pd[_c], 1.0, _x[:, _c * D:(_c + 1) * D],
                            op0=OP.mult, op1=OP.add),
                            [(ve, 1)])
                    res_done[0] = ve.n
                    res_ve = ve.n
                    B.wait("sync", ve, res_ve)
                    for _c in range(2):
                        B.op("sync", lambda e, _t=t, _x=xin_p, _cc=_c: e.dma_start(
                            out=tile_half(x_dst, _t, _cc), in_=_x[:, _cc * D:(_cc + 1) * D]), [(st_out, 16)])
                    store_done[p] = st_out.n

        if n_phases >= 2:
            build_ffn(xn0, x_in, x1s, None, wg1, wu1, wd1, w1_done,
                      final=(n_phases == 2))
            barrier()
            if n_phases >= 3:
                build_lnpass(x1s, xn1, "1")
                barrier()
        w1scope.close()

        if n_phases >= 3:
            with ExitSt() as AS:
                al = AS.enter_context
                wq_sb = al(nc.sbuf_tensor("a_wq", [128, 8 * D], BF16))
                wk_sb = al(nc.sbuf_tensor("a_wk", [128, 8 * D], BF16))
                wv_sb = al(nc.sbuf_tensor("a_wv", [128, 8 * D], BF16))
                wo_sb = al(nc.sbuf_tensor("a_wo", [128, 8 * D], BF16))
                awst = al(nc.sbuf_tensor("a_wstage", [128, D], F32))
                ysb = al(nc.sbuf_tensor("a_ys", [128, NH * L], BF16))
                xnf = al(nc.sbuf_tensor("a_xnf", [128, 2 * 8 * T], BF16))
                qt = al(nc.sbuf_tensor("a_qt", [128, NH * T], BF16))
                ktm = al(nc.sbuf_tensor("a_ktm", [128, 2 * D], BF16))
                kfm = al(nc.sbuf_tensor("a_kfm", [128, 2 * D], BF16))
                vtm = al(nc.sbuf_tensor("a_vtm", [128, 2 * D], BF16))
                adb = al(nc.sbuf_tensor("a_ad", [128, 2 * 128], BF16))
                s0f = al(nc.sbuf_tensor("a_s0f", [128, NH * DH], F32))
                s0b = al(nc.sbuf_tensor("a_s0b", [128, NH * DH], BF16))
                mask_sb = al(nc.sbuf_tensor("a_mask", [128, NH * 128], BF16))
                qdec_sb = al(nc.sbuf_tensor("a_qdec", [128, NH * T], BF16))
                kdec_sb = al(nc.sbuf_tensor("a_kdec", [128, D], BF16))
                ysum = al(nc.sbuf_tensor("a_ysum", [128, NH * 32], F32))
                ysq = al(nc.sbuf_tensor("a_ysq", [128, NH * 32], F32))
                gnw_sb = al(nc.sbuf_tensor("a_gnw", [128, NH], F32))
                gnb_sb = al(nc.sbuf_tensor("a_gnb", [128, NH], F32))
                gns = al(nc.sbuf_tensor("a_gns", [128, 64], F32))
                ynb = al(nc.sbuf_tensor("a_yn", [128, NH * T], BF16))
                xin = al(nc.sbuf_tensor("a_xin", [128, 2 * 2 * D], F32))

                B.op("sync", lambda e: e.dma_start(out=mask_sb[:], in_=mask_in), [(ld_c, 16)])
                B.op("sync", lambda e: e.dma_start(out=qdec_sb[:], in_=qdec_in), [(ld_c, 16)])
                B.op("sync", lambda e: e.dma_start(out=kdec_sb[:], in_=kdec_in), [(ld_c, 16)])
                B.op("sync", lambda e: e.dma_start(out=gnw_sb[:], in_=gnw_in), [(ld_c, 16)])
                B.op("sync", lambda e: e.dma_start(out=gnb_sb[:], in_=gnb_in), [(ld_c, 16)])
                const_ld = ld_c.n
                for wdram, wsb in ((wqT, wq_sb), (wkT, wk_sb), (wvT, wv_sb), (woT, wo_sb)):
                    for d in range(8):
                        B.wait("gpsimd", ge, ge.n)
                        B.op("gpsimd", lambda e, _w=wdram, _d=d: e.dma_start(
                            out=awst[:], in_=_w[_d * 128:(_d + 1) * 128, :]), [(ld_w, 16)])
                        B.wait("vector", ld_w, ld_w.n)
                        B.op("vector", lambda e, _sb=wsb, _d=d: e.tensor_copy(
                            _sb[:, _d * D:(_d + 1) * D], awst[:]), [(ge, 1)])
                aw_done = ge.n
                B.op("vector", lambda e: e.memset(s0f[:], 0.0))
                B.op("vector", lambda e: e.memset(s0b[:], 0.0), [(ve, 1)])
                s0_init = ve.n
                B.wait("vector", ld_c, const_ld)

                paq = [P[:, 0:T], P[:, 512:512 + T]]
                pa = [P[:, 0:128], P[:, 512:640]]
                pk = P[:, 1024:2048]
                py = [P[:, 1024:1152], P[:, 1536:1664]]
                pv = P[:, 2048:3072]
                pkv = [P[:, 2048:2176], P[:, 2560:2688]]
                ptr = [P_bf[:, 6144:7168], P_bf[:, 7168:8192]]

                B.wait("tensor", ge, aw_done)
                B.wait("tensor", ve, s0_init)

                q_evac = {}
                ktm_evac = {}
                kfm_evac = {}
                v_evac = {}
                ad_evac = {}
                supd = {}
                s0cast = {}
                a_done = {}
                y_done = {}
                kv_done = {}
                ysq_done = {}
                xnf_used = {}

                for t in range(NT):
                    p = t % 2
                    ld_t = ld_tp[p]
                    xnf_p = [xnf[:, (p * 8 + d) * T:(p * 8 + d + 1) * T] for d in range(8)]
                    if t >= 2:
                        B.wait("sync", pe, xnf_used[p])
                        B.wait("scalar", pe, xnf_used[p])
                    for d in range(8):
                        B.op("sync" if d % 2 == 0 else "scalar",
                             lambda e, _t=t, _d=d, _o=xnf_p[d]: e.dma_start_transpose(
                            out=_o, in_=xn1[_t * T:(_t + 1) * T, _d * 128:(_d + 1) * 128]),
                            [(ld_t, 16)])
                    xnf_ld = ld_t.n
                    B.wait("tensor", ld_t, xnf_ld)
                    for h in range(NH):
                        bk = h % 2
                        if t >= 1:
                            B.wait("tensor", ve, ad_evac[(2 * t - 1, 6 + bk)])
                        if h >= 2:
                            B.wait("tensor", ve, q_evac[(t, h - 2)])
                        for d in range(8):
                            B.op("tensor", lambda e, _h=h, _d=d, _b=bk, _xn=xnf_p[d]: e.matmul(
                                paq[_b], wq_sb[:, _d * D + _h * 128:_d * D + (_h + 1) * 128],
                                _xn, start=(_d == 0), stop=(_d == 7)),
                                [(pe, 1)] if d == 7 else ())
                        qpe = pe.n
                        B.wait("vector", pe, qpe)
                        if t >= 1:
                            B.wait("vector", pe, y_done[(2 * t - 1, h)])
                        B.op("vector", lambda e, _h=h, _b=bk: e.tensor_tensor(
                            qt[:, _h * T:(_h + 1) * T], paq[_b], qdec_sb[:, _h * T:(_h + 1) * T],
                            op=OP.mult), [(ve, 1)])
                        q_evac[(t, h)] = ve.n
                    for c in range(2):
                        gc = 2 * t + c
                        if c == 0:
                            if gc >= 1:
                                B.wait("tensor", se, ysq_done[(gc - 1, 7)])
                        else:
                            B.wait("tensor", ve, ktm_evac[gc - 1])
                        for half in range(2):
                            for d in range(8):
                                B.op("tensor", lambda e, _c=c, _f=half, _d=d, _xn=xnf_p[d]: e.matmul(
                                    pk[:, _f * 512:(_f + 1) * 512],
                                    _xn[:, _c * 128:(_c + 1) * 128],
                                    wk_sb[:, _d * D + _f * 512:_d * D + (_f + 1) * 512],
                                    start=(_d == 0), stop=(_d == 7)),
                                    [(pe, 1)] if (d == 7 and half == 1) else ())
                        kpe = pe.n
                        B.wait("vector", pe, kpe)
                        if gc >= 2:
                            B.wait("vector", pe, kv_done[(gc - 2, 7)])
                        B.op("vector", lambda e, _c=c: e.tensor_tensor(
                            ktm[:, _c * D:(_c + 1) * D], pk[:], kdec_sb[:], op=OP.mult),
                            [(ve, 1)])
                        ktm_evac[gc] = ve.n
                        if c == 0:
                            if gc >= 1:
                                B.wait("tensor", ve, supd[(gc - 1, 7)])
                        else:
                            B.wait("tensor", se, v_evac[gc - 1])
                        for half in range(2):
                            for d in range(8):
                                B.op("tensor", lambda e, _c=c, _f=half, _d=d, _xn=xnf_p[d]: e.matmul(
                                    pv[:, _f * 512:(_f + 1) * 512],
                                    _xn[:, _c * 128:(_c + 1) * 128],
                                    wv_sb[:, _d * D + _f * 512:_d * D + (_f + 1) * 512],
                                    start=(_d == 0), stop=(_d == 7)),
                                    [(pe, 1)] if (d == 7 and half == 1) else ())
                        vpe = pe.n
                        B.wait("scalar", pe, vpe)
                        if gc >= 2:
                            B.wait("scalar", pe, kv_done[(gc - 2, 7)])
                        B.op("scalar", lambda e, _c=c: e.activation(
                            vtm[:, _c * D:(_c + 1) * D], pv[:], AF.Copy), [(se, 1)])
                        v_evac[gc] = se.n
                        B.wait("tensor", ve, ktm_evac[gc])
                        if gc >= 2:
                            B.wait("tensor", ve, kfm_evac[gc - 2])
                        for h in range(NH):
                            B.op("tensor", lambda e, _c=c, _h=h: e.transpose(
                                ptr[_c][:, _h * 128:(_h + 1) * 128],
                                ktm[:, _c * D + _h * 128:_c * D + (_h + 1) * 128],
                                ident_sb[:]), [(pe, 1)] if h == NH - 1 else ())
                        trpe = pe.n
                        B.wait("vector", pe, trpe)
                        if gc >= 2:
                            B.wait("vector", pe, a_done[(gc - 2, 7)])
                        B.op("vector", lambda e, _c=c: e.tensor_copy(
                            kfm[:, _c * D:(_c + 1) * D], ptr[_c][:]), [(ve, 1)])
                        kfm_evac[gc] = ve.n
                    xnf_used[p] = pe.n
                    for c in range(2):
                        gc = 2 * t + c
                        B.wait("tensor", ve, kfm_evac[gc])
                        B.wait("tensor", se, v_evac[gc])
                        if c == 0:
                            B.wait("tensor", ve, q_evac[(t, NH - 1)])
                        for h in range(NH):
                            bk = h % 2
                            if h >= 2:
                                B.wait("tensor", ve, ad_evac[(gc, h - 2)])
                            elif gc >= 1 and c == 1:
                                B.wait("tensor", ve, ad_evac[(gc - 1, 6 + bk)])
                            B.op("tensor", lambda e, _c=c, _h=h, _b=bk: e.matmul(
                                pa[_b], kfm[:, _c * D + _h * 128:_c * D + (_h + 1) * 128],
                                qt[:, _h * T + _c * 128:_h * T + (_c + 1) * 128],
                                start=True, stop=True), [(pe, 1)])
                            a_done[(gc, h)] = pe.n
                            B.wait("vector", pe, a_done[(gc, h)])
                            prev_ad = (gc, h - 2) if h >= 2 else ((gc - 1, 6 + bk) if gc >= 1 else None)
                            if prev_ad is not None and prev_ad in y_done:
                                B.wait("vector", pe, y_done[prev_ad])
                            B.op("vector", lambda e, _h=h, _b=bk: e.tensor_tensor(
                                adb[:, _b * 128:(_b + 1) * 128], pa[_b],
                                mask_sb[:, _h * 128:(_h + 1) * 128], op=OP.mult), [(ve, 1)])
                            ad_evac[(gc, h)] = ve.n
                            B.wait("tensor", ve, ad_evac[(gc, h)])
                            if h >= 2:
                                B.wait("tensor", se, ysq_done[(gc, h - 2)])
                            elif c == 1:
                                B.wait("tensor", se, ysq_done[(gc - 1, 6 + bk)])
                            else:
                                B.wait("tensor", ve, ktm_evac[gc + 1])
                            if gc >= 1:
                                B.wait("tensor", se, s0cast[(gc - 1, h)])
                            B.op("tensor", lambda e, _c=c, _h=h, _b=bk: e.matmul(
                                py[_b], vtm[:, _c * D + _h * 128:_c * D + (_h + 1) * 128],
                                adb[:, _b * 128:(_b + 1) * 128], start=True, stop=False))
                            B.op("tensor", lambda e, _c=c, _h=h, _b=bk: e.matmul(
                                py[_b], s0b[:, _h * DH:(_h + 1) * DH],
                                qt[:, _h * T + _c * 128:_h * T + (_c + 1) * 128],
                                start=False, stop=True), [(pe, 1)])
                            y_done[(gc, h)] = pe.n
                            if h >= 2:
                                B.wait("tensor", ve, supd[(gc, h - 2)])
                            elif c == 1:
                                B.wait("tensor", ve, supd[(gc - 1, 6 + bk)])
                            else:
                                B.wait("tensor", se, v_evac[gc + 1])
                            B.op("tensor", lambda e, _c=c, _h=h, _b=bk: e.matmul(
                                pkv[_b], ktm[:, _c * D + _h * 128:_c * D + (_h + 1) * 128],
                                vtm[:, _c * D + _h * 128:_c * D + (_h + 1) * 128],
                                start=True, stop=True), [(pe, 1)])
                            kv_done[(gc, h)] = pe.n
                            B.wait("scalar", pe, y_done[(gc, h)])
                            B.op("scalar", lambda e, _gc=gc, _h=h, _b=bk: e.activation(
                                ysb[:, (_h * 32 + _gc) * 128:(_h * 32 + _gc + 1) * 128],
                                py[_b], AF.Copy,
                                accum_out=ysum[:, _h * 32 + _gc:_h * 32 + _gc + 1]), [(se, 1)])
                            junk_guard("scalar", ("ysq", 0), se)
                            B.op("scalar", lambda e, _gc=gc, _h=h, _b=bk: e.activation(
                                junk_s[:, 0:128], py[_b], AF.Square,
                                accum_out=ysq[:, _h * 32 + _gc:_h * 32 + _gc + 1]), [(se, 1)])
                            junk_mark(("ysq", 0), se)
                            ysq_done[(gc, h)] = se.n
                            B.wait("vector", pe, kv_done[(gc, h)])
                            if gc >= 1:
                                B.wait("vector", se, s0cast[(gc - 1, h)])
                            B.op("vector", lambda e, _h=h, _b=bk, _g=gam128[h]: e.scalar_tensor_tensor(
                                s0f[:, _h * DH:(_h + 1) * DH], s0f[:, _h * DH:(_h + 1) * DH],
                                _g, pkv[_b], op0=OP.mult, op1=OP.add), [(ve, 1)])
                            supd[(gc, h)] = ve.n
                            B.wait("scalar", ve, supd[(gc, h)])
                            B.op("scalar", lambda e, _h=h: e.activation(
                                s0b[:, _h * DH:(_h + 1) * DH], s0f[:, _h * DH:(_h + 1) * DH],
                                AF.Copy), [(se, 1)])
                            s0cast[(gc, h)] = se.n

                B.wait("vector", se, ysq_done[(2 * NT - 1, NH - 1)])
                for h in range(NH):
                    junk_guard("vector", ("gn", 0), ve)
                    B.op("vector", lambda e, _h=h: e.tensor_scalar(
                        junk_v[:, 0:32], ysum[:, _h * 32:(_h + 1) * 32], 1.0, None,
                        op0=OP.mult, op1=OP.add, accum_out=gns[:, _h:_h + 1]), [(ve, 1)])
                    junk_mark(("gn", 0), ve)
                    B.op("vector", lambda e: e.drain())
                for h in range(NH):
                    junk_guard("vector", ("gn", 0), ve)
                    B.op("vector", lambda e, _h=h: e.tensor_scalar(
                        junk_v[:, 0:32], ysq[:, _h * 32:(_h + 1) * 32], 1.0, None,
                        op0=OP.mult, op1=OP.add, accum_out=gns[:, 8 + _h:9 + _h]), [(ve, 1)])
                    junk_mark(("gn", 0), ve)
                    B.op("vector", lambda e: e.drain())
                B.op("vector", lambda e: e.drain(), [(ve, 1)])
                gsum_ve = ve.n
                B.wait("tensor", ve, gsum_ve)
                B.wait("tensor", se, ysq_done[(2 * NT - 1, NH - 1)])
                B.op("tensor", lambda e: e.matmul(
                    P[0:1, 3072:3088], ones_sb[:, 0:1], gns[:, 0:16],
                    start=True, stop=True), [(pe, 1)])
                B.wait("vector", pe, pe.n)
                B.op("vector", lambda e: e.tensor_copy(gns[0:1, 16:32], P[0:1, 3072:3088]),
                     [(ve, 1)])
                B.wait("tensor", ve, ve.n)
                B.op("tensor", lambda e: e.matmul(
                    P[:, 3584:3600], ones_sb[0:1, 0:128], gns[0:1, 16:32],
                    start=True, stop=True), [(pe, 1)])
                pb = P[:, 3584:3600]
                B.wait("vector", pe, pe.n)
                B.op("vector", lambda e: e.tensor_copy(gns[:, 0:16], pb[:]))
                B.op("vector", lambda e: e.drain())
                B.op("vector", lambda e: e.scalar_tensor_tensor(
                    gns[:, 48:56], gns[:, 0:8], 1.0 / (GN_N * GN_N), gns[:, 0:8],
                    op0=OP.mult, op1=OP.mult))
                B.op("vector", lambda e: e.tensor_scalar(
                    gns[:, 16:24], gns[:, 0:8], 1.0 / GN_N, None, op0=OP.mult))
                B.op("vector", lambda e: e.drain())
                B.op("vector", lambda e: e.scalar_tensor_tensor(
                    gns[:, 24:32], gns[:, 8:16], 1.0 / GN_N, gns[:, 48:56],
                    op0=OP.mult, op1=OP.subtract), [(ve, 1)])
                B.wait("scalar", ve, ve.n)
                B.op("scalar", lambda e: e.activation(
                    gns[:, 32:40], gns[:, 24:32], AF.Sqrt, bias=np.float32(EPS)), [(se, 1)])
                B.wait("vector", se, se.n)
                B.op("vector", lambda e: e.reciprocal(gns[:, 40:48], gns[:, 32:40]))
                B.op("vector", lambda e: e.drain())
                B.op("vector", lambda e: e.tensor_tensor(
                    gns[:, 48:56], gns[:, 40:48], gnw_sb[:], op=OP.mult))
                B.op("vector", lambda e: e.scalar_tensor_tensor(
                    gns[:, 56:64], gns[:, 16:24], -1.0, gns[:, 40:48],
                    op0=OP.mult, op1=OP.mult))
                B.op("vector", lambda e: e.drain())
                B.op("vector", lambda e: e.scalar_tensor_tensor(
                    gns[:, 16:24], gns[:, 56:64], 1.0, gnw_sb[:],
                    op0=OP.mult, op1=OP.mult))
                B.op("vector", lambda e: e.drain())
                B.op("vector", lambda e: e.tensor_tensor(
                    gns[:, 24:32], gns[:, 16:24], gnb_sb[:], op=OP.add), [(ve, 1)])
                gn_ve = ve.n

                po = [P[:, 0:1024], P[:, 1024:2048]]
                store_done = {}
                wo_done = {}
                resv = {}
                for t in range(NT):
                    p = t % 2
                    ld_x = ld_xp[p]
                    st_s = st_sp[p]
                    xin_p = xin[:, p * 2 * D:(p + 1) * 2 * D]
                    if t >= 2:
                        B.wait("sync", st_s, store_done[p])
                    for _c in range(2):
                        B.op("sync", lambda e, _t=t, _x=xin_p, _cc=_c: e.dma_start(
                            out=_x[:, _cc * D:(_cc + 1) * D], in_=tile_half(x1s, _t, _cc)), [(ld_x, 16)])
                    x_ld = ld_x.n
                    B.wait("scalar", ve, gn_ve)
                    if t >= 1:
                        B.wait("scalar", pe, wo_done[t - 1])
                    for h in range(NH):
                        B.op("scalar", lambda e, _t=t, _h=h: e.activation(
                            ynb[:, _h * T:(_h + 1) * T],
                            ysb[:, _h * L + _t * T:_h * L + (_t + 1) * T],
                            AF.Identity, scale=gns[:, 48 + _h:49 + _h],
                            bias=gns[:, 24 + _h:25 + _h]), [(se, 1)])
                    yn_se = se.n
                    B.wait("tensor", se, yn_se)
                    if t >= 2:
                        B.wait("tensor", ve, resv[t - 2])
                    for c in range(2):
                        if c == 1:
                            B.wait("tensor", ve, ve.n)
                        for eh in range(NH):
                            B.op("tensor", lambda e, _p=p, _c=c, _e=eh: e.matmul(
                                po[_p][:, 0:512], ynb[:, _e * T + _c * 128:_e * T + (_c + 1) * 128],
                                wo_sb[:, _e * D + 0:_e * D + 512],
                                start=(_e == 0), stop=(_e == 7)))
                            B.op("tensor", lambda e, _p=p, _c=c, _e=eh: e.matmul(
                                po[_p][:, 512:1024], ynb[:, _e * T + _c * 128:_e * T + (_c + 1) * 128],
                                wo_sb[:, _e * D + 512:_e * D + 1024],
                                start=(_e == 0), stop=(_e == 7)),
                                [(pe, 1)] if eh == 7 else ())
                        wope = pe.n
                        B.wait("vector", pe, wope)
                        B.wait("vector", ld_x, x_ld)
                        B.op("vector", lambda e, _c=c, _x=xin_p, _p=p: e.scalar_tensor_tensor(
                            _x[:, _c * D:(_c + 1) * D], po[_p], 1.0, _x[:, _c * D:(_c + 1) * D],
                            op0=OP.mult, op1=OP.add), [(ve, 1)])
                    wo_done[t] = pe.n
                    resv[t] = ve.n
                    res_ve = ve.n
                    B.wait("sync", ve, res_ve)
                    for _c in range(2):
                        B.op("sync", lambda e, _t=t, _x=xin_p, _cc=_c: e.dma_start(
                            out=tile_half(x2s, _t, _cc), in_=_x[:, _cc * D:(_cc + 1) * D]), [(st_s, 16)])
                    store_done[p] = st_s.n
            barrier()
            build_lnpass(x2s, xn2, "2")
            barrier()

        if n_phases >= 4:
            w3scope = ExitStack()
            wg3 = w3scope.enter_context(nc.sbuf_tensor("wg3_sb", [128, 8 * HF], BF16))
            wu3 = w3scope.enter_context(nc.sbuf_tensor("wu3_sb", [128, 8 * HF], BF16))
            wd3 = w3scope.enter_context(nc.sbuf_tensor("wd3_sb", [128, 24 * D], BF16))
            wst3 = w3scope.enter_context(nc.sbuf_tensor("wstage3", [128, HF], I32))
            w3_done = ffn_weight_prologue(wg3, wu3, wd3, wst3)
            build_ffn(xn2, x2s, out_hbm, None, wg3, wu3, wd3, w3_done, final=True, tag="3")
            w3scope.close()

        B.wait("sync", st_sp[0], st_sp[0].n)
        B.wait("sync", st_sp[1], st_sp[1].n)
        build_nc.last_sched = B.snapshot()
        B.flush_block()

    return nc


_NC_CACHE = {}


def _get_nc(n_phases=4):
    if n_phases not in _NC_CACHE:
        _NC_CACHE[n_phases] = build_nc(n_phases)
    return _NC_CACHE[n_phases]


def _host_constants():
    gam = _gammas()
    ident = np.eye(128, dtype=BF)
    j = np.arange(128)[:, None]
    i = np.arange(128)[None, :]
    maskc = np.zeros((128, NH * 128), dtype=np.float64)
    for h in range(NH):
        maskc[:, h * 128:(h + 1) * 128] = np.where(i >= j, gam[h] ** (-128.0), 0.0)
    qdec = np.zeros((128, NH * T), dtype=np.float64)
    ii = np.arange(T) % 128
    for h in range(NH):
        qdec[:, h * T:(h + 1) * T] = (gam[h] ** (ii + 1.0))[None, :]
    kdec = np.zeros((128, D), dtype=np.float64)
    jj = np.arange(128)
    for h in range(NH):
        kdec[:, h * 128:(h + 1) * 128] = (gam[h] ** (127.0 - jj))[:, None]
    return ident, maskc.astype(BF), qdec.astype(BF), kdec.astype(BF)


def kernel(L0_attn_x, L0_attn_Wq, L0_attn_Wk, L0_attn_Wv, L0_attn_Wo,
           L0_attn_gn_weight, L0_attn_gn_bias,
           L0_ffn_w_gate, L0_ffn_w_up, L0_ffn_w_down, _n_phases=4):
    x = np.ascontiguousarray(np.asarray(L0_attn_x, dtype=np.float32))
    wq = np.ascontiguousarray(np.asarray(L0_attn_Wq, dtype=np.float32).T)
    wk = np.ascontiguousarray(np.asarray(L0_attn_Wk, dtype=np.float32).T)
    wv = np.ascontiguousarray(np.asarray(L0_attn_Wv, dtype=np.float32).T)
    wo = np.ascontiguousarray(np.asarray(L0_attn_Wo, dtype=np.float32).T)
    wg = np.ascontiguousarray(np.asarray(L0_ffn_w_gate, dtype=np.int32).T)
    wu = np.ascontiguousarray(np.asarray(L0_ffn_w_up, dtype=np.int32).T)
    wd = np.ascontiguousarray(np.asarray(L0_ffn_w_down, dtype=np.int32).T)
    gnw = np.ascontiguousarray(np.asarray(L0_attn_gn_weight, dtype=np.float32).reshape(NH, 128).T)
    gnb = np.ascontiguousarray(np.asarray(L0_attn_gn_bias, dtype=np.float32).reshape(NH, 128).T)
    ident, maskc, qdec, kdec = _host_constants()

    nc = _get_nc(_n_phases)
    shared = {"wqT": wq, "wkT": wk, "wvT": wv, "woT": wo,
              "wgT": wg, "wuT": wu, "wdT": wd,
              "gnw": gnw, "gnb": gnb, "ident": ident,
              "maskc": maskc, "qdec": qdec, "kdec": kdec}
    in_maps = [dict(shared, x=np.ascontiguousarray(x[b])) for b in range(8)]
    res = run_bass_kernel_spmd(nc, in_maps, core_ids=list(range(8)))
    out = np.stack([res.results[b]["out"] for b in range(8)], axis=0)
    return out


# version 06
# speedup vs baseline: 1.4451x; 1.0505x over previous
import numpy as np
import math
from contextlib import ExitStack
ExitSt = ExitStack
import ml_dtypes
import concourse.bass as bass
import concourse.mybir as mybir
from concourse.bass_utils import run_bass_kernel_spmd

F32 = mybir.dt.float32
BF16 = mybir.dt.bfloat16
I32 = mybir.dt.int32
OP = mybir.AluOpType
AF = mybir.ActivationFunctionType
BF = ml_dtypes.bfloat16

L = 4096
D = 1024
HF = 3072
NH = 8
DH = 128
T = 256
NT = L // T
FFN_TILES = NT
ATTN_TILES = NT
EPS = 1e-5
GN_N = float(L * DH)

ENGINES = ("sync", "tensor", "vector", "scalar", "gpsimd")


def _gammas():
    return 1.0 - np.exp(np.linspace(math.log(1.0 / 32.0), math.log(1.0 / 512.0), NH))


class Sem:
    __slots__ = ("h", "n", "name")


class Sched:
    def __init__(self, nc, stack):
        self.nc = nc
        self.stack = stack
        self.q = {e: [] for e in ENGINES}

    def sem(self, name):
        s = Sem()
        s.h = self.stack.enter_context(self.nc.semaphore(name))
        s.n = 0
        s.name = name
        return s

    def wait(self, eng, sem, val=None):
        v = sem.n if val is None else val
        if v <= 0:
            return
        f = lambda e, _h=sem.h, _v=v: e.wait_ge(_h, _v)
        f.meta = ("wait", sem.name, v)
        self.q[eng].append(f)

    def op(self, eng, fn, incs=()):
        incs = tuple(incs)

        def run(e, _fn=fn, _incs=incs):
            ins = _fn(e)
            for s, k in _incs:
                ins.then_inc(s.h, k)

        run.meta = ("op", tuple((s.name, k) for s, k in incs), getattr(fn, "__qualname__", ""))
        self.q[eng].append(run)
        for s, k in incs:
            s.n += k

    def snapshot(self):
        return {e: list(self.q[e]) for e in ENGINES}

    def flush_block(self):
        with self.nc.Block() as blk:
            for eng in ENGINES:
                ops = self.q[eng]

                def body(e, _ops=ops):
                    for f in _ops:
                        f(e)

                getattr(blk, eng)(body)
        self.q = {e: [] for e in ENGINES}


def build_nc(n_phases=4):
    nc = bass.Bass("TRN2", target_bir_lowering=False)

    x_in = nc.dram_tensor("x", [L, D], F32, kind="ExternalInput").ap()
    wqT = nc.dram_tensor("wqT", [D, D], F32, kind="ExternalInput").ap()
    wkT = nc.dram_tensor("wkT", [D, D], F32, kind="ExternalInput").ap()
    wvT = nc.dram_tensor("wvT", [D, D], F32, kind="ExternalInput").ap()
    woT = nc.dram_tensor("woT", [D, D], F32, kind="ExternalInput").ap()
    wgT = nc.dram_tensor("wgT", [D, HF], I32, kind="ExternalInput").ap()
    wuT = nc.dram_tensor("wuT", [D, HF], I32, kind="ExternalInput").ap()
    wdT = nc.dram_tensor("wdT", [HF, D], I32, kind="ExternalInput").ap()
    gnw_in = nc.dram_tensor("gnw", [128, NH], F32, kind="ExternalInput").ap()
    gnb_in = nc.dram_tensor("gnb", [128, NH], F32, kind="ExternalInput").ap()
    ident_in = nc.dram_tensor("ident", [128, 128], BF16, kind="ExternalInput").ap()
    mask_in = nc.dram_tensor("maskc", [128, NH * 128], BF16, kind="ExternalInput").ap()
    qdec_in = nc.dram_tensor("qdec", [128, NH * T], BF16, kind="ExternalInput").ap()
    kdec_in = nc.dram_tensor("kdec", [128, D], BF16, kind="ExternalInput").ap()
    out_hbm = nc.dram_tensor("out", [L, D], F32, kind="ExternalOutput").ap()

    xn0 = nc.dram_tensor("xn0_scr", [L, D], BF16, kind="Internal").ap()
    xn1 = nc.dram_tensor("xn1_scr", [L, D], BF16, kind="Internal").ap()
    xn2 = nc.dram_tensor("xn2_scr", [L, D], BF16, kind="Internal").ap()
    x1s = nc.dram_tensor("x1_scr", [L, D], F32, kind="Internal").ap()
    x2s = nc.dram_tensor("x2_scr", [L, D], F32, kind="Internal").ap()

    gam = _gammas()
    gam128 = [float(g) ** 128 for g in gam]

    def tile_half(ap, t, c):
        return ap[t * T + c * 128:t * T + (c + 1) * 128, :]

    with ExitStack() as ES:
        B = Sched(nc, ES)
        ld_xp = [B.sem("ld_x0"), B.sem("ld_x1")]
        ld_tp = [B.sem("ld_t0"), B.sem("ld_t1")]
        ld_c = B.sem("ld_c")
        ld_w = B.sem("ld_w")
        st_sp = [B.sem("st_s0"), B.sem("st_s1")]
        pe = B.sem("pe")
        ve = B.sem("ve")
        se = B.sem("se")
        ge = B.sem("ge")
        bar = B.sem("bar")

        junk = ES.enter_context(nc.sbuf_tensor("junkbuf", [128, 2 * D], BF16))
        junk_v = junk[:, 0:D]
        junk_v1 = junk_v
        junk_s = junk[:, D:2 * D]
        junk_s1 = junk_s
        junk_last = {}

        def junk_guard(eng, key, sem):
            if key in junk_last:
                s_, v_ = junk_last[key]
                B.wait(eng, s_, v_)

        def junk_mark(key, sem):
            junk_last[key] = (sem, sem.n)
        ident_sb = ES.enter_context(nc.sbuf_tensor("ident_sb", [128, 128], BF16))
        ones_sb = ES.enter_context(nc.sbuf_tensor("ones_sb", [128, 128], F32))
        psum = ES.enter_context(nc.psum_tensor("psum_all", [128, 4096], F32))
        P = psum.ap()
        P_bf = P.bitcast(BF16)

        nbar = [0]

        def barrier():
            for s_ in (ld_xp[0], ld_xp[1], ld_tp[0], ld_tp[1], ld_c, st_sp[0], st_sp[1]):
                B.wait("sync", s_, s_.n)
            B.wait("gpsimd", ld_w, ld_w.n)
            for eng in ENGINES:
                if eng == "sync":
                    B.op(eng, lambda e: e.nop(), [(bar, 1)])
                else:
                    B.op(eng, lambda e: e.drain(), [(bar, 1)])
            nbar[0] += 1
            for eng in ENGINES:
                B.wait(eng, bar, 5 * nbar[0])

        B.op("sync", lambda e: e.dma_start(out=ident_sb[:], in_=ident_in), [(ld_c, 16)])
        B.op("vector", lambda e: e.memset(ones_sb[:], 1.0), [(ve, 1)])

        def ln_finalize_apply(stats, x_tile, xn_tile, sumsq_ready_se):
            B.wait("vector", se, sumsq_ready_se)
            B.op("vector", lambda e, _s=stats: e.drain())
            B.op("vector", lambda e, _s=stats: e.scalar_tensor_tensor(
                _s[:, 4:6], _s[:, 0:2], 1.0 / (D * D), _s[:, 0:2],
                op0=OP.mult, op1=OP.mult))
            B.op("vector", lambda e, _s=stats: e.tensor_scalar(
                _s[:, 6:8], _s[:, 0:2], 1.0 / D, None, op0=OP.mult))
            B.op("vector", lambda e: e.drain())
            B.op("vector", lambda e, _s=stats: e.scalar_tensor_tensor(
                _s[:, 8:10], _s[:, 2:4], 1.0 / D, _s[:, 4:6],
                op0=OP.mult, op1=OP.subtract), [(ve, 1)])
            B.wait("scalar", ve, ve.n)
            B.op("scalar", lambda e, _s=stats: e.activation(
                _s[:, 10:12], _s[:, 8:10], AF.Sqrt, bias=np.float32(EPS)), [(se, 1)])
            B.wait("vector", se, se.n)
            B.op("vector", lambda e, _s=stats: e.reciprocal(_s[:, 12:14], _s[:, 10:12]))
            B.op("vector", lambda e: e.drain())
            B.op("vector", lambda e, _s=stats: e.scalar_tensor_tensor(
                _s[:, 14:16], _s[:, 6:8], -1.0, _s[:, 12:14],
                op0=OP.mult, op1=OP.mult), [(ve, 1)])
            B.wait("scalar", ve, ve.n)
            for c in range(2):
                B.op("scalar", lambda e, _s=stats, _x=x_tile, _o=xn_tile, _c=c: e.activation(
                    _o[:, _c * D:(_c + 1) * D], _x[:, _c * D:(_c + 1) * D],
                    AF.Identity, bias=_s[:, 14 + _c:15 + _c],
                    scale=_s[:, 12 + _c:13 + _c]), [(se, 1)])
            return se.n

        def ffn_weight_prologue(wg_sb, wu_sb, wd_sb, wstage):
            for wdram, wsb, sc in ((wgT, wg_sb, 1.0 / 127.0), (wuT, wu_sb, 1.0 / 127.0)):
                for d in range(8):
                    B.wait("gpsimd", ge, ge.n)
                    B.op("gpsimd", lambda e, _w=wdram, _d=d, _st=wstage: e.dma_start(
                        out=_st[:], in_=_w[_d * 128:(_d + 1) * 128, :]), [(ld_w, 16)])
                    B.wait("vector", ld_w, ld_w.n)
                    B.op("vector", lambda e, _sb=wsb, _d=d, _st=wstage, _sc=sc: e.tensor_scalar(
                        _sb[:, _d * HF:(_d + 1) * HF], _st[:], _sc, None,
                        op0=OP.mult), [(ge, 1)])
            for h in range(24):
                B.wait("gpsimd", ge, ge.n)
                B.op("gpsimd", lambda e, _h=h, _st=wstage: e.dma_start(
                    out=_st[:, 0:D], in_=wdT[_h * 128:(_h + 1) * 128, :]), [(ld_w, 16)])
                B.wait("vector", ld_w, ld_w.n)
                B.op("vector", lambda e, _h=h, _st=wstage: e.tensor_scalar(
                    wd_sb[:, _h * D:(_h + 1) * D], _st[:, 0:D], 0.5 / 127.0, None,
                    op0=OP.mult), [(ge, 1)])
            return ge.n

        w1scope = ExitStack()
        w1_done = 0
        if n_phases >= 2:
            wg1 = w1scope.enter_context(nc.sbuf_tensor("wg_sb", [128, 8 * HF], BF16))
            wu1 = w1scope.enter_context(nc.sbuf_tensor("wu_sb", [128, 8 * HF], BF16))
            wd1 = w1scope.enter_context(nc.sbuf_tensor("wd_sb", [128, 24 * D], BF16))
            wst1 = w1scope.enter_context(nc.sbuf_tensor("wstage", [128, HF], I32))
            w1_done = ffn_weight_prologue(wg1, wu1, wd1, wst1)

        def build_lnpass(src_dram, xn_dram, tag):
            G = 4
            NCH = 2 * G
            with nc.sbuf_tensor("pp_xin" + tag, [128, NCH * D], F32) as xin, \
                 nc.sbuf_tensor("pp_stats" + tag, [128, 64], F32) as stats:
                xin_bf = xin[:].bitcast(BF16)
                st_s = st_sp[0]
                ld_x = ld_xp[0]
                store_done = 0
                for g in range(NT // G):
                    if g >= 1:
                        B.wait("sync", st_s, store_done)
                    for cc in range(NCH):
                        B.op("sync", lambda e, _g=g, _cc=cc: e.dma_start(
                            out=xin[:, _cc * D:(_cc + 1) * D],
                            in_=tile_half(src_dram, _g * G + _cc // 2, _cc % 2)), [(ld_x, 16)])
                    x_ld = ld_x.n
                    B.wait("vector", ld_x, x_ld)
                    B.wait("scalar", ld_x, x_ld)
                    if g >= 1:
                        B.wait("vector", st_s, store_done)
                        B.wait("scalar", st_s, store_done)
                    for cc in range(NCH):
                        junk_guard("vector", ("v", 0), ve)
                        B.op("vector", lambda e, _cc=cc: e.tensor_scalar(
                            junk_v[:], xin[:, _cc * D:(_cc + 1) * D], 1.0, None, op0=OP.mult,
                            op1=OP.add, accum_out=stats[:, _cc:_cc + 1]), [(ve, 1)])
                        junk_mark(("v", 0), ve)
                        B.op("vector", lambda e: e.drain())
                        junk_guard("scalar", ("s", 0), se)
                        B.op("scalar", lambda e, _cc=cc: e.activation(
                            junk_s[:], xin[:, _cc * D:(_cc + 1) * D], AF.Square,
                            accum_out=stats[:, 8 + _cc:9 + _cc]), [(se, 1)])
                        junk_mark(("s", 0), se)
                    B.wait("vector", se, se.n)
                    B.op("vector", lambda e: e.drain())
                    B.op("vector", lambda e: e.scalar_tensor_tensor(
                        stats[:, 16:24], stats[:, 0:8], 1.0 / (D * D), stats[:, 0:8],
                        op0=OP.mult, op1=OP.mult))
                    B.op("vector", lambda e: e.tensor_scalar(
                        stats[:, 24:32], stats[:, 0:8], 1.0 / D, None, op0=OP.mult))
                    B.op("vector", lambda e: e.drain())
                    B.op("vector", lambda e: e.scalar_tensor_tensor(
                        stats[:, 32:40], stats[:, 8:16], 1.0 / D, stats[:, 16:24],
                        op0=OP.mult, op1=OP.subtract), [(ve, 1)])
                    B.wait("scalar", ve, ve.n)
                    B.op("scalar", lambda e: e.activation(
                        stats[:, 40:48], stats[:, 32:40], AF.Sqrt, bias=np.float32(EPS)), [(se, 1)])
                    B.wait("vector", se, se.n)
                    B.op("vector", lambda e: e.reciprocal(stats[:, 48:56], stats[:, 40:48]))
                    B.op("vector", lambda e: e.drain())
                    B.op("vector", lambda e: e.scalar_tensor_tensor(
                        stats[:, 56:64], stats[:, 24:32], -1.0, stats[:, 48:56],
                        op0=OP.mult, op1=OP.mult), [(ve, 1)])
                    B.wait("scalar", ve, ve.n)
                    for cc in range(NCH):
                        B.op("scalar", lambda e, _cc=cc: e.activation(
                            xin_bf[:, _cc * 2048:_cc * 2048 + D], xin[:, _cc * D:(_cc + 1) * D],
                            AF.Identity, bias=stats[:, 56 + _cc:57 + _cc],
                            scale=stats[:, 48 + _cc:49 + _cc]), [(se, 1)])
                    apply_se = se.n
                    B.wait("sync", se, apply_se)
                    for cc in range(NCH):
                        B.op("sync", lambda e, _g=g, _cc=cc: e.dma_start(
                            out=tile_half(xn_dram, _g * G + _cc // 2, _cc % 2),
                            in_=xin_bf[:, _cc * 2048:_cc * 2048 + D]), [(st_s, 16)])
                    store_done = st_s.n

        if n_phases >= 1:
            build_lnpass(x_in, xn0, "0")
            barrier()

        def build_ffn(xn_src, x_src, x_dst, xn_dst, wg_sb, wu_sb, wd_sb,
                      weights_ready_ge, final, tag=""):
            with nc.sbuf_tensor("f_xnf" + tag, [128, 2 * 8 * T], BF16) as xnf, \
                 nc.sbuf_tensor("f_xin" + tag, [128, 2 * 2 * D], F32) as xin, \
                 nc.sbuf_tensor("f_hid" + tag, [128, 24 * T], BF16) as hid, \
                 nc.sbuf_tensor("f_gsb" + tag, [128, 2 * T], F32) as gsb:
                pg = [P[:, 0:T], P[:, 3072:3072 + T]]
                pu = [P[:, 512:512 + T], P[:, 3584:3584 + T]]
                pd = [P[:, 1024:2048], P[:, 2048:3072]]
                silu_done = {}
                hid_done = {}
                res_done = [0]
                store_done = {}
                xnf_used = {}
                B.wait("tensor", ge, weights_ready_ge)
                B.wait("sync", ge, weights_ready_ge)
                for t in range(FFN_TILES):
                    p = t % 2
                    ld_x = ld_xp[p]
                    ld_t = ld_tp[p]
                    st_out = st_sp[p]
                    xin_p = xin[:, p * 2 * D:(p + 1) * 2 * D]
                    xnf_p = [xnf[:, (p * 8 + d) * T:(p * 8 + d + 1) * T] for d in range(8)]
                    if t >= 2:
                        B.wait("sync", st_out, store_done[p])
                        B.wait("sync", pe, xnf_used[p])
                        B.wait("scalar", pe, xnf_used[p])
                    for d in range(8):
                        B.op("sync" if d % 2 == 0 else "scalar",
                             lambda e, _t=t, _d=d, _o=xnf_p[d]: e.dma_start_transpose(
                            out=_o, in_=xn_src[_t * T:(_t + 1) * T, _d * 128:(_d + 1) * 128]),
                            [(ld_t, 16)])
                    xnf_ld = ld_t.n
                    for _c in range(2):
                        B.op("sync", lambda e, _t=t, _x=xin_p, _cc=_c: e.dma_start(
                            out=_x[:, _cc * D:(_cc + 1) * D], in_=tile_half(x_src, _t, _cc)), [(ld_x, 16)])
                    x_ld = ld_x.n

                    B.wait("tensor", ld_t, xnf_ld)
                    if t >= 1:
                        B.wait("tensor", ve, res_done[0])

                    def down_mms(ha, hp):
                        B.wait("tensor", ve, hid_done[ha])
                        for c in range(2):
                            for half in range(2):
                                last = (hp == 23 and c == 1 and half == 1)
                                B.op("tensor", lambda e, _h=hp, _c=c, _f=half: e.matmul(
                                    pd[_c][:, _f * 512:(_f + 1) * 512],
                                    hid[:, _h * T + _c * 128:_h * T + (_c + 1) * 128],
                                    wd_sb[:, _h * D + _f * 512:_h * D + (_f + 1) * 512],
                                    start=(_h == 0), stop=(_h == 23)),
                                    [(pe, 1)] if last else ())

                    for h in range(24):
                        ha = t * 24 + h
                        bk = h % 2
                        if ha >= 2:
                            B.wait("tensor", se, silu_done[ha - 2])
                        for d in range(8):
                            B.op("tensor", lambda e, _h=h, _d=d, _b=bk, _xn=xnf_p[d]: e.matmul(
                                pg[_b], wg_sb[:, _d * HF + _h * 128:_d * HF + (_h + 1) * 128],
                                _xn, start=(_d == 0), stop=(_d == 7)),
                                [(pe, 1)] if d == 7 else ())
                        gate_pe = pe.n
                        if ha >= 2:
                            B.wait("tensor", ve, hid_done[ha - 2])
                        for d in range(8):
                            B.op("tensor", lambda e, _h=h, _d=d, _b=bk, _xn=xnf_p[d]: e.matmul(
                                pu[_b], wu_sb[:, _d * HF + _h * 128:_d * HF + (_h + 1) * 128],
                                _xn, start=(_d == 0), stop=(_d == 7)),
                                [(pe, 1)] if d == 7 else ())
                        up_pe = pe.n
                        if h >= 1:
                            down_mms(ha - 1, h - 1)
                        B.wait("scalar", pe, gate_pe)
                        if ha >= 2:
                            B.wait("scalar", ve, hid_done[ha - 2])
                        B.op("scalar", lambda e, _b=bk: e.activation(
                            gsb[:, _b * T:(_b + 1) * T], pg[_b], AF.Silu), [(se, 1)])
                        silu_done[ha] = se.n
                        B.wait("vector", se, silu_done[ha])
                        B.wait("vector", pe, up_pe)
                        B.op("vector", lambda e, _h=h, _b=bk: e.tensor_tensor(
                            hid[:, _h * T:(_h + 1) * T], pu[_b], gsb[:, _b * T:(_b + 1) * T],
                            op=OP.mult), [(ve, 1)])
                        hid_done[ha] = ve.n
                    down_mms(t * 24 + 23, 23)
                    down_pe = pe.n
                    xnf_used[p] = pe.n
                    B.wait("vector", pe, down_pe)
                    B.wait("vector", ld_x, x_ld)
                    for c in range(2):
                        B.op("vector", lambda e, _c=c, _x=xin_p: e.scalar_tensor_tensor(
                            _x[:, _c * D:(_c + 1) * D], pd[_c], 1.0, _x[:, _c * D:(_c + 1) * D],
                            op0=OP.mult, op1=OP.add),
                            [(ve, 1)])
                    res_done[0] = ve.n
                    res_ve = ve.n
                    B.wait("sync", ve, res_ve)
                    for _c in range(2):
                        B.op("sync", lambda e, _t=t, _x=xin_p, _cc=_c: e.dma_start(
                            out=tile_half(x_dst, _t, _cc), in_=_x[:, _cc * D:(_cc + 1) * D]), [(st_out, 16)])
                    store_done[p] = st_out.n

        if n_phases >= 2:
            build_ffn(xn0, x_in, x1s, None, wg1, wu1, wd1, w1_done,
                      final=(n_phases == 2))
            barrier()
            if n_phases >= 3:
                build_lnpass(x1s, xn1, "1")
                barrier()
        w1scope.close()

        if n_phases >= 3:
            with ExitSt() as AS:
                al = AS.enter_context
                wq_sb = al(nc.sbuf_tensor("a_wq", [128, 8 * D], BF16))
                wk_sb = al(nc.sbuf_tensor("a_wk", [128, 8 * D], BF16))
                wv_sb = al(nc.sbuf_tensor("a_wv", [128, 8 * D], BF16))
                wo_sb = al(nc.sbuf_tensor("a_wo", [128, 8 * D], BF16))
                awst = al(nc.sbuf_tensor("a_wstage", [128, D], F32))
                ysb = al(nc.sbuf_tensor("a_ys", [128, NH * L], BF16))
                xnf = al(nc.sbuf_tensor("a_xnf", [128, 2 * 8 * T], BF16))
                qt = al(nc.sbuf_tensor("a_qt", [128, NH * T], BF16))
                ktm = al(nc.sbuf_tensor("a_ktm", [128, 2 * D], BF16))
                kfm = al(nc.sbuf_tensor("a_kfm", [128, 2 * D], BF16))
                vtm = al(nc.sbuf_tensor("a_vtm", [128, 2 * D], BF16))
                adb = al(nc.sbuf_tensor("a_ad", [128, 2 * 128], BF16))
                s0f = al(nc.sbuf_tensor("a_s0f", [128, NH * DH], F32))
                s0b = al(nc.sbuf_tensor("a_s0b", [128, NH * DH], BF16))
                mask_sb = al(nc.sbuf_tensor("a_mask", [128, NH * 128], BF16))
                qdec_sb = al(nc.sbuf_tensor("a_qdec", [128, NH * T], BF16))
                kdec_sb = al(nc.sbuf_tensor("a_kdec", [128, D], BF16))
                ysum = al(nc.sbuf_tensor("a_ysum", [128, NH * 32], F32))
                ysq = al(nc.sbuf_tensor("a_ysq", [128, NH * 32], F32))
                gnw_sb = al(nc.sbuf_tensor("a_gnw", [128, NH], F32))
                gnb_sb = al(nc.sbuf_tensor("a_gnb", [128, NH], F32))
                gns = al(nc.sbuf_tensor("a_gns", [128, 64], F32))
                ynb = al(nc.sbuf_tensor("a_yn", [128, NH * T], BF16))
                xin = al(nc.sbuf_tensor("a_xin", [128, 2 * 2 * D], F32))

                B.op("sync", lambda e: e.dma_start(out=mask_sb[:], in_=mask_in), [(ld_c, 16)])
                B.op("sync", lambda e: e.dma_start(out=qdec_sb[:], in_=qdec_in), [(ld_c, 16)])
                B.op("sync", lambda e: e.dma_start(out=kdec_sb[:], in_=kdec_in), [(ld_c, 16)])
                B.op("sync", lambda e: e.dma_start(out=gnw_sb[:], in_=gnw_in), [(ld_c, 16)])
                B.op("sync", lambda e: e.dma_start(out=gnb_sb[:], in_=gnb_in), [(ld_c, 16)])
                const_ld = ld_c.n
                for wdram, wsb in ((wqT, wq_sb), (wkT, wk_sb), (wvT, wv_sb), (woT, wo_sb)):
                    for d in range(8):
                        B.wait("gpsimd", ge, ge.n)
                        B.op("gpsimd", lambda e, _w=wdram, _d=d: e.dma_start(
                            out=awst[:], in_=_w[_d * 128:(_d + 1) * 128, :]), [(ld_w, 16)])
                        B.wait("vector", ld_w, ld_w.n)
                        B.op("vector", lambda e, _sb=wsb, _d=d: e.tensor_copy(
                            _sb[:, _d * D:(_d + 1) * D], awst[:]), [(ge, 1)])
                aw_done = ge.n
                B.op("vector", lambda e: e.memset(s0f[:], 0.0))
                B.op("vector", lambda e: e.memset(s0b[:], 0.0), [(ve, 1)])
                s0_init = ve.n
                B.wait("vector", ld_c, const_ld)

                paq = [P[:, 0:T], P[:, 512:512 + T]]
                pa = [P[:, 0:128], P[:, 512:640]]
                pk = P[:, 1024:2048]
                py = [P[:, 1024:1152], P[:, 1536:1664]]
                pv = P[:, 2048:3072]
                pkv = [P[:, 2048:2176], P[:, 2560:2688]]
                ptr = [P_bf[:, 6144:7168], P_bf[:, 7168:8192]]

                B.wait("tensor", ge, aw_done)
                B.wait("tensor", ve, s0_init)

                q_evac = {}
                ktm_evac = {}
                kfm_evac = {}
                v_evac = {}
                ad_evac = {}
                supd = {}
                s0cast = {}
                a_done = {}
                y_done = {}
                kv_done = {}
                ysq_done = {}
                xnf_used = {}

                for t in range(NT):
                    p = t % 2
                    ld_t = ld_tp[p]
                    xnf_p = [xnf[:, (p * 8 + d) * T:(p * 8 + d + 1) * T] for d in range(8)]
                    if t >= 2:
                        B.wait("sync", pe, xnf_used[p])
                        B.wait("scalar", pe, xnf_used[p])
                    for d in range(8):
                        B.op("sync" if d % 2 == 0 else "scalar",
                             lambda e, _t=t, _d=d, _o=xnf_p[d]: e.dma_start_transpose(
                            out=_o, in_=xn1[_t * T:(_t + 1) * T, _d * 128:(_d + 1) * 128]),
                            [(ld_t, 16)])
                    xnf_ld = ld_t.n
                    B.wait("tensor", ld_t, xnf_ld)
                    for h in range(NH):
                        bk = h % 2
                        if t >= 1:
                            B.wait("tensor", ve, ad_evac[(2 * t - 1, 6 + bk)])
                        if h >= 2:
                            B.wait("tensor", ve, q_evac[(t, h - 2)])
                        for d in range(8):
                            B.op("tensor", lambda e, _h=h, _d=d, _b=bk, _xn=xnf_p[d]: e.matmul(
                                paq[_b], wq_sb[:, _d * D + _h * 128:_d * D + (_h + 1) * 128],
                                _xn, start=(_d == 0), stop=(_d == 7)),
                                [(pe, 1)] if d == 7 else ())
                        qpe = pe.n
                        B.wait("vector", pe, qpe)
                        if t >= 1:
                            B.wait("vector", pe, y_done[(2 * t - 1, h)])
                        B.op("vector", lambda e, _h=h, _b=bk: e.tensor_tensor(
                            qt[:, _h * T:(_h + 1) * T], paq[_b], qdec_sb[:, _h * T:(_h + 1) * T],
                            op=OP.mult), [(ve, 1)])
                        q_evac[(t, h)] = ve.n
                    for c in range(2):
                        gc = 2 * t + c
                        if c == 0:
                            if gc >= 1:
                                B.wait("tensor", se, ysq_done[(gc - 1, 7)])
                        else:
                            B.wait("tensor", ve, ktm_evac[gc - 1])
                        for half in range(2):
                            for d in range(8):
                                B.op("tensor", lambda e, _c=c, _f=half, _d=d, _xn=xnf_p[d]: e.matmul(
                                    pk[:, _f * 512:(_f + 1) * 512],
                                    _xn[:, _c * 128:(_c + 1) * 128],
                                    wk_sb[:, _d * D + _f * 512:_d * D + (_f + 1) * 512],
                                    start=(_d == 0), stop=(_d == 7)),
                                    [(pe, 1)] if (d == 7 and half == 1) else ())
                        kpe = pe.n
                        B.wait("vector", pe, kpe)
                        if gc >= 2:
                            B.wait("vector", pe, kv_done[(gc - 2, 7)])
                        B.op("vector", lambda e, _c=c: e.tensor_tensor(
                            ktm[:, _c * D:(_c + 1) * D], pk[:], kdec_sb[:], op=OP.mult),
                            [(ve, 1)])
                        ktm_evac[gc] = ve.n
                        if c == 0:
                            if gc >= 1:
                                B.wait("tensor", ve, supd[(gc - 1, 7)])
                        else:
                            B.wait("tensor", se, v_evac[gc - 1])
                        for half in range(2):
                            for d in range(8):
                                B.op("tensor", lambda e, _c=c, _f=half, _d=d, _xn=xnf_p[d]: e.matmul(
                                    pv[:, _f * 512:(_f + 1) * 512],
                                    _xn[:, _c * 128:(_c + 1) * 128],
                                    wv_sb[:, _d * D + _f * 512:_d * D + (_f + 1) * 512],
                                    start=(_d == 0), stop=(_d == 7)),
                                    [(pe, 1)] if (d == 7 and half == 1) else ())
                        vpe = pe.n
                        B.wait("scalar", pe, vpe)
                        if gc >= 2:
                            B.wait("scalar", pe, kv_done[(gc - 2, 7)])
                        B.op("scalar", lambda e, _c=c: e.activation(
                            vtm[:, _c * D:(_c + 1) * D], pv[:], AF.Copy), [(se, 1)])
                        v_evac[gc] = se.n
                        B.wait("tensor", ve, ktm_evac[gc])
                        if gc >= 2:
                            B.wait("tensor", ve, kfm_evac[gc - 2])
                        for h in range(NH):
                            B.op("tensor", lambda e, _c=c, _h=h: e.transpose(
                                ptr[_c][:, _h * 128:(_h + 1) * 128],
                                ktm[:, _c * D + _h * 128:_c * D + (_h + 1) * 128],
                                ident_sb[:]), [(pe, 1)] if h == NH - 1 else ())
                        trpe = pe.n
                        B.wait("vector", pe, trpe)
                        if gc >= 2:
                            B.wait("vector", pe, a_done[(gc - 2, 7)])
                        B.op("vector", lambda e, _c=c: e.tensor_copy(
                            kfm[:, _c * D:(_c + 1) * D], ptr[_c][:]), [(ve, 1)])
                        kfm_evac[gc] = ve.n
                    xnf_used[p] = pe.n
                    for c in range(2):
                        gc = 2 * t + c
                        B.wait("tensor", ve, kfm_evac[gc])
                        B.wait("tensor", se, v_evac[gc])
                        if c == 0:
                            B.wait("tensor", ve, q_evac[(t, NH - 1)])
                        for h in range(NH):
                            bk = h % 2
                            if h >= 2:
                                B.wait("tensor", ve, ad_evac[(gc, h - 2)])
                            elif gc >= 1 and c == 1:
                                B.wait("tensor", ve, ad_evac[(gc - 1, 6 + bk)])
                            B.op("tensor", lambda e, _c=c, _h=h, _b=bk: e.matmul(
                                pa[_b], kfm[:, _c * D + _h * 128:_c * D + (_h + 1) * 128],
                                qt[:, _h * T + _c * 128:_h * T + (_c + 1) * 128],
                                start=True, stop=True), [(pe, 1)])
                            a_done[(gc, h)] = pe.n
                            B.wait("vector", pe, a_done[(gc, h)])
                            prev_ad = (gc, h - 2) if h >= 2 else ((gc - 1, 6 + bk) if gc >= 1 else None)
                            if prev_ad is not None and prev_ad in y_done:
                                B.wait("vector", pe, y_done[prev_ad])
                            B.op("vector", lambda e, _h=h, _b=bk: e.tensor_tensor(
                                adb[:, _b * 128:(_b + 1) * 128], pa[_b],
                                mask_sb[:, _h * 128:(_h + 1) * 128], op=OP.mult), [(ve, 1)])
                            ad_evac[(gc, h)] = ve.n
                            B.wait("tensor", ve, ad_evac[(gc, h)])
                            if h >= 2:
                                B.wait("tensor", se, ysq_done[(gc, h - 2)])
                            elif c == 1:
                                B.wait("tensor", se, ysq_done[(gc - 1, 6 + bk)])
                            else:
                                B.wait("tensor", ve, ktm_evac[gc + 1])
                            if gc >= 1:
                                B.wait("tensor", se, s0cast[(gc - 1, h)])
                            B.op("tensor", lambda e, _c=c, _h=h, _b=bk: e.matmul(
                                py[_b], vtm[:, _c * D + _h * 128:_c * D + (_h + 1) * 128],
                                adb[:, _b * 128:(_b + 1) * 128], start=True, stop=False))
                            B.op("tensor", lambda e, _c=c, _h=h, _b=bk: e.matmul(
                                py[_b], s0b[:, _h * DH:(_h + 1) * DH],
                                qt[:, _h * T + _c * 128:_h * T + (_c + 1) * 128],
                                start=False, stop=True), [(pe, 1)])
                            y_done[(gc, h)] = pe.n
                            if h >= 2:
                                B.wait("tensor", ve, supd[(gc, h - 2)])
                            elif c == 1:
                                B.wait("tensor", ve, supd[(gc - 1, 6 + bk)])
                            else:
                                B.wait("tensor", se, v_evac[gc + 1])
                            B.op("tensor", lambda e, _c=c, _h=h, _b=bk: e.matmul(
                                pkv[_b], ktm[:, _c * D + _h * 128:_c * D + (_h + 1) * 128],
                                vtm[:, _c * D + _h * 128:_c * D + (_h + 1) * 128],
                                start=True, stop=True), [(pe, 1)])
                            kv_done[(gc, h)] = pe.n
                            B.wait("scalar", pe, y_done[(gc, h)])
                            B.op("scalar", lambda e, _gc=gc, _h=h, _b=bk: e.activation(
                                ysb[:, (_h * 32 + _gc) * 128:(_h * 32 + _gc + 1) * 128],
                                py[_b], AF.Copy,
                                accum_out=ysum[:, _h * 32 + _gc:_h * 32 + _gc + 1]), [(se, 1)])
                            junk_guard("scalar", ("ysq", 0), se)
                            B.op("scalar", lambda e, _gc=gc, _h=h, _b=bk: e.activation(
                                junk_s[:, 0:128], py[_b], AF.Square,
                                accum_out=ysq[:, _h * 32 + _gc:_h * 32 + _gc + 1]), [(se, 1)])
                            junk_mark(("ysq", 0), se)
                            ysq_done[(gc, h)] = se.n
                            B.wait("vector", pe, kv_done[(gc, h)])
                            if gc >= 1:
                                B.wait("vector", se, s0cast[(gc - 1, h)])
                            B.op("vector", lambda e, _h=h, _b=bk, _g=gam128[h]: e.scalar_tensor_tensor(
                                s0f[:, _h * DH:(_h + 1) * DH], s0f[:, _h * DH:(_h + 1) * DH],
                                _g, pkv[_b], op0=OP.mult, op1=OP.add), [(ve, 1)])
                            supd[(gc, h)] = ve.n
                            B.wait("scalar", ve, supd[(gc, h)])
                            B.op("scalar", lambda e, _h=h: e.activation(
                                s0b[:, _h * DH:(_h + 1) * DH], s0f[:, _h * DH:(_h + 1) * DH],
                                AF.Copy), [(se, 1)])
                            s0cast[(gc, h)] = se.n

                B.wait("vector", se, ysq_done[(2 * NT - 1, NH - 1)])
                for h in range(NH):
                    junk_guard("vector", ("gn", 0), ve)
                    B.op("vector", lambda e, _h=h: e.tensor_scalar(
                        junk_v[:, 0:32], ysum[:, _h * 32:(_h + 1) * 32], 1.0, None,
                        op0=OP.mult, op1=OP.add, accum_out=gns[:, _h:_h + 1]), [(ve, 1)])
                    junk_mark(("gn", 0), ve)
                    B.op("vector", lambda e: e.drain())
                for h in range(NH):
                    junk_guard("vector", ("gn", 0), ve)
                    B.op("vector", lambda e, _h=h: e.tensor_scalar(
                        junk_v[:, 0:32], ysq[:, _h * 32:(_h + 1) * 32], 1.0, None,
                        op0=OP.mult, op1=OP.add, accum_out=gns[:, 8 + _h:9 + _h]), [(ve, 1)])
                    junk_mark(("gn", 0), ve)
                    B.op("vector", lambda e: e.drain())
                B.op("vector", lambda e: e.drain(), [(ve, 1)])
                gsum_ve = ve.n
                B.wait("tensor", ve, gsum_ve)
                B.wait("tensor", se, ysq_done[(2 * NT - 1, NH - 1)])
                B.op("tensor", lambda e: e.matmul(
                    P[0:1, 3072:3088], ones_sb[:, 0:1], gns[:, 0:16],
                    start=True, stop=True), [(pe, 1)])
                B.wait("vector", pe, pe.n)
                B.op("vector", lambda e: e.tensor_copy(gns[0:1, 16:32], P[0:1, 3072:3088]),
                     [(ve, 1)])
                B.wait("tensor", ve, ve.n)
                B.op("tensor", lambda e: e.matmul(
                    P[:, 3584:3600], ones_sb[0:1, 0:128], gns[0:1, 16:32],
                    start=True, stop=True), [(pe, 1)])
                pb = P[:, 3584:3600]
                B.wait("vector", pe, pe.n)
                B.op("vector", lambda e: e.tensor_copy(gns[:, 0:16], pb[:]))
                B.op("vector", lambda e: e.drain())
                B.op("vector", lambda e: e.scalar_tensor_tensor(
                    gns[:, 48:56], gns[:, 0:8], 1.0 / (GN_N * GN_N), gns[:, 0:8],
                    op0=OP.mult, op1=OP.mult))
                B.op("vector", lambda e: e.tensor_scalar(
                    gns[:, 16:24], gns[:, 0:8], 1.0 / GN_N, None, op0=OP.mult))
                B.op("vector", lambda e: e.drain())
                B.op("vector", lambda e: e.scalar_tensor_tensor(
                    gns[:, 24:32], gns[:, 8:16], 1.0 / GN_N, gns[:, 48:56],
                    op0=OP.mult, op1=OP.subtract), [(ve, 1)])
                B.wait("scalar", ve, ve.n)
                B.op("scalar", lambda e: e.activation(
                    gns[:, 32:40], gns[:, 24:32], AF.Sqrt, bias=np.float32(EPS)), [(se, 1)])
                B.wait("vector", se, se.n)
                B.op("vector", lambda e: e.reciprocal(gns[:, 40:48], gns[:, 32:40]))
                B.op("vector", lambda e: e.drain())
                B.op("vector", lambda e: e.tensor_tensor(
                    gns[:, 48:56], gns[:, 40:48], gnw_sb[:], op=OP.mult))
                B.op("vector", lambda e: e.scalar_tensor_tensor(
                    gns[:, 56:64], gns[:, 16:24], -1.0, gns[:, 40:48],
                    op0=OP.mult, op1=OP.mult))
                B.op("vector", lambda e: e.drain())
                B.op("vector", lambda e: e.scalar_tensor_tensor(
                    gns[:, 16:24], gns[:, 56:64], 1.0, gnw_sb[:],
                    op0=OP.mult, op1=OP.mult))
                B.op("vector", lambda e: e.drain())
                B.op("vector", lambda e: e.tensor_tensor(
                    gns[:, 24:32], gns[:, 16:24], gnb_sb[:], op=OP.add), [(ve, 1)])
                gn_ve = ve.n

                po = [P[:, 0:1024], P[:, 1024:2048]]
                store_done = {}
                wo_done = {}
                resv = {}
                for t in range(NT):
                    p = t % 2
                    ld_x = ld_xp[p]
                    st_s = st_sp[p]
                    xin_p = xin[:, p * 2 * D:(p + 1) * 2 * D]
                    if t >= 2:
                        B.wait("sync", st_s, store_done[p])
                    for _c in range(2):
                        B.op("sync", lambda e, _t=t, _x=xin_p, _cc=_c: e.dma_start(
                            out=_x[:, _cc * D:(_cc + 1) * D], in_=tile_half(x1s, _t, _cc)), [(ld_x, 16)])
                    x_ld = ld_x.n
                    B.wait("scalar", ve, gn_ve)
                    if t >= 1:
                        B.wait("scalar", pe, wo_done[t - 1])
                    for h in range(NH):
                        B.op("scalar", lambda e, _t=t, _h=h: e.activation(
                            ynb[:, _h * T:(_h + 1) * T],
                            ysb[:, _h * L + _t * T:_h * L + (_t + 1) * T],
                            AF.Identity, scale=gns[:, 48 + _h:49 + _h],
                            bias=gns[:, 24 + _h:25 + _h]), [(se, 1)])
                    yn_se = se.n
                    B.wait("tensor", se, yn_se)
                    if t >= 2:
                        B.wait("tensor", ve, resv[t - 2])
                    for c in range(2):
                        if c == 1:
                            B.wait("tensor", ve, ve.n)
                        for eh in range(NH):
                            B.op("tensor", lambda e, _p=p, _c=c, _e=eh: e.matmul(
                                po[_p][:, 0:512], ynb[:, _e * T + _c * 128:_e * T + (_c + 1) * 128],
                                wo_sb[:, _e * D + 0:_e * D + 512],
                                start=(_e == 0), stop=(_e == 7)))
                            B.op("tensor", lambda e, _p=p, _c=c, _e=eh: e.matmul(
                                po[_p][:, 512:1024], ynb[:, _e * T + _c * 128:_e * T + (_c + 1) * 128],
                                wo_sb[:, _e * D + 512:_e * D + 1024],
                                start=(_e == 0), stop=(_e == 7)),
                                [(pe, 1)] if eh == 7 else ())
                        wope = pe.n
                        B.wait("vector", pe, wope)
                        B.wait("vector", ld_x, x_ld)
                        B.op("vector", lambda e, _c=c, _x=xin_p, _p=p: e.scalar_tensor_tensor(
                            _x[:, _c * D:(_c + 1) * D], po[_p], 1.0, _x[:, _c * D:(_c + 1) * D],
                            op0=OP.mult, op1=OP.add), [(ve, 1)])
                    wo_done[t] = pe.n
                    resv[t] = ve.n
                    res_ve = ve.n
                    B.wait("sync", ve, res_ve)
                    for _c in range(2):
                        B.op("sync", lambda e, _t=t, _x=xin_p, _cc=_c: e.dma_start(
                            out=tile_half(x2s, _t, _cc), in_=_x[:, _cc * D:(_cc + 1) * D]), [(st_s, 16)])
                    store_done[p] = st_s.n
            barrier()
            build_lnpass(x2s, xn2, "2")
            barrier()

        if n_phases >= 4:
            w3scope = ExitStack()
            wg3 = w3scope.enter_context(nc.sbuf_tensor("wg3_sb", [128, 8 * HF], BF16))
            wu3 = w3scope.enter_context(nc.sbuf_tensor("wu3_sb", [128, 8 * HF], BF16))
            wd3 = w3scope.enter_context(nc.sbuf_tensor("wd3_sb", [128, 24 * D], BF16))
            wst3 = w3scope.enter_context(nc.sbuf_tensor("wstage3", [128, HF], I32))
            w3_done = ffn_weight_prologue(wg3, wu3, wd3, wst3)
            build_ffn(xn2, x2s, out_hbm, None, wg3, wu3, wd3, w3_done, final=True, tag="3")
            w3scope.close()

        B.wait("sync", st_sp[0], st_sp[0].n)
        B.wait("sync", st_sp[1], st_sp[1].n)
        build_nc.last_sched = B.snapshot()
        B.flush_block()

    return nc


_NC_CACHE = {}


def _get_nc(n_phases=4):
    if n_phases not in _NC_CACHE:
        _NC_CACHE[n_phases] = build_nc(n_phases)
    return _NC_CACHE[n_phases]


def _host_constants():
    gam = _gammas()
    ident = np.eye(128, dtype=BF)
    j = np.arange(128)[:, None]
    i = np.arange(128)[None, :]
    maskc = np.zeros((128, NH * 128), dtype=np.float64)
    for h in range(NH):
        maskc[:, h * 128:(h + 1) * 128] = np.where(i >= j, gam[h] ** (-128.0), 0.0)
    qdec = np.zeros((128, NH * T), dtype=np.float64)
    ii = np.arange(T) % 128
    for h in range(NH):
        qdec[:, h * T:(h + 1) * T] = (gam[h] ** (ii + 1.0))[None, :]
    kdec = np.zeros((128, D), dtype=np.float64)
    jj = np.arange(128)
    for h in range(NH):
        kdec[:, h * 128:(h + 1) * 128] = (gam[h] ** (127.0 - jj))[:, None]
    return ident, maskc.astype(BF), qdec.astype(BF), kdec.astype(BF)


def kernel(L0_attn_x, L0_attn_Wq, L0_attn_Wk, L0_attn_Wv, L0_attn_Wo,
           L0_attn_gn_weight, L0_attn_gn_bias,
           L0_ffn_w_gate, L0_ffn_w_up, L0_ffn_w_down, _n_phases=4):
    x = np.ascontiguousarray(np.asarray(L0_attn_x, dtype=np.float32))
    wq = np.ascontiguousarray(np.asarray(L0_attn_Wq, dtype=np.float32).T)
    wk = np.ascontiguousarray(np.asarray(L0_attn_Wk, dtype=np.float32).T)
    wv = np.ascontiguousarray(np.asarray(L0_attn_Wv, dtype=np.float32).T)
    wo = np.ascontiguousarray(np.asarray(L0_attn_Wo, dtype=np.float32).T)
    wg = np.ascontiguousarray(np.asarray(L0_ffn_w_gate, dtype=np.int32).T)
    wu = np.ascontiguousarray(np.asarray(L0_ffn_w_up, dtype=np.int32).T)
    wd = np.ascontiguousarray(np.asarray(L0_ffn_w_down, dtype=np.int32).T)
    gnw = np.ascontiguousarray(np.asarray(L0_attn_gn_weight, dtype=np.float32).reshape(NH, 128).T)
    gnb = np.ascontiguousarray(np.asarray(L0_attn_gn_bias, dtype=np.float32).reshape(NH, 128).T)
    ident, maskc, qdec, kdec = _host_constants()

    nc = _get_nc(_n_phases)
    shared = {"wqT": wq, "wkT": wk, "wvT": wv, "woT": wo,
              "wgT": wg, "wuT": wu, "wdT": wd,
              "gnw": gnw, "gnb": gnb, "ident": ident,
              "maskc": maskc, "qdec": qdec, "kdec": kdec}
    in_maps = [dict(shared, x=np.ascontiguousarray(x[b])) for b in range(8)]
    res = run_bass_kernel_spmd(nc, in_maps, core_ids=list(range(8)))
    out = np.stack([res.results[b]["out"] for b in range(8)], axis=0)
    return out


# version 08
# speedup vs baseline: 1.4877x; 1.0295x over previous
import numpy as np
import math
from contextlib import ExitStack
ExitSt = ExitStack
import ml_dtypes
import concourse.bass as bass
import concourse.mybir as mybir
from concourse.bass_utils import run_bass_kernel_spmd

F32 = mybir.dt.float32
BF16 = mybir.dt.bfloat16
I32 = mybir.dt.int32
OP = mybir.AluOpType
AF = mybir.ActivationFunctionType
BF = ml_dtypes.bfloat16

L = 4096
D = 1024
HF = 3072
NH = 8
DH = 128
T = 256
NT = L // T
FFN_TILES = NT
ATTN_TILES = NT
EPS = 1e-5
GN_N = float(L * DH)

ENGINES = ("sync", "tensor", "vector", "scalar", "gpsimd")


def _gammas():
    return 1.0 - np.exp(np.linspace(math.log(1.0 / 32.0), math.log(1.0 / 512.0), NH))


class Sem:
    __slots__ = ("h", "n", "name")


class Sched:
    def __init__(self, nc, stack):
        self.nc = nc
        self.stack = stack
        self.q = {e: [] for e in ENGINES}

    def sem(self, name):
        s = Sem()
        s.h = self.stack.enter_context(self.nc.semaphore(name))
        s.n = 0
        s.name = name
        return s

    def wait(self, eng, sem, val=None):
        v = sem.n if val is None else val
        if v <= 0:
            return
        f = lambda e, _h=sem.h, _v=v: e.wait_ge(_h, _v)
        f.meta = ("wait", sem.name, v)
        self.q[eng].append(f)

    def op(self, eng, fn, incs=()):
        incs = tuple(incs)

        def run(e, _fn=fn, _incs=incs):
            ins = _fn(e)
            for s, k in _incs:
                ins.then_inc(s.h, k)

        run.meta = ("op", tuple((s.name, k) for s, k in incs), getattr(fn, "__qualname__", ""))
        self.q[eng].append(run)
        for s, k in incs:
            s.n += k

    def snapshot(self):
        return {e: list(self.q[e]) for e in ENGINES}

    def flush_block(self):
        with self.nc.Block() as blk:
            for eng in ENGINES:
                ops = self.q[eng]

                def body(e, _ops=ops):
                    for f in _ops:
                        f(e)

                getattr(blk, eng)(body)
        self.q = {e: [] for e in ENGINES}


def build_nc(n_phases=4):
    nc = bass.Bass("TRN2", target_bir_lowering=False)

    x_in = nc.dram_tensor("x", [L, D], F32, kind="ExternalInput").ap()
    wqT = nc.dram_tensor("wqT", [D, D], F32, kind="ExternalInput").ap()
    wkT = nc.dram_tensor("wkT", [D, D], F32, kind="ExternalInput").ap()
    wvT = nc.dram_tensor("wvT", [D, D], F32, kind="ExternalInput").ap()
    woT = nc.dram_tensor("woT", [D, D], F32, kind="ExternalInput").ap()
    wgT = nc.dram_tensor("wgT", [D, HF], I32, kind="ExternalInput").ap()
    wuT = nc.dram_tensor("wuT", [D, HF], I32, kind="ExternalInput").ap()
    wdT = nc.dram_tensor("wdT", [HF, D], I32, kind="ExternalInput").ap()
    gnw_in = nc.dram_tensor("gnw", [128, NH], F32, kind="ExternalInput").ap()
    gnb_in = nc.dram_tensor("gnb", [128, NH], F32, kind="ExternalInput").ap()
    ident_in = nc.dram_tensor("ident", [128, 128], BF16, kind="ExternalInput").ap()
    mask_in = nc.dram_tensor("maskc", [128, NH * 128], BF16, kind="ExternalInput").ap()
    qdec_in = nc.dram_tensor("qdec", [128, NH * T], BF16, kind="ExternalInput").ap()
    kdec_in = nc.dram_tensor("kdec", [128, D], BF16, kind="ExternalInput").ap()
    out_hbm = nc.dram_tensor("out", [L, D], F32, kind="ExternalOutput").ap()

    xn0 = nc.dram_tensor("xn0_scr", [L, D], BF16, kind="Internal").ap()
    xn1 = nc.dram_tensor("xn1_scr", [L, D], BF16, kind="Internal").ap()
    xn2 = nc.dram_tensor("xn2_scr", [L, D], BF16, kind="Internal").ap()
    x1s = nc.dram_tensor("x1_scr", [L, D], F32, kind="Internal").ap()
    x2s = nc.dram_tensor("x2_scr", [L, D], F32, kind="Internal").ap()

    gam = _gammas()
    gam128 = [float(g) ** 128 for g in gam]

    def tile_half(ap, t, c):
        return ap[t * T + c * 128:t * T + (c + 1) * 128, :]

    with ExitStack() as ES:
        B = Sched(nc, ES)
        ld_xp = [B.sem("ld_x0"), B.sem("ld_x1")]
        ld_tp = [B.sem("ld_t0"), B.sem("ld_t1")]
        ld_c = B.sem("ld_c")
        ld_w = B.sem("ld_w")
        st_sp = [B.sem("st_s0"), B.sem("st_s1")]
        pe = B.sem("pe")
        ve = B.sem("ve")
        se = B.sem("se")
        ge = B.sem("ge")
        bar = B.sem("bar")

        junk = ES.enter_context(nc.sbuf_tensor("junkbuf", [128, 2 * D], BF16))
        junk_v = junk[:, 0:D]
        junk_v1 = junk_v
        junk_s = junk[:, D:2 * D]
        junk_s1 = junk_s
        junk_last = {}

        def junk_guard(eng, key, sem):
            if key in junk_last:
                s_, v_ = junk_last[key]
                B.wait(eng, s_, v_)

        def junk_mark(key, sem):
            junk_last[key] = (sem, sem.n)
        ident_sb = ES.enter_context(nc.sbuf_tensor("ident_sb", [128, 128], BF16))
        ones_sb = ES.enter_context(nc.sbuf_tensor("ones_sb", [128, 128], F32))
        psum = ES.enter_context(nc.psum_tensor("psum_all", [128, 4096], F32))
        P = psum.ap()
        P_bf = P.bitcast(BF16)

        nbar = [0]

        def barrier():
            for s_ in (ld_xp[0], ld_xp[1], ld_tp[0], ld_tp[1], ld_c, st_sp[0], st_sp[1]):
                B.wait("sync", s_, s_.n)
            B.wait("gpsimd", ld_w, ld_w.n)
            for eng in ENGINES:
                if eng == "sync":
                    B.op(eng, lambda e: e.nop(), [(bar, 1)])
                else:
                    B.op(eng, lambda e: e.drain(), [(bar, 1)])
            nbar[0] += 1
            for eng in ENGINES:
                B.wait(eng, bar, 5 * nbar[0])

        B.op("sync", lambda e: e.dma_start(out=ident_sb[:], in_=ident_in), [(ld_c, 16)])
        B.op("vector", lambda e: e.memset(ones_sb[:], 1.0), [(ve, 1)])

        def ln_finalize_apply(stats, x_tile, xn_tile, sumsq_ready_se):
            B.wait("vector", se, sumsq_ready_se)
            B.op("vector", lambda e, _s=stats: e.drain())
            B.op("vector", lambda e, _s=stats: e.scalar_tensor_tensor(
                _s[:, 4:6], _s[:, 0:2], 1.0 / (D * D), _s[:, 0:2],
                op0=OP.mult, op1=OP.mult))
            B.op("vector", lambda e, _s=stats: e.tensor_scalar(
                _s[:, 6:8], _s[:, 0:2], 1.0 / D, None, op0=OP.mult))
            B.op("vector", lambda e: e.drain())
            B.op("vector", lambda e, _s=stats: e.scalar_tensor_tensor(
                _s[:, 8:10], _s[:, 2:4], 1.0 / D, _s[:, 4:6],
                op0=OP.mult, op1=OP.subtract), [(ve, 1)])
            B.wait("scalar", ve, ve.n)
            B.op("scalar", lambda e, _s=stats: e.activation(
                _s[:, 10:12], _s[:, 8:10], AF.Sqrt, bias=np.float32(EPS)), [(se, 1)])
            B.wait("vector", se, se.n)
            B.op("vector", lambda e, _s=stats: e.reciprocal(_s[:, 12:14], _s[:, 10:12]))
            B.op("vector", lambda e: e.drain())
            B.op("vector", lambda e, _s=stats: e.scalar_tensor_tensor(
                _s[:, 14:16], _s[:, 6:8], -1.0, _s[:, 12:14],
                op0=OP.mult, op1=OP.mult), [(ve, 1)])
            B.wait("scalar", ve, ve.n)
            for c in range(2):
                B.op("scalar", lambda e, _s=stats, _x=x_tile, _o=xn_tile, _c=c: e.activation(
                    _o[:, _c * D:(_c + 1) * D], _x[:, _c * D:(_c + 1) * D],
                    AF.Identity, bias=_s[:, 14 + _c:15 + _c],
                    scale=_s[:, 12 + _c:13 + _c]), [(se, 1)])
            return se.n

        def ffn_weight_prologue(wg_sb, wu_sb, wd_sb, wstage):
            for wdram, wsb, sc in ((wgT, wg_sb, 1.0 / 127.0), (wuT, wu_sb, 1.0 / 127.0)):
                for d in range(8):
                    B.wait("gpsimd", ge, ge.n)
                    B.op("gpsimd", lambda e, _w=wdram, _d=d, _st=wstage: e.dma_start(
                        out=_st[:], in_=_w[_d * 128:(_d + 1) * 128, :]), [(ld_w, 16)])
                    B.wait("vector", ld_w, ld_w.n)
                    B.op("vector", lambda e, _sb=wsb, _d=d, _st=wstage, _sc=sc: e.tensor_scalar(
                        _sb[:, _d * HF:(_d + 1) * HF], _st[:], _sc, None,
                        op0=OP.mult), [(ge, 1)])
            for h in range(24):
                B.wait("gpsimd", ge, ge.n)
                B.op("gpsimd", lambda e, _h=h, _st=wstage: e.dma_start(
                    out=_st[:, 0:D], in_=wdT[_h * 128:(_h + 1) * 128, :]), [(ld_w, 16)])
                B.wait("vector", ld_w, ld_w.n)
                B.op("vector", lambda e, _h=h, _st=wstage: e.tensor_scalar(
                    wd_sb[:, _h * D:(_h + 1) * D], _st[:, 0:D], 0.5 / 127.0, None,
                    op0=OP.mult), [(ge, 1)])
            return ge.n

        w1scope = ExitStack()
        w1_done = 0
        if n_phases >= 2:
            wg1 = w1scope.enter_context(nc.sbuf_tensor("wg_sb", [128, 8 * HF], BF16))
            wu1 = w1scope.enter_context(nc.sbuf_tensor("wu_sb", [128, 8 * HF], BF16))
            wd1 = w1scope.enter_context(nc.sbuf_tensor("wd_sb", [128, 24 * D], BF16))
            wst1 = w1scope.enter_context(nc.sbuf_tensor("wstage", [128, HF], I32))
            w1_done = ffn_weight_prologue(wg1, wu1, wd1, wst1)

        def build_lnpass(src_dram, xn_dram, tag):
            G = 4
            NCH = 2 * G
            with nc.sbuf_tensor("pp_xin" + tag, [128, NCH * D], F32) as xin, \
                 nc.sbuf_tensor("pp_stats" + tag, [128, 64], F32) as stats:
                xin_bf = xin[:].bitcast(BF16)
                st_s = st_sp[0]
                ld_x = ld_xp[0]
                store_done = 0
                for g in range(NT // G):
                    if g >= 1:
                        B.wait("sync", st_s, store_done)
                    for cc in range(NCH):
                        B.op("sync", lambda e, _g=g, _cc=cc: e.dma_start(
                            out=xin[:, _cc * D:(_cc + 1) * D],
                            in_=tile_half(src_dram, _g * G + _cc // 2, _cc % 2)), [(ld_x, 16)])
                    x_ld = ld_x.n
                    B.wait("vector", ld_x, x_ld)
                    B.wait("scalar", ld_x, x_ld)
                    if g >= 1:
                        B.wait("vector", st_s, store_done)
                        B.wait("scalar", st_s, store_done)
                    for cc in range(NCH):
                        junk_guard("vector", ("v", 0), ve)
                        B.op("vector", lambda e, _cc=cc: e.tensor_scalar(
                            junk_v[:], xin[:, _cc * D:(_cc + 1) * D], 1.0, None, op0=OP.mult,
                            op1=OP.add, accum_out=stats[:, _cc:_cc + 1]), [(ve, 1)])
                        junk_mark(("v", 0), ve)
                        B.op("vector", lambda e: e.drain())
                        junk_guard("scalar", ("s", 0), se)
                        B.op("scalar", lambda e, _cc=cc: e.activation(
                            junk_s[:], xin[:, _cc * D:(_cc + 1) * D], AF.Square,
                            accum_out=stats[:, 8 + _cc:9 + _cc]), [(se, 1)])
                        junk_mark(("s", 0), se)
                    B.wait("vector", se, se.n)
                    B.op("vector", lambda e: e.drain())
                    B.op("vector", lambda e: e.scalar_tensor_tensor(
                        stats[:, 16:24], stats[:, 0:8], 1.0 / (D * D), stats[:, 0:8],
                        op0=OP.mult, op1=OP.mult))
                    B.op("vector", lambda e: e.tensor_scalar(
                        stats[:, 24:32], stats[:, 0:8], 1.0 / D, None, op0=OP.mult))
                    B.op("vector", lambda e: e.drain())
                    B.op("vector", lambda e: e.scalar_tensor_tensor(
                        stats[:, 32:40], stats[:, 8:16], 1.0 / D, stats[:, 16:24],
                        op0=OP.mult, op1=OP.subtract), [(ve, 1)])
                    B.wait("scalar", ve, ve.n)
                    B.op("scalar", lambda e: e.activation(
                        stats[:, 40:48], stats[:, 32:40], AF.Sqrt, bias=np.float32(EPS)), [(se, 1)])
                    B.wait("vector", se, se.n)
                    B.op("vector", lambda e: e.reciprocal(stats[:, 48:56], stats[:, 40:48]))
                    B.op("vector", lambda e: e.drain())
                    B.op("vector", lambda e: e.scalar_tensor_tensor(
                        stats[:, 56:64], stats[:, 24:32], -1.0, stats[:, 48:56],
                        op0=OP.mult, op1=OP.mult), [(ve, 1)])
                    B.wait("scalar", ve, ve.n)
                    for cc in range(NCH):
                        B.op("scalar", lambda e, _cc=cc: e.activation(
                            xin_bf[:, _cc * 2048:_cc * 2048 + D], xin[:, _cc * D:(_cc + 1) * D],
                            AF.Identity, bias=stats[:, 56 + _cc:57 + _cc],
                            scale=stats[:, 48 + _cc:49 + _cc]), [(se, 1)])
                    apply_se = se.n
                    B.wait("sync", se, apply_se)
                    for cc in range(NCH):
                        B.op("sync", lambda e, _g=g, _cc=cc: e.dma_start(
                            out=tile_half(xn_dram, _g * G + _cc // 2, _cc % 2),
                            in_=xin_bf[:, _cc * 2048:_cc * 2048 + D]), [(st_s, 16)])
                    store_done = st_s.n
            return st_s, store_done

        if n_phases >= 1:
            build_lnpass(x_in, xn0, "0")
            barrier()

        def build_ffn(xn_src, x_src, x_dst, xn_dst, wg_sb, wu_sb, wd_sb,
                      weights_ready_ge, final, tag="", xn_ready=None):
            with nc.sbuf_tensor("f_xnf" + tag, [128, 2 * 8 * T], BF16) as xnf, \
                 nc.sbuf_tensor("f_xin" + tag, [128, 2 * 2 * D], F32) as xin, \
                 nc.sbuf_tensor("f_hid" + tag, [128, 24 * T], BF16) as hid, \
                 nc.sbuf_tensor("f_gsb" + tag, [128, 2 * T], F32) as gsb:
                pg = [P[:, 0:T], P[:, 3072:3072 + T]]
                pu = [P[:, 512:512 + T], P[:, 3584:3584 + T]]
                pd = [P[:, 1024:2048], P[:, 2048:3072]]
                silu_done = {}
                hid_done = {}
                res_done = [0]
                store_done = {}
                xnf_used = {}
                B.wait("tensor", ge, weights_ready_ge)
                B.wait("sync", ge, weights_ready_ge)
                if xn_ready is not None:
                    B.wait("sync", xn_ready[0], xn_ready[1])
                    B.wait("scalar", xn_ready[0], xn_ready[1])
                for t in range(FFN_TILES):
                    p = t % 2
                    ld_x = ld_xp[p]
                    ld_t = ld_tp[p]
                    st_out = st_sp[p]
                    xin_p = xin[:, p * 2 * D:(p + 1) * 2 * D]
                    xnf_p = [xnf[:, (p * 8 + d) * T:(p * 8 + d + 1) * T] for d in range(8)]
                    if t >= 2:
                        B.wait("sync", st_out, store_done[p])
                        B.wait("sync", pe, xnf_used[p])
                    for d in range(8):
                        B.op("sync", lambda e, _t=t, _d=d, _o=xnf_p[d]: e.dma_start_transpose(
                            out=_o, in_=xn_src[_t * T:(_t + 1) * T, _d * 128:(_d + 1) * 128]),
                            [(ld_t, 16)])
                    xnf_ld = ld_t.n
                    for _c in range(2):
                        B.op("sync", lambda e, _t=t, _x=xin_p, _cc=_c: e.dma_start(
                            out=_x[:, _cc * D:(_cc + 1) * D], in_=tile_half(x_src, _t, _cc)), [(ld_x, 16)])
                    x_ld = ld_x.n

                    B.wait("tensor", ld_t, xnf_ld)
                    if t >= 1:
                        B.wait("tensor", ve, res_done[0])

                    def down_mms(ha, hp):
                        B.wait("tensor", ve, hid_done[ha])
                        for c in range(2):
                            for half in range(2):
                                last = (hp == 23 and c == 1 and half == 1)
                                B.op("tensor", lambda e, _h=hp, _c=c, _f=half: e.matmul(
                                    pd[_c][:, _f * 512:(_f + 1) * 512],
                                    hid[:, _h * T + _c * 128:_h * T + (_c + 1) * 128],
                                    wd_sb[:, _h * D + _f * 512:_h * D + (_f + 1) * 512],
                                    start=(_h == 0), stop=(_h == 23)),
                                    [(pe, 1)] if last else ())

                    for h in range(24):
                        ha = t * 24 + h
                        bk = h % 2
                        if ha >= 2:
                            B.wait("tensor", se, silu_done[ha - 2])
                        for d in range(8):
                            B.op("tensor", lambda e, _h=h, _d=d, _b=bk, _xn=xnf_p[d]: e.matmul(
                                pg[_b], wg_sb[:, _d * HF + _h * 128:_d * HF + (_h + 1) * 128],
                                _xn, start=(_d == 0), stop=(_d == 7)),
                                [(pe, 1)] if d == 7 else ())
                        gate_pe = pe.n
                        if ha >= 2:
                            B.wait("tensor", ve, hid_done[ha - 2])
                        for d in range(8):
                            B.op("tensor", lambda e, _h=h, _d=d, _b=bk, _xn=xnf_p[d]: e.matmul(
                                pu[_b], wu_sb[:, _d * HF + _h * 128:_d * HF + (_h + 1) * 128],
                                _xn, start=(_d == 0), stop=(_d == 7)),
                                [(pe, 1)] if d == 7 else ())
                        up_pe = pe.n
                        if h >= 1:
                            down_mms(ha - 1, h - 1)
                        B.wait("scalar", pe, gate_pe)
                        if ha >= 2:
                            B.wait("scalar", ve, hid_done[ha - 2])
                        B.op("scalar", lambda e, _b=bk: e.activation(
                            gsb[:, _b * T:(_b + 1) * T], pg[_b], AF.Silu), [(se, 1)])
                        silu_done[ha] = se.n
                        B.wait("vector", se, silu_done[ha])
                        B.wait("vector", pe, up_pe)
                        B.op("vector", lambda e, _h=h, _b=bk: e.tensor_tensor(
                            hid[:, _h * T:(_h + 1) * T], pu[_b], gsb[:, _b * T:(_b + 1) * T],
                            op=OP.mult), [(ve, 1)])
                        hid_done[ha] = ve.n
                    down_mms(t * 24 + 23, 23)
                    down_pe = pe.n
                    xnf_used[p] = pe.n
                    B.wait("vector", pe, down_pe)
                    B.wait("vector", ld_x, x_ld)
                    for c in range(2):
                        B.op("vector", lambda e, _c=c, _x=xin_p: e.scalar_tensor_tensor(
                            _x[:, _c * D:(_c + 1) * D], pd[_c], 1.0, _x[:, _c * D:(_c + 1) * D],
                            op0=OP.mult, op1=OP.add),
                            [(ve, 1)])
                    res_done[0] = ve.n
                    res_ve = ve.n
                    B.wait("sync", ve, res_ve)
                    for _c in range(2):
                        B.op("sync", lambda e, _t=t, _x=xin_p, _cc=_c: e.dma_start(
                            out=tile_half(x_dst, _t, _cc), in_=_x[:, _cc * D:(_cc + 1) * D]), [(st_out, 16)])
                    store_done[p] = st_out.n

        if n_phases >= 2:
            build_ffn(xn0, x_in, x1s, None, wg1, wu1, wd1, w1_done,
                      final=(n_phases == 2))
            barrier()
            if n_phases >= 3:
                build_lnpass(x1s, xn1, "1")
                barrier()
        w1scope.close()

        if n_phases >= 3:
            with ExitSt() as AS:
                al = AS.enter_context
                wq_sb = al(nc.sbuf_tensor("a_wq", [128, 8 * D], BF16))
                wk_sb = al(nc.sbuf_tensor("a_wk", [128, 8 * D], BF16))
                wv_sb = al(nc.sbuf_tensor("a_wv", [128, 8 * D], BF16))
                wo_sb = al(nc.sbuf_tensor("a_wo", [128, 8 * D], BF16))
                awst = al(nc.sbuf_tensor("a_wstage", [128, D], F32))
                ysb = al(nc.sbuf_tensor("a_ys", [128, NH * L], BF16))
                xnf = al(nc.sbuf_tensor("a_xnf", [128, 2 * 8 * T], BF16))
                qt = al(nc.sbuf_tensor("a_qt", [128, NH * T], BF16))
                ktm = al(nc.sbuf_tensor("a_ktm", [128, 2 * D], BF16))
                kfm = al(nc.sbuf_tensor("a_kfm", [128, 2 * D], BF16))
                vtm = al(nc.sbuf_tensor("a_vtm", [128, 2 * D], BF16))
                adb = al(nc.sbuf_tensor("a_ad", [128, 2 * 128], BF16))
                s0f = al(nc.sbuf_tensor("a_s0f", [128, NH * DH], F32))
                s0b = al(nc.sbuf_tensor("a_s0b", [128, NH * DH], BF16))
                mask_sb = al(nc.sbuf_tensor("a_mask", [128, NH * 128], BF16))
                qdec_sb = al(nc.sbuf_tensor("a_qdec", [128, NH * T], BF16))
                kdec_sb = al(nc.sbuf_tensor("a_kdec", [128, D], BF16))
                ysum = al(nc.sbuf_tensor("a_ysum", [128, NH * 32], F32))
                ysq = al(nc.sbuf_tensor("a_ysq", [128, NH * 32], F32))
                gnw_sb = al(nc.sbuf_tensor("a_gnw", [128, NH], F32))
                gnb_sb = al(nc.sbuf_tensor("a_gnb", [128, NH], F32))
                gns = al(nc.sbuf_tensor("a_gns", [128, 64], F32))
                ynb = al(nc.sbuf_tensor("a_yn", [128, NH * T], BF16))
                xin = al(nc.sbuf_tensor("a_xin", [128, 2 * 2 * D], F32))

                B.op("sync", lambda e: e.dma_start(out=mask_sb[:], in_=mask_in), [(ld_c, 16)])
                B.op("sync", lambda e: e.dma_start(out=qdec_sb[:], in_=qdec_in), [(ld_c, 16)])
                B.op("sync", lambda e: e.dma_start(out=kdec_sb[:], in_=kdec_in), [(ld_c, 16)])
                B.op("sync", lambda e: e.dma_start(out=gnw_sb[:], in_=gnw_in), [(ld_c, 16)])
                B.op("sync", lambda e: e.dma_start(out=gnb_sb[:], in_=gnb_in), [(ld_c, 16)])
                const_ld = ld_c.n
                for wdram, wsb in ((wqT, wq_sb), (wkT, wk_sb), (wvT, wv_sb), (woT, wo_sb)):
                    for d in range(8):
                        B.wait("gpsimd", ge, ge.n)
                        B.op("gpsimd", lambda e, _w=wdram, _d=d: e.dma_start(
                            out=awst[:], in_=_w[_d * 128:(_d + 1) * 128, :]), [(ld_w, 16)])
                        B.wait("vector", ld_w, ld_w.n)
                        B.op("vector", lambda e, _sb=wsb, _d=d: e.tensor_copy(
                            _sb[:, _d * D:(_d + 1) * D], awst[:]), [(ge, 1)])
                aw_done = ge.n
                B.op("vector", lambda e: e.memset(s0f[:], 0.0))
                B.op("vector", lambda e: e.memset(s0b[:], 0.0), [(ve, 1)])
                s0_init = ve.n
                B.wait("vector", ld_c, const_ld)

                paq = [P[:, 0:T], P[:, 512:512 + T]]
                pa = [P[:, 0:128], P[:, 512:640]]
                pk = P[:, 1024:2048]
                py = [P[:, 1024:1152], P[:, 1536:1664]]
                pv = P[:, 2048:3072]
                pkv = [P[:, 2048:2176], P[:, 2560:2688]]
                ptr = [P_bf[:, 6144:7168], P_bf[:, 7168:8192]]

                B.wait("tensor", ge, aw_done)
                B.wait("tensor", ve, s0_init)

                q_evac = {}
                ktm_evac = {}
                kfm_evac = {}
                v_evac = {}
                ad_evac = {}
                supd = {}
                s0cast = {}
                a_done = {}
                y_done = {}
                kv_done = {}
                ysq_done = {}
                xnf_used = {}

                for t in range(NT):
                    p = t % 2
                    ld_t = ld_tp[p]
                    xnf_p = [xnf[:, (p * 8 + d) * T:(p * 8 + d + 1) * T] for d in range(8)]
                    if t >= 2:
                        B.wait("sync", pe, xnf_used[p])
                    for d in range(8):
                        B.op("sync", lambda e, _t=t, _d=d, _o=xnf_p[d]: e.dma_start_transpose(
                            out=_o, in_=xn1[_t * T:(_t + 1) * T, _d * 128:(_d + 1) * 128]),
                            [(ld_t, 16)])
                    xnf_ld = ld_t.n
                    B.wait("tensor", ld_t, xnf_ld)
                    for h in range(NH):
                        bk = h % 2
                        if t >= 1:
                            B.wait("tensor", ve, ad_evac[(2 * t - 1, 6 + bk)])
                        if h >= 2:
                            B.wait("tensor", ve, q_evac[(t, h - 2)])
                        for d in range(8):
                            B.op("tensor", lambda e, _h=h, _d=d, _b=bk, _xn=xnf_p[d]: e.matmul(
                                paq[_b], wq_sb[:, _d * D + _h * 128:_d * D + (_h + 1) * 128],
                                _xn, start=(_d == 0), stop=(_d == 7)),
                                [(pe, 1)] if d == 7 else ())
                        qpe = pe.n
                        B.wait("vector", pe, qpe)
                        if t >= 1:
                            B.wait("vector", pe, y_done[(2 * t - 1, h)])
                        B.op("vector", lambda e, _h=h, _b=bk: e.tensor_tensor(
                            qt[:, _h * T:(_h + 1) * T], paq[_b], qdec_sb[:, _h * T:(_h + 1) * T],
                            op=OP.mult), [(ve, 1)])
                        q_evac[(t, h)] = ve.n
                    for c in range(2):
                        gc = 2 * t + c
                        if c == 0:
                            if gc >= 1:
                                B.wait("tensor", se, ysq_done[(gc - 1, 7)])
                        else:
                            B.wait("tensor", ve, ktm_evac[gc - 1])
                        for half in range(2):
                            for d in range(8):
                                B.op("tensor", lambda e, _c=c, _f=half, _d=d, _xn=xnf_p[d]: e.matmul(
                                    pk[:, _f * 512:(_f + 1) * 512],
                                    _xn[:, _c * 128:(_c + 1) * 128],
                                    wk_sb[:, _d * D + _f * 512:_d * D + (_f + 1) * 512],
                                    start=(_d == 0), stop=(_d == 7)),
                                    [(pe, 1)] if (d == 7 and half == 1) else ())
                        kpe = pe.n
                        B.wait("vector", pe, kpe)
                        if gc >= 2:
                            B.wait("vector", pe, kv_done[(gc - 2, 7)])
                        B.op("vector", lambda e, _c=c: e.tensor_tensor(
                            ktm[:, _c * D:(_c + 1) * D], pk[:], kdec_sb[:], op=OP.mult),
                            [(ve, 1)])
                        ktm_evac[gc] = ve.n
                        if c == 0:
                            if gc >= 1:
                                B.wait("tensor", ve, supd[(gc - 1, 7)])
                        else:
                            B.wait("tensor", se, v_evac[gc - 1])
                        for half in range(2):
                            for d in range(8):
                                B.op("tensor", lambda e, _c=c, _f=half, _d=d, _xn=xnf_p[d]: e.matmul(
                                    pv[:, _f * 512:(_f + 1) * 512],
                                    _xn[:, _c * 128:(_c + 1) * 128],
                                    wv_sb[:, _d * D + _f * 512:_d * D + (_f + 1) * 512],
                                    start=(_d == 0), stop=(_d == 7)),
                                    [(pe, 1)] if (d == 7 and half == 1) else ())
                        vpe = pe.n
                        B.wait("scalar", pe, vpe)
                        if gc >= 2:
                            B.wait("scalar", pe, kv_done[(gc - 2, 7)])
                        B.op("scalar", lambda e, _c=c: e.activation(
                            vtm[:, _c * D:(_c + 1) * D], pv[:], AF.Copy), [(se, 1)])
                        v_evac[gc] = se.n
                        B.wait("tensor", ve, ktm_evac[gc])
                        if gc >= 2:
                            B.wait("tensor", ve, kfm_evac[gc - 2])
                        for h in range(NH):
                            B.op("tensor", lambda e, _c=c, _h=h: e.transpose(
                                ptr[_c][:, _h * 128:(_h + 1) * 128],
                                ktm[:, _c * D + _h * 128:_c * D + (_h + 1) * 128],
                                ident_sb[:]), [(pe, 1)] if h == NH - 1 else ())
                        trpe = pe.n
                        B.wait("vector", pe, trpe)
                        if gc >= 2:
                            B.wait("vector", pe, a_done[(gc - 2, 7)])
                        B.op("vector", lambda e, _c=c: e.tensor_copy(
                            kfm[:, _c * D:(_c + 1) * D], ptr[_c][:]), [(ve, 1)])
                        kfm_evac[gc] = ve.n
                    xnf_used[p] = pe.n
                    for c in range(2):
                        gc = 2 * t + c

                        def emit_A(h, _c=None, _gc=None):
                            _c = c if _c is None else _c
                            _gc = gc if _gc is None else _gc
                            bk = h % 2
                            if h >= 2:
                                B.wait("tensor", ve, ad_evac[(_gc, h - 2)])
                            elif _gc >= 1 and _c == 1:
                                B.wait("tensor", ve, ad_evac[(_gc - 1, 6 + bk)])
                            B.op("tensor", lambda e, _cc=_c, _h=h, _b=bk: e.matmul(
                                pa[_b], kfm[:, _cc * D + _h * 128:_cc * D + (_h + 1) * 128],
                                qt[:, _h * T + _cc * 128:_h * T + (_cc + 1) * 128],
                                start=True, stop=True), [(pe, 1)])
                            a_done[(_gc, h)] = pe.n

                        def emit_mask(h, _c=None, _gc=None):
                            _c = c if _c is None else _c
                            _gc = gc if _gc is None else _gc
                            bk = h % 2
                            B.wait("vector", pe, a_done[(_gc, h)])
                            prev_ad = (_gc, h - 2) if h >= 2 else ((_gc - 1, 6 + bk) if _gc >= 1 else None)
                            if prev_ad is not None and prev_ad in y_done:
                                B.wait("vector", pe, y_done[prev_ad])
                            B.op("vector", lambda e, _h=h, _b=bk: e.tensor_tensor(
                                adb[:, _b * 128:(_b + 1) * 128], pa[_b],
                                mask_sb[:, _h * 128:(_h + 1) * 128], op=OP.mult), [(ve, 1)])
                            ad_evac[(_gc, h)] = ve.n

                        B.wait("tensor", ve, kfm_evac[gc])
                        B.wait("tensor", se, v_evac[gc])
                        if c == 0:
                            B.wait("tensor", ve, q_evac[(t, NH - 1)])
                        emit_A(0); emit_mask(0)
                        emit_A(1); emit_mask(1)
                        for h in range(NH):
                            bk = h % 2
                            B.wait("tensor", ve, ad_evac[(gc, h)])
                            if h >= 2:
                                B.wait("tensor", se, ysq_done[(gc, h - 2)])
                            elif c == 1:
                                B.wait("tensor", se, ysq_done[(gc - 1, 6 + bk)])
                            else:
                                B.wait("tensor", ve, ktm_evac[gc + 1])
                            if gc >= 1:
                                B.wait("tensor", ve, s0cast[(gc - 1, h)])
                            B.op("tensor", lambda e, _c=c, _h=h, _b=bk: e.matmul(
                                py[_b], vtm[:, _c * D + _h * 128:_c * D + (_h + 1) * 128],
                                adb[:, _b * 128:(_b + 1) * 128], start=True, stop=False))
                            B.op("tensor", lambda e, _c=c, _h=h, _b=bk: e.matmul(
                                py[_b], s0b[:, _h * DH:(_h + 1) * DH],
                                qt[:, _h * T + _c * 128:_h * T + (_c + 1) * 128],
                                start=False, stop=True), [(pe, 1)])
                            y_done[(gc, h)] = pe.n
                            if h >= 2:
                                B.wait("tensor", ve, supd[(gc, h - 2)])
                            elif c == 1:
                                B.wait("tensor", ve, supd[(gc - 1, 6 + bk)])
                            else:
                                B.wait("tensor", se, v_evac[gc + 1])
                            B.op("tensor", lambda e, _c=c, _h=h, _b=bk: e.matmul(
                                pkv[_b], ktm[:, _c * D + _h * 128:_c * D + (_h + 1) * 128],
                                vtm[:, _c * D + _h * 128:_c * D + (_h + 1) * 128],
                                start=True, stop=True), [(pe, 1)])
                            kv_done[(gc, h)] = pe.n
                            if h + 2 < NH:
                                emit_A(h + 2); emit_mask(h + 2)
                            B.wait("scalar", pe, y_done[(gc, h)])
                            B.op("scalar", lambda e, _gc=gc, _h=h, _b=bk: e.activation(
                                ysb[:, (_h * 32 + _gc) * 128:(_h * 32 + _gc + 1) * 128],
                                py[_b], AF.Copy,
                                accum_out=ysum[:, _h * 32 + _gc:_h * 32 + _gc + 1]), [(se, 1)])
                            junk_guard("scalar", ("ysq", 0), se)
                            B.op("scalar", lambda e, _gc=gc, _h=h, _b=bk: e.activation(
                                junk_s[:, 0:128], py[_b], AF.Square,
                                accum_out=ysq[:, _h * 32 + _gc:_h * 32 + _gc + 1]), [(se, 1)])
                            junk_mark(("ysq", 0), se)
                            ysq_done[(gc, h)] = se.n
                            B.wait("vector", pe, kv_done[(gc, h)])
                            B.op("vector", lambda e, _h=h, _b=bk, _g=gam128[h]: e.scalar_tensor_tensor(
                                s0b[:, _h * DH:(_h + 1) * DH], s0f[:, _h * DH:(_h + 1) * DH],
                                _g, pkv[_b], op0=OP.mult, op1=OP.add), [(ve, 1)])
                            s0cast[(gc, h)] = ve.n
                            B.op("vector", lambda e, _h=h, _b=bk, _g=gam128[h]: e.scalar_tensor_tensor(
                                s0f[:, _h * DH:(_h + 1) * DH], s0f[:, _h * DH:(_h + 1) * DH],
                                _g, pkv[_b], op0=OP.mult, op1=OP.add), [(ve, 1)])
                            supd[(gc, h)] = ve.n

                B.wait("vector", se, ysq_done[(2 * NT - 1, NH - 1)])
                for h in range(NH):
                    junk_guard("vector", ("gn", 0), ve)
                    B.op("vector", lambda e, _h=h: e.tensor_scalar(
                        junk_v[:, 0:32], ysum[:, _h * 32:(_h + 1) * 32], 1.0, None,
                        op0=OP.mult, op1=OP.add, accum_out=gns[:, _h:_h + 1]), [(ve, 1)])
                    junk_mark(("gn", 0), ve)
                    B.op("vector", lambda e: e.drain())
                for h in range(NH):
                    junk_guard("vector", ("gn", 0), ve)
                    B.op("vector", lambda e, _h=h: e.tensor_scalar(
                        junk_v[:, 0:32], ysq[:, _h * 32:(_h + 1) * 32], 1.0, None,
                        op0=OP.mult, op1=OP.add, accum_out=gns[:, 8 + _h:9 + _h]), [(ve, 1)])
                    junk_mark(("gn", 0), ve)
                    B.op("vector", lambda e: e.drain())
                B.op("vector", lambda e: e.drain(), [(ve, 1)])
                gsum_ve = ve.n
                B.wait("tensor", ve, gsum_ve)
                B.wait("tensor", se, ysq_done[(2 * NT - 1, NH - 1)])
                B.op("tensor", lambda e: e.matmul(
                    P[0:1, 3072:3088], ones_sb[:, 0:1], gns[:, 0:16],
                    start=True, stop=True), [(pe, 1)])
                B.wait("vector", pe, pe.n)
                B.op("vector", lambda e: e.tensor_copy(gns[0:1, 16:32], P[0:1, 3072:3088]),
                     [(ve, 1)])
                B.wait("tensor", ve, ve.n)
                B.op("tensor", lambda e: e.matmul(
                    P[:, 3584:3600], ones_sb[0:1, 0:128], gns[0:1, 16:32],
                    start=True, stop=True), [(pe, 1)])
                pb = P[:, 3584:3600]
                B.wait("vector", pe, pe.n)
                B.op("vector", lambda e: e.tensor_copy(gns[:, 0:16], pb[:]))
                B.op("vector", lambda e: e.drain())
                B.op("vector", lambda e: e.scalar_tensor_tensor(
                    gns[:, 48:56], gns[:, 0:8], 1.0 / (GN_N * GN_N), gns[:, 0:8],
                    op0=OP.mult, op1=OP.mult))
                B.op("vector", lambda e: e.tensor_scalar(
                    gns[:, 16:24], gns[:, 0:8], 1.0 / GN_N, None, op0=OP.mult))
                B.op("vector", lambda e: e.drain())
                B.op("vector", lambda e: e.scalar_tensor_tensor(
                    gns[:, 24:32], gns[:, 8:16], 1.0 / GN_N, gns[:, 48:56],
                    op0=OP.mult, op1=OP.subtract), [(ve, 1)])
                B.wait("scalar", ve, ve.n)
                B.op("scalar", lambda e: e.activation(
                    gns[:, 32:40], gns[:, 24:32], AF.Sqrt, bias=np.float32(EPS)), [(se, 1)])
                B.wait("vector", se, se.n)
                B.op("vector", lambda e: e.reciprocal(gns[:, 40:48], gns[:, 32:40]))
                B.op("vector", lambda e: e.drain())
                B.op("vector", lambda e: e.tensor_tensor(
                    gns[:, 48:56], gns[:, 40:48], gnw_sb[:], op=OP.mult))
                B.op("vector", lambda e: e.scalar_tensor_tensor(
                    gns[:, 56:64], gns[:, 16:24], -1.0, gns[:, 40:48],
                    op0=OP.mult, op1=OP.mult))
                B.op("vector", lambda e: e.drain())
                B.op("vector", lambda e: e.scalar_tensor_tensor(
                    gns[:, 16:24], gns[:, 56:64], 1.0, gnw_sb[:],
                    op0=OP.mult, op1=OP.mult))
                B.op("vector", lambda e: e.drain())
                B.op("vector", lambda e: e.tensor_tensor(
                    gns[:, 24:32], gns[:, 16:24], gnb_sb[:], op=OP.add), [(ve, 1)])
                gn_ve = ve.n

                po = [P[:, 0:1024], P[:, 1024:2048]]
                store_done = {}
                wo_done = {}
                resv = {}
                for t in range(NT):
                    p = t % 2
                    ld_x = ld_xp[p]
                    st_s = st_sp[p]
                    xin_p = xin[:, p * 2 * D:(p + 1) * 2 * D]
                    if t >= 2:
                        B.wait("sync", st_s, store_done[p])
                    for _c in range(2):
                        B.op("sync", lambda e, _t=t, _x=xin_p, _cc=_c: e.dma_start(
                            out=_x[:, _cc * D:(_cc + 1) * D], in_=tile_half(x1s, _t, _cc)), [(ld_x, 16)])
                    x_ld = ld_x.n
                    B.wait("scalar", ve, gn_ve)
                    if t >= 1:
                        B.wait("scalar", pe, wo_done[t - 1])
                    for h in range(NH):
                        B.op("scalar", lambda e, _t=t, _h=h: e.activation(
                            ynb[:, _h * T:(_h + 1) * T],
                            ysb[:, _h * L + _t * T:_h * L + (_t + 1) * T],
                            AF.Identity, scale=gns[:, 48 + _h:49 + _h],
                            bias=gns[:, 24 + _h:25 + _h]), [(se, 1)])
                    yn_se = se.n
                    B.wait("tensor", se, yn_se)
                    for c in range(2):
                        gcc = 2 * t + c
                        bp = gcc % 2
                        if gcc >= 2:
                            B.wait("tensor", ve, resv[gcc - 2])
                        for eh in range(NH):
                            B.op("tensor", lambda e, _p=bp, _c=c, _e=eh: e.matmul(
                                po[_p][:, 0:512], ynb[:, _e * T + _c * 128:_e * T + (_c + 1) * 128],
                                wo_sb[:, _e * D + 0:_e * D + 512],
                                start=(_e == 0), stop=(_e == 7)))
                            B.op("tensor", lambda e, _p=bp, _c=c, _e=eh: e.matmul(
                                po[_p][:, 512:1024], ynb[:, _e * T + _c * 128:_e * T + (_c + 1) * 128],
                                wo_sb[:, _e * D + 512:_e * D + 1024],
                                start=(_e == 0), stop=(_e == 7)),
                                [(pe, 1)] if eh == 7 else ())
                        wope = pe.n
                        B.wait("vector", pe, wope)
                        B.wait("vector", ld_x, x_ld)
                        B.op("vector", lambda e, _c=c, _x=xin_p, _p=bp: e.scalar_tensor_tensor(
                            _x[:, _c * D:(_c + 1) * D], po[_p], 1.0, _x[:, _c * D:(_c + 1) * D],
                            op0=OP.mult, op1=OP.add), [(ve, 1)])
                        resv[gcc] = ve.n
                    wo_done[t] = pe.n
                    res_ve = ve.n
                    B.wait("sync", ve, res_ve)
                    for _c in range(2):
                        B.op("sync", lambda e, _t=t, _x=xin_p, _cc=_c: e.dma_start(
                            out=tile_half(x2s, _t, _cc), in_=_x[:, _cc * D:(_cc + 1) * D]), [(st_s, 16)])
                    store_done[p] = st_s.n
            barrier()

        if n_phases >= 4:
            w3scope = ExitStack()
            wg3 = w3scope.enter_context(nc.sbuf_tensor("wg3_sb", [128, 8 * HF], BF16))
            wu3 = w3scope.enter_context(nc.sbuf_tensor("wu3_sb", [128, 8 * HF], BF16))
            wd3 = w3scope.enter_context(nc.sbuf_tensor("wd3_sb", [128, 24 * D], BF16))
            wst3 = w3scope.enter_context(nc.sbuf_tensor("wstage3", [128, HF], I32))
            w3_done = ffn_weight_prologue(wg3, wu3, wd3, wst3)
            xnr = build_lnpass(x2s, xn2, "2")
            build_ffn(xn2, x2s, out_hbm, None, wg3, wu3, wd3, w3_done, final=True, tag="3",
                      xn_ready=xnr)
            w3scope.close()
        elif n_phases >= 3:
            build_lnpass(x2s, xn2, "2")

        B.wait("sync", st_sp[0], st_sp[0].n)
        B.wait("sync", st_sp[1], st_sp[1].n)
        build_nc.last_sched = B.snapshot()
        B.flush_block()

    return nc


_NC_CACHE = {}


def _get_nc(n_phases=4):
    if n_phases not in _NC_CACHE:
        _NC_CACHE[n_phases] = build_nc(n_phases)
    return _NC_CACHE[n_phases]


def _host_constants():
    gam = _gammas()
    ident = np.eye(128, dtype=BF)
    j = np.arange(128)[:, None]
    i = np.arange(128)[None, :]
    maskc = np.zeros((128, NH * 128), dtype=np.float64)
    for h in range(NH):
        maskc[:, h * 128:(h + 1) * 128] = np.where(i >= j, gam[h] ** (-128.0), 0.0)
    qdec = np.zeros((128, NH * T), dtype=np.float64)
    ii = np.arange(T) % 128
    for h in range(NH):
        qdec[:, h * T:(h + 1) * T] = (gam[h] ** (ii + 1.0))[None, :]
    kdec = np.zeros((128, D), dtype=np.float64)
    jj = np.arange(128)
    for h in range(NH):
        kdec[:, h * 128:(h + 1) * 128] = (gam[h] ** (127.0 - jj))[:, None]
    return ident, maskc.astype(BF), qdec.astype(BF), kdec.astype(BF)


def kernel(L0_attn_x, L0_attn_Wq, L0_attn_Wk, L0_attn_Wv, L0_attn_Wo,
           L0_attn_gn_weight, L0_attn_gn_bias,
           L0_ffn_w_gate, L0_ffn_w_up, L0_ffn_w_down, _n_phases=4):
    x = np.ascontiguousarray(np.asarray(L0_attn_x, dtype=np.float32))
    wq = np.ascontiguousarray(np.asarray(L0_attn_Wq, dtype=np.float32).T)
    wk = np.ascontiguousarray(np.asarray(L0_attn_Wk, dtype=np.float32).T)
    wv = np.ascontiguousarray(np.asarray(L0_attn_Wv, dtype=np.float32).T)
    wo = np.ascontiguousarray(np.asarray(L0_attn_Wo, dtype=np.float32).T)
    wg = np.ascontiguousarray(np.asarray(L0_ffn_w_gate, dtype=np.int32).T)
    wu = np.ascontiguousarray(np.asarray(L0_ffn_w_up, dtype=np.int32).T)
    wd = np.ascontiguousarray(np.asarray(L0_ffn_w_down, dtype=np.int32).T)
    gnw = np.ascontiguousarray(np.asarray(L0_attn_gn_weight, dtype=np.float32).reshape(NH, 128).T)
    gnb = np.ascontiguousarray(np.asarray(L0_attn_gn_bias, dtype=np.float32).reshape(NH, 128).T)
    ident, maskc, qdec, kdec = _host_constants()

    nc = _get_nc(_n_phases)
    shared = {"wqT": wq, "wkT": wk, "wvT": wv, "woT": wo,
              "wgT": wg, "wuT": wu, "wdT": wd,
              "gnw": gnw, "gnb": gnb, "ident": ident,
              "maskc": maskc, "qdec": qdec, "kdec": kdec}
    in_maps = [dict(shared, x=np.ascontiguousarray(x[b])) for b in range(8)]
    res = run_bass_kernel_spmd(nc, in_maps, core_ids=list(range(8)))
    out = np.stack([res.results[b]["out"] for b in range(8)], axis=0)
    return out
